# version 08
# baseline (speedup 1.0000x reference)
import numpy as np
import ml_dtypes
import concourse.bass as bass
import concourse.mybir as mybir
from concourse.bass_utils import run_bass_kernel_spmd

F32 = mybir.dt.float32
BF16 = mybir.dt.bfloat16
I32 = mybir.dt.int32
AF = mybir.ActivationFunctionType
ALU = mybir.AluOpType
AX = mybir.AxisListType

D = 2048
SEQ = 2048
TOK = 1024
C_STAB = 4.0
LN_EPS = 1e-5
BF = ml_dtypes.bfloat16



def lay_w_out(W):
    O, I = W.shape
    return np.ascontiguousarray(
        W.reshape(O // 128, 128, I // 128, 128).transpose(0, 3, 2, 1).reshape(O // 128, 128, I))


def lay_w_in(W):
    O, I = W.shape
    return np.ascontiguousarray(
        W.reshape(O // 128, 128, I // 128, 128).transpose(2, 3, 0, 1).reshape(I // 128, 128, O))


def lay_vec(b):
    return np.ascontiguousarray(b.reshape(-1, 128).T)


def lay_moe_up(w):
    E, H, I = w.shape
    return np.ascontiguousarray(
        w.reshape(E, H // 128, 128, I // 128, 128).transpose(0, 3, 2, 1, 4).reshape(E, I // 128, 128, H))


def lay_moe_down(w):
    E, I, H = w.shape
    return np.ascontiguousarray(w.reshape(E, I // 128, 128, H))


def l1_q_perm():
    perm = np.empty(2048, np.int64)
    for a in range(2048):
        p_ = a // 512
        r = (a // 128) % 4
        gpar = (a // 64) % 2
        d = a % 64
        perm[a] = (2 * p_ + gpar) * 256 + r * 64 + d
    return perm



class P:
    def __init__(self, nc):
        self.nc = nc
        self.c = {"pe": 0, "act": 0, "dve": 0, "w": 0, "x": 0, "out": 0, "ind": 0}
        self.XB = nc.alloc_sbuf_tensor("XB", [128, 16 * SEQ], BF16)
        self.XF = nc.alloc_sbuf_tensor("XF", [128, 16 * TOK], F32)
        self.AO = nc.alloc_sbuf_tensor("AO", [128, 16 * TOK], BF16)
        self.AB = nc.alloc_sbuf_tensor("AB", [128, 16384], BF16)
        self.AFR = nc.alloc_sbuf_tensor("AFR", [128, 3590], F32)
        self.SM = nc.alloc_sbuf_tensor("SM", [128, 312], F32)
        self.SMB = nc.alloc_sbuf_tensor("SMB", [128, 8], BF16)
        self.SLOTS = nc.alloc_sbuf_tensor("SLOTS", [128, 16], I32)
        self.S01 = nc.alloc_psum_tensor("S01", [128, 1024], F32)
        self.S23 = nc.alloc_psum_tensor("S23", [128, 1024], F32)
        self.PVP = nc.alloc_psum_tensor("PVP", [128, 1024], F32)
        self.ZAp = nc.alloc_psum_tensor("ZAp", [128, 512], F32)
        self.ZBp = nc.alloc_psum_tensor("ZBp", [128, 512], F32)
        self.sem = {k: nc.alloc_semaphore("sem_" + k)
                    for k in ("pe", "act", "dve", "w", "x", "out", "cc", "ind")}
        self.ring = []
        self.ring_next = 0
        self.last_in_slot = {}
        self.freed = {}
        self.w_pos = {}
        self.w_queue = []
        self.slab_home = {}

    def bump(self, k, n=1):
        self.c[k] += n
        return self.c[k]

    def xb(self, hc, a, b):
        return self.XB[:, hc * SEQ + a: hc * SEQ + b]

    def xf(self, hc, a, b):
        return self.XF[:, hc * TOK + a: hc * TOK + b]

    def ao(self, hc, a, b):
        return self.AO[:, hc * TOK + a: hc * TOK + b]

    @property
    def V32(self):
        return self.AFR[:, 0:2048]

    @property
    def SG(self):
        return self.AFR[:, 0:1024]

    @property
    def RZB(self):
        return self.AFR[:, 0:1024]

    @property
    def ZT(self):
        return self.AFR[:, 0:2048]

    @property
    def ID32(self):
        return self.AFR[:, 2048:2176]

    @property
    def COMB(self):
        return self.AFR[:, 2176:2304]

    def rowsp(self, a, b):
        return self.AFR[0:1, 2304 + a:2304 + b]

    def rowalt(self, a, b):
        return self.AFR[0:1, a:b]

    @property
    def ONESROW(self):
        return self.AFR[0:1, 3328:3456]

    @property
    def NEGROW(self):
        return self.AFR[0:1, 3456:3584]

    @property
    def GATESROW(self):
        return self.AFR[0:1, 3584:3589]

    @property
    def ONESB(self):
        return self.SMB[:, 0:1]

    def smv(self, a, b):
        return self.SM[:, a:b]

    def set_ring(self, aps):
        self.ring = list(aps)
        self.ring_next = 0
        self.last_in_slot = {}
        self.freed = {}

    def queue_slab(self, key, src_ap):
        slot = self.ring_next % len(self.ring)
        self.ring_next += 1
        guard_key = self.last_in_slot.get(slot)
        self.last_in_slot[slot] = key
        self.w_queue.append((src_ap, ("ring", slot), guard_key))
        self.c["w"] += 1
        self.w_pos[key] = self.c["w"]
        self.slab_home[key] = ("ring", slot)

    def queue_slab_at(self, key, src_ap, dst_ap, guard_key=None, parts=None):
        if parts is None:
            self.w_queue.append((src_ap, ("fixed", dst_ap), guard_key))
            self.c["w"] += 1
        else:
            for (s, d_) in parts:
                self.w_queue.append((s, ("fixed", d_), guard_key))
                self.c["w"] += 1
                guard_key = None
        self.w_pos[key] = self.c["w"]
        self.slab_home[key] = ("fixed", dst_ap)

    def slab(self, key):
        kind, v = self.slab_home[key]
        return self.ring[v] if kind == "ring" else v

    def slab_wait(self, key):
        return self.w_pos[key] * 16

    def flush_w(self, g):
        for src, home, guard_key in self.w_queue:
            if guard_key is not None:
                if isinstance(guard_key, tuple) and guard_key[0] == "val":
                    g.wait_ge(self.sem["pe"], guard_key[1])
                else:
                    g.wait_ge(self.sem["pe"], self.freed[guard_key])
            kind, v = home
            dst = self.ring[v] if kind == "ring" else v
            g.dma_start(dst, src).then_inc(self.sem["w"], 16)
        self.w_queue.clear()

    def free_slab(self, key, pe_value):
        self.freed[key] = pe_value


def wge(eng, sem, v):
    if v > 0:
        eng.wait_ge(sem, v)



def emit_init(p, t, l0):
    nc = p.nc
    with nc.Block() as blk:
        @blk.sync
        def _(sy):
            src = t["xT"].rearrange("(c p) s -> p c s", p=128)
            for i in range(4):
                dst = p.XF[:, i * 4 * TOK:(i + 1) * 4 * TOK].rearrange(
                    "q (c s) -> q c s", c=4)
                sy.dma_start(dst, src[:, i * 4:(i + 1) * 4, 0:TOK]).then_inc(p.sem["x"], 16)
                p.bump("x")
            if l0:
                sy.dma_start(p.smv(0, 64), t["b_attn"]).then_inc(p.sem["x"], 16)
                sy.dma_start(p.smv(64, 80), t["obc"]).then_inc(p.sem["x"], 16)
                sy.dma_start(p.smv(80, 144), t["gsc"]).then_inc(p.sem["x"], 16)
                sy.dma_start(p.smv(144, 208), t["usc"]).then_inc(p.sem["x"], 16)
                sy.dma_start(p.smv(208, 224), t["dsc"]).then_inc(p.sem["x"], 16)
                sy.dma_start(p.GATESROW, t["gates"]).then_inc(p.sem["x"], 16)
                p.bump("x", 6)
            if "b_qkv" in t:
                sy.dma_start(p.smv(270, 294), t["b_qkv"]).then_inc(p.sem["x"], 16)
                sy.dma_start(p.smv(294, 310), t["b_out"]).then_inc(p.sem["x"], 16)
                p.bump("x", 2)
            sy.dma_start(p.ID32, t["ident"]).then_inc(p.sem["x"], 16)
            p.bump("x")

        @blk.gpsimd
        def _(g):
            src = t["xT"].rearrange("(c p) s -> p c s", p=128)
            for i in range(4):
                dst = p.XB[:, i * 4 * SEQ:(i + 1) * 4 * SEQ].rearrange(
                    "q (c s) -> q c s", c=4)
                g.dma_start(dst, src[:, i * 4:(i + 1) * 4, :]).then_inc(p.sem["w"], 16)
                p.bump("w")

        @blk.vector
        def _(v):
            v.memset(p.ONESB, 1.0)
            v.memset(p.ONESROW, 1.0)
            v.memset(p.NEGROW, -128.0)
            v.memset(p.smv(264, 265), 1.0 / D)
            v.memset(p.smv(265, 266), LN_EPS)
            v.memset(p.smv(266, 267), -C_STAB)
            v.memset(p.smv(267, 268), 0.125)
            v.memset(p.smv(310, 311), 1.0)

    if not l0:
        with nc.Block() as blk:
            @blk.sync
            def _(sy):
                sy.wait_ge(p.sem["x"], p.c["x"] * 16)
            @blk.tensor
            def _(pe):
                pe.wait_ge(p.sem["x"], p.c["x"] * 16)
            @blk.vector
            def _(v):
                v.wait_ge(p.sem["x"], p.c["x"] * 16)
                v.tensor_scalar(p.smv(0, 16), p.smv(0, 16), 0.125, None, ALU.mult)
    if l0:
        with nc.Block() as blk:
            @blk.tensor
            def _(pe):
                pe.wait_ge(p.sem["x"], p.c["x"] * 16)
                pe.matmul(p.ZAp[:, 0:5], p.ONESROW, p.GATESROW,
                          start=True, stop=True).then_inc(p.sem["pe"], 1)
                p.bump("pe")

            @blk.scalar
            def _(a):
                a.wait_ge(p.sem["pe"], p.c["pe"])
                a.activation(p.smv(257, 262), p.ZAp[:, 0:5],
                             AF.Sigmoid).then_inc(p.sem["act"], 1)
                p.bump("act")

            @blk.vector
            def _(v):
                v.wait_ge(p.sem["act"], p.c["act"])
                inv = 1.0 / np.sqrt(128.0)
                v.tensor_scalar(p.smv(256, 257), p.smv(260, 261), -0.1 * inv,
                                inv, ALU.mult, ALU.add)
                v.tensor_scalar(p.smv(240, 256), p.smv(64, 80),
                                p.smv(259, 260), -C_STAB, ALU.mult, ALU.add)
                ins = v.tensor_scalar(p.smv(224, 240), p.smv(0, 16),
                                      p.smv(256, 257), None, ALU.mult)
                if "b_qkv" in t:
                    ins = v.tensor_scalar(p.smv(270, 286), p.smv(270, 286),
                                          0.125, None, ALU.mult)
                ins.then_inc(p.sem["dve"], 1)
                p.bump("dve")


def emit_attn_l0(p, t):
    nc = p.nc
    p.set_ring([p.AB[:, i * 2048:(i + 1) * 2048] for i in range(4)])
    KSB = p.AB[:, 8192:10240]
    QSB = p.AB[:, 10240:11264]
    PT = [p.AB[:, 11264:12288], p.AB[:, 12288:13312]]
    VT = p.AB[:, 13312:15360]

    for h in range(16):
        P0, A0, D0 = p.c["pe"], p.c["act"], p.c["dve"]
        if h == 0:
            p.queue_slab("k0", t["wk"][0])
            p.queue_slab("v0", t["wv"][0])
            p.queue_slab("q0", t["wq"][0])
        if h < 15:
            p.queue_slab(f"k{h+1}", t["wk"][h + 1])
            p.queue_slab(f"v{h+1}", t["wv"][h + 1])
            p.queue_slab(f"q{h+1}", t["wq"][h + 1])
        else:
            for jc in range(4):
                p.queue_slab(f"o{jc}", t["wo"][jc])
        with nc.Block() as blk:
            @blk.tensor
            def _(pe):
                pe.wait_ge(p.sem["w"], p.slab_wait(f"q{h}"))
                if h == 0:
                    wge(pe, p.sem["dve"], p.c["dve"])
                gi = 0
                for kind, n_t, key in (("k", 4, f"k{h}"), ("v", 4, f"v{h}"),
                                       ("q", 2, f"q{h}")):
                    slab = p.slab(key)
                    for tt in range(n_t):
                        ps = (p.S01, p.S23)[(gi % 4) // 2][:, (gi % 2) * 512:(gi % 2) * 512 + 512]
                        if gi >= 4:
                            pe.wait_ge(p.sem["act"], A0 + gi - 4 + 1)
                        ins = None
                        for hc in range(16):
                            ins = pe.matmul(ps, slab[:, hc * 128:(hc + 1) * 128],
                                            p.xb(hc, tt * 512, (tt + 1) * 512),
                                            start=(hc == 0), stop=(hc == 15))
                        ins.then_inc(p.sem["pe"], 1)
                        p.bump("pe")
                        gi += 1
                    p.free_slab(key, p.c["pe"])
                for tt in range(16):
                    pe.wait_ge(p.sem["act"], A0 + 5 + tt // 4)
                    if tt >= 2:
                        pe.wait_ge(p.sem["dve"], D0 + tt - 1)
                    dstp = (p.ZAp, p.ZBp)[tt % 2][:, 0:128]
                    pe.transpose(dstp, p.V32[:, tt * 128:(tt + 1) * 128],
                                 p.ID32).then_inc(p.sem["pe"], 1)
                    p.bump("pe")
                s_inc = {}

                def emit_S(kc):
                    Sp = (p.S01, p.S23)[kc % 2]
                    pe.wait_ge(p.sem["act"],
                               A0 + 10 + (kc - 2) + 1 if kc >= 2 else A0 + 10)
                    ins = None
                    for half in range(2):
                        ins = pe.matmul(Sp[:, half * 512:(half + 1) * 512],
                                        KSB[:, kc * 128:(kc + 1) * 128],
                                        QSB[:, half * 512:(half + 1) * 512],
                                        start=True, stop=True)
                    ins.then_inc(p.sem["pe"], 1)
                    s_inc[kc] = p.bump("pe")

                emit_S(0)
                emit_S(1)
                for kc in range(16):
                    pe.wait_ge(p.sem["act"], A0 + 10 + kc + 1)
                    pe.wait_ge(p.sem["dve"], D0 + kc + 1)
                    ins = None
                    for half in range(2):
                        ins = pe.matmul(p.PVP[:, half * 512:(half + 1) * 512],
                                        VT[:, kc * 128:(kc + 1) * 128],
                                        PT[kc % 2][:, half * 512:(half + 1) * 512],
                                        start=(kc == 0), stop=(kc == 15))
                    ins.then_inc(p.sem["pe"], 1)
                    p.bump("pe")
                    if kc == 0:
                        pe.wait_ge(p.sem["dve"], D0 + 16)
                    ins = None
                    for half, zp in ((0, p.ZAp), (1, p.ZBp)):
                        ins = pe.matmul(zp[0:1, :], p.ONESB,
                                        PT[kc % 2][:, half * 512:(half + 1) * 512],
                                        start=(kc == 0), stop=(kc == 15))
                    ins.then_inc(p.sem["pe"], 1)
                    p.bump("pe")
                    if kc + 2 < 16:
                        emit_S(kc + 2)
                pe.wait_ge(p.sem["dve"], D0 + 18)
                pe.wait_ge(p.sem["act"], A0 + 26)
                zr = p.rowsp(0, 1024)
                pe.matmul(p.S01[:, 0:512], p.ONESROW, zr[0:1, 0:512],
                          start=True, stop=True)
                pe.matmul(p.S01[:, 512:1024], p.ONESROW, zr[0:1, 512:1024],
                          start=True, stop=True).then_inc(p.sem["pe"], 1)
                p.bump("pe")
                assert p.c["pe"] == P0 + 75, p.c["pe"]
                p.l0_s_inc = s_inc

            @blk.scalar
            def _(a):
                gi = 0
                for kind, n_t in (("k", 4), ("v", 4), ("q", 2)):
                    for tt in range(n_t):
                        ps = (p.S01, p.S23)[(gi % 4) // 2][:, (gi % 2) * 512:(gi % 2) * 512 + 512]
                        a.wait_ge(p.sem["pe"], P0 + gi + 1)
                        if kind == "k":
                            ins = a.activation(KSB[:, tt * 512:(tt + 1) * 512], ps,
                                               AF.Identity, bias=p.smv(16 + h, 17 + h))
                        elif kind == "v":
                            ins = a.activation(p.V32[:, tt * 512:(tt + 1) * 512], ps,
                                               AF.Identity, bias=p.smv(32 + h, 33 + h))
                        else:
                            ins = a.activation(QSB[:, tt * 512:(tt + 1) * 512], ps,
                                               AF.Identity, bias=p.smv(224 + h, 225 + h),
                                               scale=p.smv(256, 257))
                        ins.then_inc(p.sem["act"], 1)
                        p.bump("act")
                        gi += 1
                for kc in range(16):
                    Sp = (p.S01, p.S23)[kc % 2]
                    a.wait_ge(p.sem["pe"], p.l0_s_inc[kc])
                    a.activation(PT[kc % 2], Sp[:, :], AF.Exp,
                                 bias=p.smv(240 + kc, 241 + kc)).then_inc(p.sem["act"], 1)
                    p.bump("act")
                assert p.c["act"] == A0 + 26

            @blk.vector
            def _(v):
                for tt in range(16):
                    v.wait_ge(p.sem["pe"], P0 + 10 + tt + 1)
                    v.tensor_copy(VT[:, tt * 128:(tt + 1) * 128],
                                  (p.ZAp, p.ZBp)[tt % 2][:, 0:128]).then_inc(p.sem["dve"], 1)
                    p.bump("dve")
                zr = p.rowsp(0, 1024)
                v.wait_ge(p.sem["pe"], P0 + 74)
                v.reciprocal(zr[0:1, 0:512], p.ZAp[0:1, :]).then_inc(p.sem["dve"], 1)
                v.reciprocal(zr[0:1, 512:1024], p.ZBp[0:1, :]).then_inc(p.sem["dve"], 1)
                p.bump("dve", 2)
                v.wait_ge(p.sem["pe"], P0 + 75)
                v.tensor_copy(p.RZB, p.S01[:, :]).then_inc(p.sem["dve"], 1)
                p.bump("dve")
                v.tensor_tensor(p.ao(h, 0, TOK), p.PVP[:, :], p.RZB,
                                ALU.mult).then_inc(p.sem["dve"], 1)
                p.bump("dve")
                assert p.c["dve"] == D0 + 20

            @blk.gpsimd
            def _(g):
                p.flush_w(g)


def emit_outproj(p, wo_dram, bias_col0, n_queued):
    nc = p.nc
    P0, A0, D0 = p.c["pe"], p.c["act"], p.c["dve"]
    for jc in range(n_queued, 16):
        p.queue_slab(f"o{jc}", wo_dram[jc])
    with nc.Block() as blk:
        @blk.tensor
        def _(pe):
            gi = 0
            for jc in range(16):
                pe.wait_ge(p.sem["w"], p.slab_wait(f"o{jc}"))
                slab = p.slab(f"o{jc}")
                for th in range(2):
                    ps = (p.S01, p.S23)[(gi % 4) // 2][:, (gi % 2) * 512:(gi % 2) * 512 + 512]
                    if gi >= 4:
                        pe.wait_ge(p.sem["dve"], D0 + gi - 4 + 1)
                    ins = None
                    for hc in range(16):
                        ins = pe.matmul(ps, slab[:, hc * 128:(hc + 1) * 128],
                                        p.ao(hc, th * 512, (th + 1) * 512),
                                        start=(hc == 0), stop=(hc == 15))
                    ins.then_inc(p.sem["pe"], 1)
                    p.bump("pe")
                    gi += 1
                p.free_slab(f"o{jc}", p.c["pe"])

        @blk.vector
        def _(v):
            gi = 0
            for jc in range(16):
                for th in range(2):
                    ps = (p.S01, p.S23)[(gi % 4) // 2][:, (gi % 2) * 512:(gi % 2) * 512 + 512]
                    v.wait_ge(p.sem["pe"], P0 + gi + 1)
                    xfr = p.xf(jc, th * 512, (th + 1) * 512)
                    v.scalar_tensor_tensor(
                        xfr, ps, p.smv(bias_col0 + jc, bias_col0 + jc + 1),
                        xfr, ALU.add, ALU.add).then_inc(p.sem["dve"], 1)
                    p.bump("dve")
                    gi += 1

        @blk.gpsimd
        def _(g):
            p.flush_w(g)


def emit_ln(p, sq_area):
    nc = p.nc
    P0, A0, D0 = p.c["pe"], p.c["act"], p.c["dve"]
    m_row = p.rowsp(0, 1024)
    r_row = p.rowalt(0, 1024)
    e2a = p.rowalt(1024, 1536)
    e2b = p.rowalt(1536, 2048)
    with nc.Block() as blk:
        @blk.tensor
        def _(pe):
            for hc in range(16):
                pe.wait_ge(p.sem["dve"], D0 + hc + 1)
                pe.matmul(p.ZAp[0:1, :], p.ONESB, p.xb(hc, 0, 512),
                          start=(hc == 0), stop=(hc == 15))
                pe.matmul(p.ZBp[0:1, :], p.ONESB, p.xb(hc, 512, 1024),
                          start=(hc == 0), stop=(hc == 15))
                pe.wait_ge(p.sem["act"], A0 + hc + 1)
                sq = sq_area[:, (hc % 2) * 1024:(hc % 2) * 1024 + 1024]
                pe.matmul(p.S01[0:1, 0:512], p.ONESB, sq[:, 0:512],
                          start=(hc == 0), stop=(hc == 15))
                ins = pe.matmul(p.S01[0:1, 512:1024], p.ONESB, sq[:, 512:1024],
                                start=(hc == 0), stop=(hc == 15))
                ins.then_inc(p.sem["pe"], 1)
                p.bump("pe")
            pe.wait_ge(p.sem["dve"], D0 + 19)
            pe.matmul(p.S23[:, 0:512], p.ONESROW, r_row[0:1, 0:512],
                      start=True, stop=True)
            pe.matmul(p.S23[:, 512:1024], p.ONESROW, r_row[0:1, 512:1024],
                      start=True, stop=True).then_inc(p.sem["pe"], 1)
            p.bump("pe")
            pe.matmul(p.PVP[:, 0:512], p.ONESROW, m_row[0:1, 0:512],
                      start=True, stop=True)
            pe.matmul(p.PVP[:, 512:1024], p.ONESROW, m_row[0:1, 512:1024],
                      start=True, stop=True).then_inc(p.sem["pe"], 1)
            p.bump("pe")

        @blk.scalar
        def _(a):
            for hc in range(16):
                a.wait_ge(p.sem["dve"], D0 + hc + 1)
                if hc >= 2:
                    a.wait_ge(p.sem["pe"], P0 + hc - 1)
                a.activation(sq_area[:, (hc % 2) * 1024:(hc % 2) * 1024 + 1024],
                             p.xb(hc, 0, TOK), AF.Square).then_inc(p.sem["act"], 1)
                p.bump("act")
            a.wait_ge(p.sem["pe"], P0 + 16)
            a.activation(m_row[0:1, 0:512], p.ZAp[0:1, :], AF.Identity, scale=p.smv(264, 265)[0:1, :])
            a.activation(m_row[0:1, 512:1024], p.ZBp[0:1, :], AF.Identity,
                         scale=p.smv(264, 265)[0:1, :]).then_inc(p.sem["act"], 2)
            p.bump("act", 2)
            a.wait_ge(p.sem["act"], p.c["act"])
            a.activation(e2a, p.S01[0:1, 0:512], AF.Identity, scale=p.smv(264, 265)[0:1, :])
            a.activation(e2b, p.S01[0:1, 512:1024], AF.Identity,
                         scale=p.smv(264, 265)[0:1, :]).then_inc(p.sem["act"], 2)
            p.bump("act", 2)
            a.activation(p.ZAp[0:1, :], m_row[0:1, 0:512], AF.Square)
            a.activation(p.ZBp[0:1, :], m_row[0:1, 512:1024],
                         AF.Square).then_inc(p.sem["act"], 2)
            p.bump("act", 2)
            a.wait_ge(p.sem["dve"], D0 + 17)
            a.activation(p.ZAp[0:1, :], r_row[0:1, 0:512], AF.Sqrt, bias=p.smv(265, 266)[0:1, :])
            a.activation(p.ZBp[0:1, :], r_row[0:1, 512:1024], AF.Sqrt,
                         bias=p.smv(265, 266)[0:1, :]).then_inc(p.sem["act"], 2)
            p.bump("act", 2)

        @blk.vector
        def _(v):
            for hc in range(16):
                v.tensor_copy(p.xb(hc, 0, TOK),
                              p.xf(hc, 0, TOK)).then_inc(p.sem["dve"], 1)
                p.bump("dve")
            v.wait_ge(p.sem["act"], A0 + 22)
            v.tensor_tensor(r_row[0:1, 0:512], e2a, p.ZAp[0:1, :], ALU.subtract)
            v.tensor_tensor(r_row[0:1, 512:1024], e2b, p.ZBp[0:1, :],
                            ALU.subtract).then_inc(p.sem["dve"], 1)
            p.bump("dve")
            v.wait_ge(p.sem["act"], A0 + 24)
            v.reciprocal(r_row[0:1, 0:512], p.ZAp[0:1, :])
            v.reciprocal(r_row[0:1, 512:1024], p.ZBp[0:1, :]).then_inc(p.sem["dve"], 1)
            p.bump("dve")
            v.wait_ge(p.sem["dve"], p.c["dve"])
            v.tensor_tensor(m_row, m_row, r_row, ALU.mult).then_inc(p.sem["dve"], 1)
            p.bump("dve")
            v.wait_ge(p.sem["pe"], P0 + 18)
            for hc in range(16):
                v.tensor_tensor(p.xf(hc, 0, TOK), p.xf(hc, 0, TOK),
                                p.S23[:, :], ALU.mult)
                v.tensor_tensor(p.xf(hc, 0, TOK), p.xf(hc, 0, TOK),
                                p.PVP[:, :], ALU.subtract)
                v.tensor_copy(p.xb(hc, 0, TOK),
                              p.xf(hc, 0, TOK)).then_inc(p.sem["dve"], 1)
                p.bump("dve")


def emit_ffn_l0(p, t):
    nc = p.nc
    p.set_ring([p.AB[:, i * 2048:(i + 1) * 2048] for i in range(8)]
               + [p.AO[:, 8192 + i * 2048:8192 + (i + 1) * 2048] for i in range(4)])
    s_row = p.rowsp(0, 1024)
    sh_row = p.rowalt(1024, 2048)

    seq = []
    for r in range(8):
        for i in range(8):
            seq.append(("g", 8 * r + i))
            seq.append(("u", 8 * r + i))
        for i in range(8):
            seq.append(("d", 8 * r + i))

    def src_of(kind, idx):
        return {"g": t["gw"], "u": t["uw"], "d": t["dw"]}[kind][idx]

    qptr = [0]

    def queue_upto(n):
        while qptr[0] < min(n, len(seq)):
            kind, idx = seq[qptr[0]]
            p.queue_slab(f"{kind}{idx}", src_of(kind, idx))
            qptr[0] += 1

    P0 = p.c["pe"]
    pre_w = p.c["w"]
    queue_upto(12)
    with nc.Block() as blk:
        @blk.tensor
        def _(pe):
            ins = None
            for hc in range(16):
                pe.matmul(p.ZAp[0:1, :], p.ONESB, p.xb(hc, 0, 512),
                          start=(hc == 0), stop=(hc == 15))
                ins = pe.matmul(p.ZBp[0:1, :], p.ONESB, p.xb(hc, 512, 1024),
                                start=(hc == 0), stop=(hc == 15))
            ins.then_inc(p.sem["pe"], 1)
            p.bump("pe")

        @blk.scalar
        def _(a):
            a.wait_ge(p.sem["pe"], P0 + 1)
            a.activation(s_row[0:1, 0:512], p.ZAp[0:1, :], AF.Identity)
            a.activation(s_row[0:1, 512:1024], p.ZBp[0:1, :],
                         AF.Identity).then_inc(p.sem["act"], 1)
            p.bump("act")

        @blk.gpsimd
        def _(g):
            g.wait_ge(p.sem["w"], pre_w * 16)
            p.flush_w(g)

    for r in range(8):
        P0, A0, D0 = p.c["pe"], p.c["act"], p.c["dve"]
        queue_upto(12 + 24 * (r + 1))
        with nc.Block() as blk:
            @blk.tensor
            def _(pe):
                for i in range(8):
                    ic = 8 * r + i
                    pe.wait_ge(p.sem["w"], p.slab_wait(f"u{ic}"))
                    gs = p.slab(f"g{ic}")
                    us = p.slab(f"u{ic}")
                    for th in range(2):
                        if i >= 1:
                            pe.wait_ge(p.sem["act"], A0 + 2 * (i - 1) + th + 1)
                        ps = p.S01[:, th * 512:(th + 1) * 512]
                        for hc in range(16):
                            pe.matmul(ps, gs[:, hc * 128:(hc + 1) * 128],
                                      p.xb(hc, th * 512, (th + 1) * 512),
                                      start=(hc == 0), stop=False)
                        pe.matmul(ps, p.NEGROW, s_row[0:1, th * 512:(th + 1) * 512],
                                  start=False, stop=True).then_inc(p.sem["pe"], 1)
                        p.bump("pe")
                    for th in range(2):
                        if i >= 1:
                            pe.wait_ge(p.sem["dve"], D0 + 2 * (i - 1) + th + 1)
                        ps = p.S23[:, th * 512:(th + 1) * 512]
                        for hc in range(16):
                            pe.matmul(ps, us[:, hc * 128:(hc + 1) * 128],
                                      p.xb(hc, th * 512, (th + 1) * 512),
                                      start=(hc == 0), stop=False)
                        pe.matmul(ps, p.NEGROW, s_row[0:1, th * 512:(th + 1) * 512],
                                  start=False, stop=True).then_inc(p.sem["pe"], 1)
                        p.bump("pe")
                    p.free_slab(f"g{ic}", p.c["pe"])
                    p.free_slab(f"u{ic}", p.c["pe"])
                ins = None
                for i in range(8):
                    pe.wait_ge(p.sem["dve"], D0 + 2 * i + 2)
                    hsb = p.AO[:, i * 1024:(i + 1) * 1024]
                    pe.matmul(p.ZAp[0:1, :], p.ONESB, hsb[:, 0:512],
                              start=(i == 0), stop=(i == 7))
                    ins = pe.matmul(p.ZBp[0:1, :], p.ONESB, hsb[:, 512:1024],
                                    start=(i == 0), stop=(i == 7))
                ins.then_inc(p.sem["pe"], 1)
                p.bump("pe")
                gi = 0
                for jc in range(16):
                    if jc == 0:
                        pe.wait_ge(p.sem["act"], A0 + 18)
                        pe.wait_ge(p.sem["w"], p.slab_wait(f"d{8 * r + 7}"))
                    for th in range(2):
                        rot = gi % 4
                        ps = (p.PVP, p.S01)[rot // 2][:, (rot % 2) * 512:(rot % 2) * 512 + 512]
                        if gi >= 4:
                            pe.wait_ge(p.sem["dve"], D0 + 16 + (gi - 4) + 1)
                        for i in range(8):
                            ds = p.slab(f"d{8 * r + i}")
                            pe.matmul(ps, ds[:, jc * 128:(jc + 1) * 128],
                                      p.AO[:, i * 1024 + th * 512:i * 1024 + (th + 1) * 512],
                                      start=(i == 0), stop=False)
                        pe.matmul(ps, p.NEGROW, sh_row[0:1, th * 512:(th + 1) * 512],
                                  start=False, stop=True).then_inc(p.sem["pe"], 1)
                        p.bump("pe")
                        gi += 1
                for i in range(8):
                    p.free_slab(f"d{8 * r + i}", p.c["pe"])

            @blk.scalar
            def _(a):
                for i in range(8):
                    ic = 8 * r + i
                    for th in range(2):
                        a.wait_ge(p.sem["pe"], P0 + 4 * i + th + 1)
                        if i >= 1:
                            a.wait_ge(p.sem["dve"], D0 + 2 * (i - 1) + th + 1)
                        a.activation(p.SG[:, th * 512:(th + 1) * 512],
                                     p.S01[:, th * 512:(th + 1) * 512], AF.Silu,
                                     scale=p.smv(80 + ic, 81 + ic)).then_inc(p.sem["act"], 1)
                        p.bump("act")
                a.wait_ge(p.sem["pe"], P0 + 33)
                a.activation(sh_row[0:1, 0:512], p.ZAp[0:1, :], AF.Identity)
                a.activation(sh_row[0:1, 512:1024], p.ZBp[0:1, :],
                             AF.Identity).then_inc(p.sem["act"], 2)
                p.bump("act", 2)

            @blk.vector
            def _(v):
                for i in range(8):
                    ic = 8 * r + i
                    for th in range(2):
                        v.wait_ge(p.sem["pe"], P0 + 4 * i + 2 + th + 1)
                        v.wait_ge(p.sem["act"], A0 + 2 * i + th + 1)
                        v.scalar_tensor_tensor(
                            p.AO[:, i * 1024 + th * 512:i * 1024 + (th + 1) * 512],
                            p.S23[:, th * 512:(th + 1) * 512],
                            p.smv(144 + ic, 145 + ic),
                            p.SG[:, th * 512:(th + 1) * 512],
                            ALU.mult, ALU.mult).then_inc(p.sem["dve"], 1)
                        p.bump("dve")
                gi = 0
                for jc in range(16):
                    for th in range(2):
                        rot = gi % 4
                        ps = (p.PVP, p.S01)[rot // 2][:, (rot % 2) * 512:(rot % 2) * 512 + 512]
                        v.wait_ge(p.sem["pe"], P0 + 33 + gi + 1)
                        xfr = p.xf(jc, th * 512, (th + 1) * 512)
                        v.scalar_tensor_tensor(
                            xfr, ps, p.smv(208 + jc, 209 + jc),
                            xfr, ALU.mult, ALU.add).then_inc(p.sem["dve"], 1)
                        p.bump("dve")
                        gi += 1

            @blk.gpsimd
            def _(g):
                p.flush_w(g)


def emit_kv_exchange(p, t):
    nc = p.nc
    p.set_ring([p.AB[:, i * 2048:(i + 1) * 2048] for i in range(4)])
    KVO = p.AB[:, 8192:16384]
    for c in range(8):
        p.queue_slab(f"KV{c}", t["wqkv"][16 + c])
    P0, A0 = p.c["pe"], p.c["act"]
    with nc.Block() as blk:
        @blk.tensor
        def _(pe):
            gi = 0
            for c in range(8):
                pe.wait_ge(p.sem["w"], p.slab_wait(f"KV{c}"))
                slab = p.slab(f"KV{c}")
                for tc in range(2):
                    ps = (p.S01, p.S23)[(gi % 4) // 2][:, (gi % 2) * 512:(gi % 2) * 512 + 512]
                    if gi >= 4:
                        pe.wait_ge(p.sem["act"], A0 + gi - 4 + 1)
                    ins = None
                    for hc in range(16):
                        ins = pe.matmul(ps, slab[:, hc * 128:(hc + 1) * 128],
                                        p.xb(hc, tc * 512, (tc + 1) * 512),
                                        start=(hc == 0), stop=(hc == 15))
                    ins.then_inc(p.sem["pe"], 1)
                    p.bump("pe")
                    gi += 1
                p.free_slab(f"KV{c}", p.c["pe"])

        @blk.scalar
        def _(a):
            gi = 0
            for c in range(8):
                for tc in range(2):
                    ps = (p.S01, p.S23)[(gi % 4) // 2][:, (gi % 2) * 512:(gi % 2) * 512 + 512]
                    a.wait_ge(p.sem["pe"], P0 + gi + 1)
                    a.activation(KVO[:, c * 1024 + tc * 512:c * 1024 + (tc + 1) * 512],
                                 ps, AF.Identity,
                                 bias=p.smv(286 + c, 287 + c)).then_inc(p.sem["act"], 1)
                    p.bump("act")
                    gi += 1

        @blk.gpsimd
        def _(g):
            p.flush_w(g)

    with nc.Block() as blk:
        @blk.gpsimd
        def _(g):
            dst = t["cc_in"].rearrange("(c q) s -> q c s", q=128)
            src = KVO.rearrange("q (c s) -> q c s", c=8)
            g.dma_start(dst, src).then_inc(p.sem["w"], 16)
            p.bump("w")
            g.wait_ge(p.sem["w"], p.c["w"] * 16)
            g.collective_compute(
                "AllGather", ALU.bypass,
                replica_groups=[[0, 1], [2, 3], [4, 5], [6, 7]],
                ins=[t["cc_in"][:, :]], outs=[t["cc_out"][:, :]]).then_inc(p.sem["cc"], 1)


def emit_attn_l1(p, t):
    nc = p.nc
    p.set_ring([p.AB[:, i * 2048:(i + 1) * 2048] for i in range(2)])
    KSB = p.AB[:, 4096:6144]
    QSB = p.AB[:, 6144:10240]
    PT = [p.AB[:, 10240:11264], p.AB[:, 11264:12288]]
    VB = p.AB[:, 10240:12288]

    def vta(g_loc, kc):
        off = 12288 + g_loc * 1040 + kc * 65
        return p.AB[:, off:off + 65]

    for pr in range(4):
        P0, A0, D0 = p.c["pe"], p.c["act"], p.c["dve"]
        for j in range(4):
            p.queue_slab(f"Q{pr}_{j}", t["wqkv"][4 * pr + j])
        xkv = p.c["x"] + 4
        with nc.Block() as blk:
            @blk.sync
            def _(sy):
                if pr == 0:
                    sy.wait_ge(p.sem["cc"], 1)
                cco = t["cc_out"]
                for r in range(2):
                    sy.dma_start(KSB[:, r * 1024:(r + 1) * 1024],
                                 cco[r * 1024 + pr * 128:r * 1024 + (pr + 1) * 128, :]
                                 ).then_inc(p.sem["x"], 16)
                    sy.dma_start(VB[:, r * 1024:(r + 1) * 1024],
                                 cco[r * 1024 + 512 + pr * 128:r * 1024 + 512 + (pr + 1) * 128, :]
                                 ).then_inc(p.sem["x"], 16)
                    p.bump("x", 2)

            @blk.vector
            def _(v):
                v.wait_ge(p.sem["x"], xkv * 16)
                v.tensor_copy(p.V32, VB).then_inc(p.sem["dve"], 1)
                p.bump("dve")
                if pr == 0:
                    for g_loc in range(2):
                        for kc in range(16):
                            v.memset(vta(g_loc, kc)[:, 64:65], 1.0)
                idx = 0
                for g_loc in range(2):
                    for tt in range(16):
                        v.wait_ge(p.sem["pe"], P0 + idx + 1)
                        v.tensor_copy(vta(g_loc, tt)[:, 0:64],
                                      (p.ZAp, p.ZBp)[idx % 2][:, 0:64]).then_inc(p.sem["dve"], 1)
                        p.bump("dve")
                        idx += 1
                for hd in range(8):
                    g_loc, rr = hd // 4, hd % 4
                    HB = P0 + 40 + hd * 49
                    b64 = g_loc * 64
                    zrow = 0
                    zr = p.rowsp(0, 1024)
                    v.wait_ge(p.sem["pe"], HB + 48)
                    v.reciprocal(zr[0:1, 0:512], p.ZAp[zrow:zrow + 1, :])
                    v.reciprocal(zr[0:1, 512:1024],
                                 p.ZBp[zrow:zrow + 1, :]).then_inc(p.sem["dve"], 1)
                    p.bump("dve")
                    v.wait_ge(p.sem["pe"], HB + 49)
                    v.tensor_copy(p.RZB[:, 0:512], p.ZAp[:, :])
                    v.tensor_copy(p.RZB[:, 512:1024], p.ZBp[:, :]).then_inc(p.sem["dve"], 1)
                    p.bump("dve")
                    ao_c = 4 * pr + rr
                    v.tensor_tensor(
                        p.AO[b64:b64 + 64, ao_c * TOK:(ao_c + 1) * TOK],
                        p.PVP[b64:b64 + 64, :], p.RZB[b64:b64 + 64, :],
                        ALU.mult).then_inc(p.sem["dve"], 1)
                    p.bump("dve")

            @blk.tensor
            def _(pe):
                if pr == 0:
                    pe.wait_ge(p.sem["cc"], 1)
                idx = 0
                for g_loc in range(2):
                    idv = p.ID32[g_loc * 64:(g_loc + 1) * 64,
                                 g_loc * 64:(g_loc + 1) * 64]
                    for tt in range(16):
                        pe.wait_ge(p.sem["dve"], D0 + 1)
                        if idx >= 2:
                            pe.wait_ge(p.sem["dve"], D0 + 1 + idx - 1)
                        dstp = (p.ZAp, p.ZBp)[idx % 2][:, 0:64]
                        pe.transpose(dstp,
                                     p.V32[g_loc * 64:(g_loc + 1) * 64,
                                           tt * 128:(tt + 1) * 128],
                                     idv).then_inc(p.sem["pe"], 1)
                        p.bump("pe")
                        idx += 1
                gi = 0
                for j in range(4):
                    slab = p.slab(f"Q{pr}_{j}")
                    pe.wait_ge(p.sem["w"], p.slab_wait(f"Q{pr}_{j}"))
                    for th in range(2):
                        ps = (p.S01, p.S23)[(gi % 4) // 2][:, (gi % 2) * 512:(gi % 2) * 512 + 512]
                        if gi >= 4:
                            pe.wait_ge(p.sem["act"], A0 + gi - 4 + 1)
                        ins = None
                        for hc in range(16):
                            ins = pe.matmul(ps, slab[:, hc * 128:(hc + 1) * 128],
                                            p.xb(hc, th * 512, (th + 1) * 512),
                                            start=(hc == 0), stop=(hc == 15))
                        ins.then_inc(p.sem["pe"], 1)
                        p.bump("pe")
                        gi += 1
                    p.free_slab(f"Q{pr}_{j}", p.c["pe"])
                p.l1_s_inc = {}
                for hd in range(8):
                    g_loc, rr = hd // 4, hd % 4
                    b64 = g_loc * 64
                    zrow = 0
                    tpos_pv = (0, 64) if g_loc == 1 else None
                    tpos_z = None

                    def emit_S(kc, hd=hd, rr=rr, b64=b64):
                        Sp = (p.S01, p.S23)[kc % 2]
                        e = hd * 16 + kc
                        pe.wait_ge(p.sem["act"],
                                   A0 + 8 + e - 2 + 1 if e >= 2 else A0 + 8)
                        ins = None
                        for half in range(2):
                            ins = pe.matmul(
                                Sp[:, half * 512:(half + 1) * 512],
                                KSB[b64:b64 + 64, kc * 128:(kc + 1) * 128],
                                QSB[b64:b64 + 64,
                                    rr * 1024 + half * 512:rr * 1024 + (half + 1) * 512],
                                start=True, stop=True)
                        ins.then_inc(p.sem["pe"], 1)
                        p.l1_s_inc[(hd, kc)] = p.bump("pe")

                    emit_S(0)
                    emit_S(1)
                    for kc in range(16):
                        e = hd * 16 + kc
                        pe.wait_ge(p.sem["act"], A0 + 8 + e + 1)
                        pe.wait_ge(p.sem["dve"], D0 + 1 + g_loc * 16 + kc + 1)
                        if kc == 0 and hd >= 1:
                            pe.wait_ge(p.sem["dve"], D0 + 33 + (hd - 1) * 3 + 3)
                        if kc == 0 and hd == 0:
                            pe.wait_ge(p.sem["dve"], D0 + 33)
                        ins = None
                        for half in range(2):
                            ins = pe.matmul(p.PVP[b64:b64 + 64, half * 512:(half + 1) * 512],
                                            vta(g_loc, kc)[:, 0:64],
                                            PT[kc % 2][:, half * 512:(half + 1) * 512],
                                            start=(kc == 0), stop=(kc == 15),
                                            tile_position=tpos_pv)
                        ins.then_inc(p.sem["pe"], 1)
                        p.bump("pe")
                        ins = None
                        for half, zp in ((0, p.ZAp), (1, p.ZBp)):
                            ins = pe.matmul(zp[zrow:zrow + 1, :],
                                            vta(g_loc, kc)[:, 64:65],
                                            PT[kc % 2][:, half * 512:(half + 1) * 512],
                                            start=(kc == 0), stop=(kc == 15),
                                            tile_position=tpos_z)
                        ins.then_inc(p.sem["pe"], 1)
                        p.bump("pe")
                        if kc + 2 < 16:
                            emit_S(kc + 2)
                    pe.wait_ge(p.sem["dve"], D0 + 33 + hd * 3 + 1)
                    zr = p.rowsp(0, 1024)
                    pe.matmul(p.ZAp[:, :], p.ONESROW, zr[0:1, 0:512],
                              start=True, stop=True)
                    pe.matmul(p.ZBp[:, :], p.ONESROW, zr[0:1, 512:1024],
                              start=True, stop=True).then_inc(p.sem["pe"], 1)
                    p.bump("pe")
                assert p.c["pe"] == P0 + 40 + 8 * 49

            @blk.scalar
            def _(a):
                gi = 0
                for j in range(4):
                    for th in range(2):
                        ps = (p.S01, p.S23)[(gi % 4) // 2][:, (gi % 2) * 512:(gi % 2) * 512 + 512]
                        a.wait_ge(p.sem["pe"], P0 + 32 + gi + 1)
                        a.activation(
                            QSB[:, j * 1024 + th * 512:j * 1024 + (th + 1) * 512],
                            ps, AF.Identity,
                            bias=p.smv(270 + 4 * pr + j, 271 + 4 * pr + j),
                            scale=p.smv(267, 268)).then_inc(p.sem["act"], 1)
                        p.bump("act")
                        gi += 1
                a.wait_ge(p.sem["dve"], D0 + 1)
                for hd in range(8):
                    for kc in range(16):
                        Sp = (p.S01, p.S23)[kc % 2]
                        a.wait_ge(p.sem["pe"], p.l1_s_inc[(hd, kc)])
                        a.activation(PT[kc % 2], Sp[:, :], AF.Exp,
                                     bias=p.smv(266, 267)).then_inc(p.sem["act"], 1)
                        p.bump("act")
                assert p.c["act"] == A0 + 8 + 128

            @blk.gpsimd
            def _(g):
                p.flush_w(g)


MOE_C = 512


def emit_moe(p, t):
    nc = p.nc
    IOA = bass.IndirectOffsetOnAxis
    RGWB = p.AFR[:, 2304:2560]
    LTRI = p.AFR[:, 2176:2304]
    BASE = p.AFR[0:1, 2560:2576]
    ECROW = p.AFR[0:1, 2576:2592]
    ECB = p.SG[:, 656:672]
    MSCR = p.SG[:, 672:688]
    onesf = p.smv(310, 311)

    def eq1m(tc):
        return p.SG[:, 384 + tc * 16:384 + (tc + 1) * 16]

    def eq2m(tc):
        return p.SG[:, 512 + tc * 16:512 + (tc + 1) * 16]

    def wcol(tc, i):
        return p.smv(246 + 2 * tc + i, 247 + 2 * tc + i)

    def xo_piece(i):
        return p.XB[:, i * SEQ:i * SEQ + 1024]

    def xe(hc):
        return p.AB[:, 8192 + hc * 512:8192 + (hc + 1) * 512]

    def hte(ic):
        pc = 12 + ic // 2
        off = (ic % 2) * 512
        return p.XB[:, pc * SEQ + off:pc * SEQ + off + 512]

    def oute(sc, half):
        pc = (sc % 2) * 2 + half
        return p.XB[:, pc * SEQ:pc * SEQ + 1024]

    def d_split_piece(k, hi):
        if k < 4:
            piece = 4 + 2 * k + hi
            return p.XB[:, piece * SEQ:piece * SEQ + 1024]
        piece = 8 + 2 * (k - 4) + hi
        return p.XB[:, piece * SEQ + 1024:(piece + 1) * SEQ]

    p.set_ring([p.AB[:, i * 2048:(i + 1) * 2048] for i in range(4)])

    P0, A0, D0 = p.c["pe"], p.c["act"], p.c["dve"]
    pre_w = p.c["w"]
    rgw_w = pre_w + 3
    with nc.Block() as blk:
        @blk.gpsimd
        def _(g):
            g.wait_ge(p.sem["w"], pre_w * 16)
            g.dma_start(LTRI, t["ltri"]).then_inc(p.sem["w"], 16)
            g.dma_start(ECROW, t["ecrow"]).then_inc(p.sem["w"], 16)
            g.dma_start(RGWB, t["rgw"]).then_inc(p.sem["w"], 16)
            p.bump("w", 3)

        @blk.tensor
        def _(pe):
            pe.wait_ge(p.sem["w"], rgw_w * 16)
            for tc in range(8):
                zp = (p.ZAp, p.ZBp)[tc % 2]
                if tc >= 2:
                    pe.wait_ge(p.sem["act"], A0 + (tc - 2) + 1)
                ins = None
                for hc in range(16):
                    ins = pe.matmul(zp[:, 0:16],
                                    p.xf(hc, tc * 128, (tc + 1) * 128),
                                    RGWB[:, hc * 16:(hc + 1) * 16],
                                    start=(hc == 0), stop=(hc == 15))
                ins.then_inc(p.sem["pe"], 1)
                p.bump("pe")
            pe.wait_ge(p.sem["act"], A0 + 8)
            pe.matmul(p.ZAp[:, 0:16], p.ONESROW, ECROW,
                      start=True, stop=True).then_inc(p.sem["pe"], 1)
            p.bump("pe")

        @blk.scalar
        def _(a):
            for tc in range(8):
                zp = (p.ZAp, p.ZBp)[tc % 2]
                a.wait_ge(p.sem["pe"], P0 + tc + 1)
                a.activation(p.SG[:, tc * 16:(tc + 1) * 16], zp[:, 0:16],
                             AF.Exp).then_inc(p.sem["act"], 1)
                p.bump("act")

        @blk.vector
        def _(v):
            def step(ins):
                ins.then_inc(p.sem["dve"], 1)
                p.bump("dve")
                v.wait_ge(p.sem["dve"], p.c["dve"])

            v.memset(BASE, 0.0)
            c0, c1, c2, c3 = (p.smv(240, 241), p.smv(241, 242),
                              p.smv(242, 243), p.smv(243, 244))
            for tc in range(8):
                pp = p.SG[:, tc * 16:(tc + 1) * 16]
                p2 = p.SG[:, 128 + tc * 16:128 + (tc + 1) * 16]
                v.wait_ge(p.sem["act"], A0 + tc + 1)
                step(v.tensor_reduce(c0, pp, AX.X, ALU.max))
                step(v.tensor_scalar(eq1m(tc), pp, c0, None, ALU.is_equal))
                step(v.scalar_tensor_tensor(p2, eq1m(tc), -1e30, pp,
                                            ALU.mult, ALU.add))
                step(v.tensor_reduce(c1, p2, AX.X, ALU.max))
                step(v.tensor_tensor(c2, c0, c1, ALU.add))
                step(v.reciprocal(c3, c2))
                step(v.tensor_tensor(wcol(tc, 0), c0, c3, ALU.mult))
                step(v.tensor_tensor(wcol(tc, 1), c1, c3, ALU.mult))
                step(v.tensor_scalar(eq2m(tc), p2, c1, None, ALU.is_equal))
            v.wait_ge(p.sem["pe"], P0 + 9)
            v.tensor_copy(ECB, p.ZAp[:, 0:16]).then_inc(p.sem["dve"], 1)
            p.bump("dve")

    P0, A0, D0 = p.c["pe"], p.c["act"], p.c["dve"]
    with nc.Block() as blk:
        @blk.tensor
        def _(pe):
            for tc in range(8):
                zp = (p.ZAp, p.ZBp)[tc % 2]
                zrow = (p.ZBp, p.ZAp)[tc % 2]
                pe.wait_ge(p.sem["dve"], D0 + 10 * tc + 2)
                pe.matmul(zp[:, 0:16], LTRI, MSCR, start=True, stop=False)
                pe.matmul(zp[:, 0:16], p.ONESROW, BASE, start=False, stop=True)
                pe.matmul(zrow[0:1, 0:16], onesf, MSCR,
                          start=True, stop=True).then_inc(p.sem["pe"], 2)
                p.bump("pe", 2)

        @blk.vector
        def _(v):
            def step(ins):
                ins.then_inc(p.sem["dve"], 1)
                p.bump("dve")
                v.wait_ge(p.sem["dve"], p.c["dve"])

            c0 = p.smv(240, 241)
            sv = p.SG[:, 688:704]
            tmp = p.SG[:, 704:720]
            ecap = p.SG[:, 720:736]
            step(v.tensor_scalar(ecap, ECB, float(MOE_C), None, ALU.add))
            for tc in range(8):
                step(v.tensor_tensor(MSCR, eq1m(tc), eq2m(tc), ALU.add))
                v.wait_ge(p.sem["pe"], P0 + 2 * tc + 2)
                zp = (p.ZAp, p.ZBp)[tc % 2]
                zrow = (p.ZBp, p.ZAp)[tc % 2]
                step(v.tensor_tensor(sv, zp[:, 0:16], ECB, ALU.add))
                step(v.tensor_tensor(sv, sv, ecap, ALU.min))
                step(v.tensor_tensor(tmp, sv, eq1m(tc), ALU.mult))
                step(v.tensor_reduce(c0, tmp, AX.X, ALU.add))
                step(v.tensor_copy(p.SLOTS[:, 2 * tc:2 * tc + 1], c0))
                step(v.tensor_tensor(tmp, sv, eq2m(tc), ALU.mult))
                step(v.tensor_reduce(c0, tmp, AX.X, ALU.add))
                step(v.tensor_copy(p.SLOTS[:, 2 * tc + 1:2 * tc + 2], c0))
                step(v.tensor_tensor(BASE, BASE, zrow[0:1, 0:16], ALU.add))

    P0, A0, D0 = p.c["pe"], p.c["act"], p.c["dve"]
    with nc.Block() as blk:
        @blk.tensor
        def _(pe):
            for idx in range(128):
                tc, hc = idx // 16, idx % 16
                if idx >= 2:
                    pe.wait_ge(p.sem["dve"], D0 + idx - 1)
                dstp = (p.ZAp, p.ZBp)[idx % 2][:, 0:128]
                pe.transpose(dstp, p.xf(hc, tc * 128, (tc + 1) * 128),
                             p.ID32).then_inc(p.sem["pe"], 1)
                p.bump("pe")

        @blk.vector
        def _(v):
            for idx in range(128):
                tc, hc = idx // 16, idx % 16
                v.wait_ge(p.sem["pe"], P0 + idx + 1)
                v.tensor_copy(
                    p.AO[:, tc * 2048 + hc * 128:tc * 2048 + (hc + 1) * 128],
                    (p.ZAp, p.ZBp)[idx % 2][:, 0:128]).then_inc(p.sem["dve"], 1)
                p.bump("dve")

        @blk.gpsimd
        def _(g):
            for tc in range(8):
                g.wait_ge(p.sem["dve"], D0 + (tc + 1) * 16)
                for i in range(2):
                    g.indirect_dma_start(
                        t["ein"][:, :],
                        IOA(ap=p.SLOTS[:, 2 * tc + i:2 * tc + i + 1], axis=0),
                        p.AO[:, tc * 2048:(tc + 1) * 2048],
                        None).then_inc(p.sem["ind"], 16)
                    p.bump("ind")
    scatter_done_ind = p.c["ind"]

    NSC = MOE_C // 128

    def xe_dma_dest(sc):
        return (p.AB[:, 8192:16384]
                .rearrange("q (c s) -> q c s", c=16)
                [:, :, sc * 128:sc * 128 + 128])

    def queue_expert_head(e):
        for i in range(2):
            p.queue_slab(f"g{e}_{i}", t["wg"][e, i])
            p.queue_slab(f"u{e}_{i}", t["wu"][e, i])
        for k in range(8):
            gk = f"d{e-1}_{k}" if e > 0 else None
            dsrc = t["wd"][e, k]
            p.queue_slab_at(f"d{e}_{k}", None, None, guard_key=gk, parts=[
                (dsrc[:, 0:1024], d_split_piece(k, 0)),
                (dsrc[:, 1024:2048], d_split_piece(k, 1))])

    def queue_expert_tail(e):
        for i in range(2, 8):
            p.queue_slab(f"g{e}_{i}", t["wg"][e, i])
            p.queue_slab(f"u{e}_{i}", t["wu"][e, i])

    for e in range(16):
        P0, A0, D0 = p.c["pe"], p.c["act"], p.c["dve"]
        X0 = p.c["x"]
        if e == 0:
            queue_expert_head(0)
        queue_expert_tail(e)
        if e < 15:
            queue_expert_head(e + 1)

        with nc.Block() as blk:
            @blk.sync
            def _(sy):
                if e == 0:
                    sy.wait_ge(p.sem["ind"], scatter_done_ind * 16)
                for sc in range(NSC):
                    sy.dma_start(xe_dma_dest(sc),
                                 t["ein"][e * MOE_C + sc * 128:e * MOE_C + (sc + 1) * 128, :],
                                 transpose=True).then_inc(p.sem["x"], 16)
                    p.bump("x")
                for sc in range(NSC):
                    sy.wait_ge(p.sem["dve"], D0 + 8 + 4 * sc + 4)
                    for half in range(2):
                        sy.dma_start(
                            t["eout"][e * MOE_C + sc * 128:e * MOE_C + (sc + 1) * 128,
                                      half * 1024:(half + 1) * 1024],
                            oute(sc, half)).then_inc(p.sem["x"], 16)
                        p.bump("x")

            @blk.tensor
            def _(pe):
                pe.wait_ge(p.sem["x"], (X0 + NSC) * 16)
                for i in range(8):
                    pe.wait_ge(p.sem["w"], p.slab_wait(f"u{e}_{i}"))
                    gs = p.slab(f"g{e}_{i}")
                    us = p.slab(f"u{e}_{i}")
                    ps = p.S01[:, (i % 2) * 512:(i % 2) * 512 + 512]
                    if i >= 2:
                        pe.wait_ge(p.sem["act"], A0 + (i - 2) + 1)
                    ins = None
                    for hc in range(16):
                        ins = pe.matmul(ps, gs[:, hc * 128:(hc + 1) * 128],
                                        xe(hc), start=(hc == 0), stop=(hc == 15))
                    ins.then_inc(p.sem["pe"], 1)
                    p.bump("pe")
                    ps = p.S23[:, (i % 2) * 512:(i % 2) * 512 + 512]
                    if i >= 2:
                        pe.wait_ge(p.sem["dve"], D0 + (i - 2) + 1)
                    ins = None
                    for hc in range(16):
                        ins = pe.matmul(ps, us[:, hc * 128:(hc + 1) * 128],
                                        xe(hc), start=(hc == 0), stop=(hc == 15))
                    ins.then_inc(p.sem["pe"], 1)
                    p.bump("pe")
                    p.free_slab(f"g{e}_{i}", p.c["pe"])
                    p.free_slab(f"u{e}_{i}", p.c["pe"])
                pe.wait_ge(p.sem["w"], p.slab_wait(f"d{e}_7"))
                pe.wait_ge(p.sem["dve"], D0 + 8)
                gi = 0
                for sc in range(NSC):
                    for jq in range(4):
                        rot = gi % 4
                        ps = (p.PVP, p.S01)[rot // 2][:, (rot % 2) * 512:(rot % 2) * 512 + 512]
                        if gi == 2:
                            pe.wait_ge(p.sem["act"], A0 + 8)
                        if gi >= 4:
                            pe.wait_ge(p.sem["dve"], D0 + 8 + (gi - 4) + 1)
                        ins = None
                        for i in range(8):
                            rhs = d_split_piece(i, jq // 2)[:, (jq % 2) * 512:(jq % 2) * 512 + 512]
                            ins = pe.matmul(ps, hte(i)[:, sc * 128:(sc + 1) * 128],
                                            rhs, start=(i == 0), stop=(i == 7))
                        ins.then_inc(p.sem["pe"], 1)
                        p.bump("pe")
                        gi += 1
                for i in range(8):
                    p.free_slab(f"d{e}_{i}", p.c["pe"])

            @blk.vector
            def _(v):
                for i in range(8):
                    v.wait_ge(p.sem["pe"], P0 + 2 * i + 2)
                    v.wait_ge(p.sem["act"], A0 + i + 1)
                    v.tensor_tensor(
                        hte(i), p.SG[:, (i % 2) * 512:(i % 2) * 512 + 512],
                        p.S23[:, (i % 2) * 512:(i % 2) * 512 + 512],
                        ALU.mult).then_inc(p.sem["dve"], 1)
                    p.bump("dve")
                gi = 0
                for sc in range(NSC):
                    for jq in range(4):
                        rot = gi % 4
                        ps = (p.PVP, p.S01)[rot // 2][:, (rot % 2) * 512:(rot % 2) * 512 + 512]
                        v.wait_ge(p.sem["pe"], P0 + 16 + gi + 1)
                        if sc >= 2:
                            v.wait_ge(p.sem["x"], (X0 + NSC + 2 * (sc - 2) + 2) * 16)
                        v.tensor_copy(oute(sc, jq // 2)[:, (jq % 2) * 512:(jq % 2) * 512 + 512],
                                      ps).then_inc(p.sem["dve"], 1)
                        p.bump("dve")
                        gi += 1

            @blk.scalar
            def _(a):
                for i in range(8):
                    a.wait_ge(p.sem["pe"], P0 + 2 * i + 1)
                    if i >= 2:
                        a.wait_ge(p.sem["dve"], D0 + (i - 2) + 1)
                    a.activation(p.SG[:, (i % 2) * 512:(i % 2) * 512 + 512],
                                 p.S01[:, (i % 2) * 512:(i % 2) * 512 + 512],
                                 AF.Silu).then_inc(p.sem["act"], 1)
                    p.bump("act")

            @blk.gpsimd
            def _(g):
                p.flush_w(g)

    eout_done_x = p.c["x"]

    for tc in range(8):
        P0, A0, D0 = p.c["pe"], p.c["act"], p.c["dve"]
        I0 = p.c["ind"]
        o1 = p.AB[:, 0:2048]
        o2 = p.AB[:, 2048:4096]
        mile = {}
        with nc.Block() as blk:
            @blk.gpsimd
            def _(g):
                if tc == 0:
                    g.wait_ge(p.sem["x"], eout_done_x * 16)
                for i, dst in ((0, o1), (1, o2)):
                    g.indirect_dma_start(
                        dst, None, t["eout"][:, :],
                        IOA(ap=p.SLOTS[:, 2 * tc + i:2 * tc + i + 1], axis=0)
                    ).then_inc(p.sem["ind"], 16)
                    p.bump("ind")

            @blk.vector
            def _(v):
                def step(ins):
                    ins.then_inc(p.sem["dve"], 1)
                    p.bump("dve")
                    v.wait_ge(p.sem["dve"], p.c["dve"])
                if tc > 0:
                    v.wait_ge(p.sem["out"], p.c["out"] * 16)
                v.wait_ge(p.sem["ind"], (I0 + 2) * 16)
                step(v.tensor_scalar(p.ZT, o2, wcol(tc, 1), None, ALU.mult))
                v.scalar_tensor_tensor(p.ZT, o1, wcol(tc, 0), p.ZT,
                                       ALU.mult, ALU.add)
                v.tensor_tensor(p.ZT, p.ZT,
                                p.AO[:, tc * 2048:(tc + 1) * 2048],
                                ALU.add).then_inc(p.sem["dve"], 1)
                p.bump("dve")
                mile["zt"] = p.c["dve"]
                step(v.tensor_reduce(p.smv(241, 242), p.ZT, AX.X, ALU.add))
                v.wait_ge(p.sem["act"], A0 + 1)
                step(v.tensor_scalar(p.smv(242, 243), p.smv(241, 242), 1.0 / D, None, ALU.mult))
                step(v.tensor_scalar(p.smv(243, 244), p.smv(240, 241), 1.0 / D, None, ALU.mult))
                step(v.tensor_tensor(p.smv(245, 246), p.smv(242, 243), p.smv(242, 243), ALU.mult))
                step(v.tensor_tensor(p.smv(243, 244), p.smv(243, 244), p.smv(245, 246), ALU.subtract))
                mile["var"] = p.c["dve"]
                v.wait_ge(p.sem["act"], A0 + 2)
                step(v.reciprocal(p.smv(244, 245), p.smv(246, 247)))
                step(v.tensor_tensor(p.smv(245, 246), p.smv(242, 243), p.smv(244, 245), ALU.mult))
                v.tensor_scalar(p.ZT, p.ZT, p.smv(244, 245), p.smv(245, 246),
                                ALU.mult, ALU.subtract).then_inc(p.sem["dve"], 1)
                p.bump("dve")
                mile["zn"] = p.c["dve"]

            @blk.scalar
            def _(a):
                a.wait_ge(p.sem["dve"], mile["zt"])
                a.activation(p.AB[:, 4096:6144], p.ZT, AF.Square,
                             accum_out=p.smv(240, 241)).then_inc(p.sem["act"], 1)
                p.bump("act")
                a.wait_ge(p.sem["dve"], mile["var"])
                a.activation(p.smv(246 + 16, 247 + 16) if False else p.smv(246, 247),
                             p.smv(243, 244), AF.Sqrt,
                             bias=p.smv(265, 266)).then_inc(p.sem["act"], 1)
                p.bump("act")

        with nc.Block() as blk:
            @blk.sync
            def _(sy):
                sy.dma_start(t["out"][tc * 128:(tc + 1) * 128, :],
                             p.ZT).then_inc(p.sem["out"], 16)
                p.bump("out")

    with nc.Block() as blk:
        @blk.sync
        def _(sy):
            sy.wait_ge(p.sem["out"], p.c["out"] * 16)
            sy.wait_ge(p.sem["x"], p.c["x"] * 16)
            sy.wait_ge(p.sem["w"], p.c["w"] * 16)
            sy.wait_ge(p.sem["ind"], p.c["ind"] * 16)
            sy.wait_ge(p.sem["cc"], 1)


def emit_store_x2(p, t):
    nc = p.nc
    with nc.Block() as blk:
        @blk.sync
        def _(sy):
            dst = t["x2T"].rearrange("(c p) s -> p c s", p=128)
            for i in range(2):
                src = p.XF[:, i * 8 * TOK:(i + 1) * 8 * TOK].rearrange(
                    "q (c s) -> q c s", c=8)
                sy.dma_start(dst[:, i * 8:(i + 1) * 8, :], src).then_inc(p.sem["out"], 16)
                p.bump("out")
            sy.wait_ge(p.sem["out"], p.c["out"] * 16)
            sy.wait_ge(p.sem["x"], p.c["x"] * 16)
            sy.wait_ge(p.sem["w"], p.c["w"] * 16)



def build_l0():
    nc = bass.Bass("TRN2", target_bir_lowering=False)
    t = {}
    t["xT"] = nc.dram_tensor("xT", [D, SEQ], F32, kind="ExternalInput").ap()
    for nm in ("wq", "wk", "wv", "wo"):
        t[nm] = nc.dram_tensor(nm, [16, 128, D], F32, kind="ExternalInput").ap()
    t["b_attn"] = nc.dram_tensor("b_attn", [128, 64], F32, kind="ExternalInput").ap()
    t["gates"] = nc.dram_tensor("gates", [1, 5], F32, kind="ExternalInput").ap()
    t["obc"] = nc.dram_tensor("obc", [128, 16], F32, kind="ExternalInput").ap()
    t["ident"] = nc.dram_tensor("ident", [128, 128], F32, kind="ExternalInput").ap()
    t["gw"] = nc.dram_tensor("gw", [64, 128, D], I32, kind="ExternalInput").ap()
    t["uw"] = nc.dram_tensor("uw", [64, 128, D], I32, kind="ExternalInput").ap()
    t["dw"] = nc.dram_tensor("dw", [64, 128, D], I32, kind="ExternalInput").ap()
    t["gsc"] = nc.dram_tensor("gsc", [128, 64], F32, kind="ExternalInput").ap()
    t["usc"] = nc.dram_tensor("usc", [128, 64], F32, kind="ExternalInput").ap()
    t["dsc"] = nc.dram_tensor("dsc", [128, 16], F32, kind="ExternalInput").ap()
    t["x2T"] = nc.dram_tensor("x2T", [D, TOK], F32, kind="ExternalOutput").ap()
    p = P(nc)
    emit_init(p, t, l0=True)
    emit_attn_l0(p, t)
    emit_outproj(p, t["wo"], bias_col0=48, n_queued=4)
    emit_ln(p, p.AB[:, 11264:13312])
    emit_ffn_l0(p, t)
    emit_ln(p, p.AB[:, 0:2048])
    emit_store_x2(p, t)
    return nc


def build_fused(dbg_x3=False):
    nc = bass.Bass("TRN2", target_bir_lowering=False)
    t = {}
    t["xT"] = nc.dram_tensor("xT", [D, SEQ], F32, kind="ExternalInput").ap()
    for nm in ("wq", "wk", "wv", "wo"):
        t[nm] = nc.dram_tensor(nm, [16, 128, D], F32, kind="ExternalInput").ap()
    t["b_attn"] = nc.dram_tensor("b_attn", [128, 64], F32, kind="ExternalInput").ap()
    t["gates"] = nc.dram_tensor("gates", [1, 5], F32, kind="ExternalInput").ap()
    t["obc"] = nc.dram_tensor("obc", [128, 16], F32, kind="ExternalInput").ap()
    t["ident"] = nc.dram_tensor("ident", [128, 128], F32, kind="ExternalInput").ap()
    t["gw"] = nc.dram_tensor("gw", [64, 128, D], I32, kind="ExternalInput").ap()
    t["uw"] = nc.dram_tensor("uw", [64, 128, D], I32, kind="ExternalInput").ap()
    t["dw"] = nc.dram_tensor("dw", [64, 128, D], I32, kind="ExternalInput").ap()
    t["gsc"] = nc.dram_tensor("gsc", [128, 64], F32, kind="ExternalInput").ap()
    t["usc"] = nc.dram_tensor("usc", [128, 64], F32, kind="ExternalInput").ap()
    t["dsc"] = nc.dram_tensor("dsc", [128, 16], F32, kind="ExternalInput").ap()
    t["wqkv"] = nc.dram_tensor("wqkv", [24, 128, D], F32, kind="ExternalInput").ap()
    t["b_qkv"] = nc.dram_tensor("b_qkv", [128, 24], F32, kind="ExternalInput").ap()
    t["wout"] = nc.dram_tensor("wout", [16, 128, D], F32, kind="ExternalInput").ap()
    t["b_out"] = nc.dram_tensor("b_out", [128, 16], F32, kind="ExternalInput").ap()
    t["rgw"] = nc.dram_tensor("rgw", [128, 256], F32, kind="ExternalInput").ap()
    t["wg"] = nc.dram_tensor("wg", [16, 8, 128, D], F32, kind="ExternalInput").ap()
    t["wu"] = nc.dram_tensor("wu", [16, 8, 128, D], F32, kind="ExternalInput").ap()
    t["wd"] = nc.dram_tensor("wd", [16, 8, 128, D], F32, kind="ExternalInput").ap()
    t["cc_in"] = nc.dram_tensor("cc_in", [1024, TOK], BF16, kind="Internal").ap()
    t["cc_out"] = nc.dram_tensor("cc_out", [2048, TOK], BF16, kind="Internal").ap()
    t["ltri"] = nc.dram_tensor("ltri", [128, 128], F32, kind="ExternalInput").ap()
    t["ecrow"] = nc.dram_tensor("ecrow", [1, 16], F32, kind="ExternalInput").ap()
    t["ein"] = nc.dram_tensor("ein", [16 * MOE_C, D], BF16, kind="Internal").ap()
    t["eout"] = nc.dram_tensor("eout", [16 * MOE_C, D], BF16, kind="Internal").ap()
    t["out"] = nc.dram_tensor("out", [TOK, D], F32, kind="ExternalOutput").ap()
    if dbg_x3:
        t["x2T"] = nc.dram_tensor("x2T", [D, TOK], F32, kind="ExternalOutput").ap()
        t["x3T"] = nc.dram_tensor("x3T", [D, TOK], F32, kind="ExternalOutput").ap()
    p = P(nc)
    emit_init(p, t, l0=True)
    emit_attn_l0(p, t)
    emit_outproj(p, t["wo"], bias_col0=48, n_queued=4)
    emit_ln(p, p.AB[:, 11264:13312])
    emit_ffn_l0(p, t)
    emit_ln(p, p.AB[:, 0:2048])
    if dbg_x3:
        with nc.Block() as blk:
            @blk.sync
            def _(sy):
                dst = t["x2T"].rearrange("(c q) s -> q c s", q=128)
                for i in range(2):
                    src_ = p.XF[:, i * 8 * TOK:(i + 1) * 8 * TOK].rearrange(
                        "q (c s) -> q c s", c=8)
                    sy.dma_start(dst[:, i * 8:(i + 1) * 8, :], src_).then_inc(p.sem["out"], 16)
                    p.bump("out")
    emit_kv_exchange(p, t)
    emit_attn_l1(p, t)
    emit_outproj(p, t["wout"], bias_col0=294, n_queued=0)
    emit_ln(p, p.AB[:, 0:2048])
    if dbg_x3:
        with nc.Block() as blk:
            @blk.sync
            def _(sy):
                dst = t["x3T"].rearrange("(c q) s -> q c s", q=128)
                for i in range(2):
                    src_ = p.XF[:, i * 8 * TOK:(i + 1) * 8 * TOK].rearrange(
                        "q (c s) -> q c s", c=8)
                    sy.dma_start(dst[:, i * 8:(i + 1) * 8, :], src_).then_inc(p.sem["out"], 16)
                    p.bump("out")
    emit_moe(p, t)
    return nc


def kernel(**inp):
    f32 = np.float32
    x = np.asarray(inp["L0_attn_hidden_states"], f32)

    perm = l1_q_perm()
    w_qkv = np.asarray(inp["L1_attn_w_qkv"], f32)
    wqkv_al = np.concatenate([w_qkv[0:2048][perm], w_qkv[2048:3072]], axis=0)
    b_qkv = np.asarray(inp["L1_attn_b_qkv"], f32)
    b_al = np.concatenate([b_qkv[0:2048][perm], b_qkv[2048:3072]])
    w_out = np.asarray(inp["L1_attn_w_out"], f32)
    rg = np.asarray(inp["L1_ffn_gate_w"], f32)
    rgw = np.ascontiguousarray(
        rg.T.reshape(16, 128, 16).transpose(1, 0, 2).reshape(128, 256))

    shared = {
        "wq": lay_w_out(np.asarray(inp["L0_attn_Wq"], f32)),
        "wk": lay_w_out(np.asarray(inp["L0_attn_Wk"], f32)),
        "wv": lay_w_out(np.asarray(inp["L0_attn_Wv"], f32)),
        "wo": lay_w_out(np.asarray(inp["L0_attn_Wo"], f32)),
        "b_attn": np.concatenate([lay_vec(np.asarray(inp["L0_attn_b" + k], f32))
                                  for k in ("q", "k", "v", "o")], axis=1),
        "gates": np.stack([np.asarray(inp["L0_attn_truth_gate"], f32)[0],
                           np.asarray(inp["L0_attn_balance_gate"], f32)[0],
                           np.asarray(inp["L0_attn_order_gate"], f32)[0],
                           np.asarray(inp["L0_attn_justice_gate"], f32)[0],
                           np.asarray(inp["L0_attn_harmony_gate"], f32)[0]])[None, :].astype(f32),
        "ident": np.eye(128, dtype=f32),
        "gw": lay_w_out(np.asarray(inp["L0_ffn_gate_w"], np.int32)),
        "uw": lay_w_out(np.asarray(inp["L0_ffn_up_w"], np.int32)),
        "dw": lay_w_in(np.asarray(inp["L0_ffn_down_w"], np.int32)),
        "gsc": lay_vec(np.asarray(inp["L0_ffn_gate_scale"], f32)),
        "usc": lay_vec(np.asarray(inp["L0_ffn_up_scale"], f32)),
        "dsc": lay_vec(np.asarray(inp["L0_ffn_down_scale"], f32)),
        "wqkv": lay_w_out(wqkv_al),
        "b_qkv": lay_vec(b_al),
        "wout": lay_w_out(np.ascontiguousarray(w_out[:, perm])),
        "b_out": lay_vec(np.asarray(inp["L1_attn_b_out"], f32)),
        "rgw": rgw,
        "wg": lay_moe_up(np.asarray(inp["L1_ffn_wg"], f32)),
        "wu": lay_moe_up(np.asarray(inp["L1_ffn_wu"], f32)),
        "wd": lay_moe_down(np.asarray(inp["L1_ffn_wd"], f32)),
        "ltri": np.tril(np.ones((128, 128), dtype=f32)).T.copy(),
        "ecrow": (np.arange(16, dtype=f32) * MOE_C - 1.0)[None, :].copy(),
    }

    pos = np.arange(SEQ, dtype=f32) / SEQ * 0.05
    in_maps = []
    for c in range(8):
        b, hh = c // 2, c % 2
        xT = np.ascontiguousarray(x[b].T)
        own = slice(hh * TOK, (hh + 1) * TOK)
        par = slice((1 - hh) * TOK, (2 - hh) * TOK)
        m = dict(shared)
        m["xT"] = np.ascontiguousarray(np.concatenate([xT[:, own], xT[:, par]], axis=1))
        m["obc"] = lay_vec(np.concatenate([pos[own], pos[par]]))
        in_maps.append(m)

    nc = build_fused()
    res = run_bass_kernel_spmd(nc, in_maps, core_ids=list(range(8)))

    out = np.empty((4, SEQ, D), np.float32)
    for c in range(8):
        b, hh = c // 2, c % 2
        out[b, hh * TOK:(hh + 1) * TOK, :] = np.asarray(res.results[c]["out"])
    return out


# version 09
# speedup vs baseline: 1.1569x; 1.1569x over previous
import numpy as np
import ml_dtypes
import concourse.bass as bass
import concourse.mybir as mybir
from concourse.bass_utils import run_bass_kernel_spmd

F32 = mybir.dt.float32
BF16 = mybir.dt.bfloat16
I32 = mybir.dt.int32
AF = mybir.ActivationFunctionType
ALU = mybir.AluOpType
AX = mybir.AxisListType

D = 2048
SEQ = 2048
TOK = 1024
C_STAB = 4.0
LN_EPS = 1e-5
BF = ml_dtypes.bfloat16



def lay_w_out(W):
    O, I = W.shape
    return np.ascontiguousarray(
        W.reshape(O // 128, 128, I // 128, 128).transpose(0, 3, 2, 1).reshape(O // 128, 128, I))


def lay_w_in(W):
    O, I = W.shape
    return np.ascontiguousarray(
        W.reshape(O // 128, 128, I // 128, 128).transpose(2, 3, 0, 1).reshape(I // 128, 128, O))


def lay_vec(b):
    return np.ascontiguousarray(b.reshape(-1, 128).T)


def lay_moe_up(w):
    E, H, I = w.shape
    return np.ascontiguousarray(
        w.reshape(E, H // 128, 128, I // 128, 128).transpose(0, 3, 2, 1, 4).reshape(E, I // 128, 128, H))


def lay_moe_down(w):
    E, I, H = w.shape
    return np.ascontiguousarray(w.reshape(E, I // 128, 128, H))


def l1_q_perm():
    perm = np.empty(2048, np.int64)
    for a in range(2048):
        p_ = a // 512
        r = (a // 128) % 4
        gpar = (a // 64) % 2
        d = a % 64
        perm[a] = (2 * p_ + gpar) * 256 + r * 64 + d
    return perm



class P:
    def __init__(self, nc):
        self.nc = nc
        self.c = {"pe": 0, "act": 0, "dve": 0, "w": 0, "x": 0, "out": 0, "ind": 0}
        self.XB = nc.alloc_sbuf_tensor("XB", [128, 16 * SEQ], BF16)
        self.XF = nc.alloc_sbuf_tensor("XF", [128, 16 * TOK], F32)
        self.AO = nc.alloc_sbuf_tensor("AO", [128, 16 * TOK], BF16)
        self.AB = nc.alloc_sbuf_tensor("AB", [128, 16384], BF16)
        self.AFR = nc.alloc_sbuf_tensor("AFR", [128, 3590], F32)
        self.SM = nc.alloc_sbuf_tensor("SM", [128, 312], F32)
        self.SMB = nc.alloc_sbuf_tensor("SMB", [128, 8], BF16)
        self.SLOTS = nc.alloc_sbuf_tensor("SLOTS", [128, 16], I32)
        self.S01 = nc.alloc_psum_tensor("S01", [128, 1024], F32)
        self.S23 = nc.alloc_psum_tensor("S23", [128, 1024], F32)
        self.PVP = nc.alloc_psum_tensor("PVP", [128, 1024], F32)
        self.ZAp = nc.alloc_psum_tensor("ZAp", [128, 512], F32)
        self.ZBp = nc.alloc_psum_tensor("ZBp", [128, 512], F32)
        self.sem = {k: nc.alloc_semaphore("sem_" + k)
                    for k in ("pe", "act", "dve", "w", "x", "out", "cc", "ind")}
        self.ring = []
        self.ring_next = 0
        self.last_in_slot = {}
        self.freed = {}
        self.w_pos = {}
        self.w_queue = []
        self.slab_home = {}

    def bump(self, k, n=1):
        self.c[k] += n
        return self.c[k]

    def xb(self, hc, a, b):
        return self.XB[:, hc * SEQ + a: hc * SEQ + b]

    def xf(self, hc, a, b):
        return self.XF[:, hc * TOK + a: hc * TOK + b]

    def ao(self, hc, a, b):
        return self.AO[:, hc * TOK + a: hc * TOK + b]

    @property
    def V32(self):
        return self.AFR[:, 0:2048]

    @property
    def SG(self):
        return self.AFR[:, 0:1024]

    @property
    def RZB(self):
        return self.AFR[:, 0:1024]

    @property
    def ZT(self):
        return self.AFR[:, 0:2048]

    @property
    def ID32(self):
        return self.AFR[:, 2048:2176]

    @property
    def COMB(self):
        return self.AFR[:, 2176:2304]

    def rowsp(self, a, b):
        return self.AFR[0:1, 2304 + a:2304 + b]

    def rowalt(self, a, b):
        return self.AFR[0:1, a:b]

    @property
    def ONESROW(self):
        return self.AFR[0:1, 3328:3456]

    @property
    def NEGROW(self):
        return self.AFR[0:1, 3456:3584]

    @property
    def GATESROW(self):
        return self.AFR[0:1, 3584:3589]

    @property
    def ONESB(self):
        return self.SMB[:, 0:1]

    def smv(self, a, b):
        return self.SM[:, a:b]

    def set_ring(self, aps):
        self.ring = list(aps)
        self.ring_next = 0
        self.last_in_slot = {}
        self.freed = {}

    def queue_slab(self, key, src_ap):
        slot = self.ring_next % len(self.ring)
        self.ring_next += 1
        guard_key = self.last_in_slot.get(slot)
        self.last_in_slot[slot] = key
        self.w_queue.append((src_ap, ("ring", slot), guard_key))
        self.c["w"] += 1
        self.w_pos[key] = self.c["w"]
        self.slab_home[key] = ("ring", slot)

    def queue_slab_at(self, key, src_ap, dst_ap, guard_key=None, parts=None):
        if parts is None:
            self.w_queue.append((src_ap, ("fixed", dst_ap), guard_key))
            self.c["w"] += 1
        else:
            for (s, d_) in parts:
                self.w_queue.append((s, ("fixed", d_), guard_key))
                self.c["w"] += 1
                guard_key = None
        self.w_pos[key] = self.c["w"]
        self.slab_home[key] = ("fixed", dst_ap)

    def slab(self, key):
        kind, v = self.slab_home[key]
        return self.ring[v] if kind == "ring" else v

    def slab_wait(self, key):
        return self.w_pos[key] * 16

    def flush_w(self, g):
        for src, home, guard_key in self.w_queue:
            if guard_key is not None:
                if isinstance(guard_key, tuple) and guard_key[0] == "val":
                    g.wait_ge(self.sem["pe"], guard_key[1])
                else:
                    g.wait_ge(self.sem["pe"], self.freed[guard_key])
            kind, v = home
            dst = self.ring[v] if kind == "ring" else v
            g.dma_start(dst, src).then_inc(self.sem["w"], 16)
        self.w_queue.clear()

    def free_slab(self, key, pe_value):
        self.freed[key] = pe_value


def wge(eng, sem, v):
    if v > 0:
        eng.wait_ge(sem, v)



def emit_init(p, t, l0):
    nc = p.nc
    with nc.Block() as blk:
        @blk.sync
        def _(sy):
            src = t["xT"].rearrange("(c p) s -> p c s", p=128)
            for i in range(4):
                dst = p.XF[:, i * 4 * TOK:(i + 1) * 4 * TOK].rearrange(
                    "q (c s) -> q c s", c=4)
                sy.dma_start(dst, src[:, i * 4:(i + 1) * 4, 0:TOK]).then_inc(p.sem["x"], 16)
                p.bump("x")
            if l0:
                sy.dma_start(p.smv(0, 64), t["b_attn"]).then_inc(p.sem["x"], 16)
                sy.dma_start(p.smv(64, 80), t["obc"]).then_inc(p.sem["x"], 16)
                sy.dma_start(p.smv(80, 144), t["gsc"]).then_inc(p.sem["x"], 16)
                sy.dma_start(p.smv(144, 208), t["usc"]).then_inc(p.sem["x"], 16)
                sy.dma_start(p.smv(208, 224), t["dsc"]).then_inc(p.sem["x"], 16)
                sy.dma_start(p.GATESROW, t["gates"]).then_inc(p.sem["x"], 16)
                p.bump("x", 6)
            if "b_qkv" in t:
                sy.dma_start(p.smv(270, 294), t["b_qkv"]).then_inc(p.sem["x"], 16)
                sy.dma_start(p.smv(294, 310), t["b_out"]).then_inc(p.sem["x"], 16)
                p.bump("x", 2)
            sy.dma_start(p.ID32, t["ident"]).then_inc(p.sem["x"], 16)
            p.bump("x")

        @blk.gpsimd
        def _(g):
            src = t["xT"].rearrange("(c p) s -> p c s", p=128)
            for i in range(4):
                dst = p.XB[:, i * 4 * SEQ:(i + 1) * 4 * SEQ].rearrange(
                    "q (c s) -> q c s", c=4)
                g.dma_start(dst, src[:, i * 4:(i + 1) * 4, :]).then_inc(p.sem["w"], 16)
                p.bump("w")

        @blk.vector
        def _(v):
            v.memset(p.ONESB, 1.0)
            v.memset(p.ONESROW, 1.0)
            v.memset(p.NEGROW, -128.0)
            v.memset(p.smv(264, 265), 1.0 / D)
            v.memset(p.smv(265, 266), LN_EPS)
            v.memset(p.smv(266, 267), -C_STAB)
            v.memset(p.smv(267, 268), 0.125)
            v.memset(p.smv(310, 311), 1.0)

    if not l0:
        with nc.Block() as blk:
            @blk.sync
            def _(sy):
                sy.wait_ge(p.sem["x"], p.c["x"] * 16)
            @blk.tensor
            def _(pe):
                pe.wait_ge(p.sem["x"], p.c["x"] * 16)
            @blk.vector
            def _(v):
                v.wait_ge(p.sem["x"], p.c["x"] * 16)
                v.tensor_scalar(p.smv(0, 16), p.smv(0, 16), 0.125, None, ALU.mult)
    if l0:
        with nc.Block() as blk:
            @blk.tensor
            def _(pe):
                pe.wait_ge(p.sem["x"], p.c["x"] * 16)
                pe.matmul(p.ZAp[:, 0:5], p.ONESROW, p.GATESROW,
                          start=True, stop=True).then_inc(p.sem["pe"], 1)
                p.bump("pe")

            @blk.scalar
            def _(a):
                a.wait_ge(p.sem["pe"], p.c["pe"])
                a.activation(p.smv(257, 262), p.ZAp[:, 0:5],
                             AF.Sigmoid).then_inc(p.sem["act"], 1)
                p.bump("act")

            @blk.vector
            def _(v):
                v.wait_ge(p.sem["act"], p.c["act"])
                inv = 1.0 / np.sqrt(128.0)
                v.tensor_scalar(p.smv(256, 257), p.smv(260, 261), -0.1 * inv,
                                inv, ALU.mult, ALU.add)
                v.tensor_scalar(p.smv(240, 256), p.smv(64, 80),
                                p.smv(259, 260), -C_STAB, ALU.mult, ALU.add)
                ins = v.tensor_scalar(p.smv(224, 240), p.smv(0, 16),
                                      p.smv(256, 257), None, ALU.mult)
                if "b_qkv" in t:
                    ins = v.tensor_scalar(p.smv(270, 286), p.smv(270, 286),
                                          0.125, None, ALU.mult)
                ins.then_inc(p.sem["dve"], 1)
                p.bump("dve")


def emit_attn_l0(p, t):
    nc = p.nc
    p.set_ring([p.AB[:, i * 2048:(i + 1) * 2048] for i in range(4)])
    KSB = p.AB[:, 8192:10240]
    QSB = p.AB[:, 10240:11264]
    PT = [p.AB[:, 11264:12288], p.AB[:, 12288:13312]]
    VT = p.AB[:, 13312:15360]

    for h in range(16):
        P0, A0, D0 = p.c["pe"], p.c["act"], p.c["dve"]
        if h == 0:
            p.queue_slab("k0", t["wk"][0])
            p.queue_slab("v0", t["wv"][0])
            p.queue_slab("q0", t["wq"][0])
        if h < 15:
            p.queue_slab(f"k{h+1}", t["wk"][h + 1])
            p.queue_slab(f"v{h+1}", t["wv"][h + 1])
            p.queue_slab(f"q{h+1}", t["wq"][h + 1])
        else:
            for jc in range(4):
                p.queue_slab(f"o{jc}", t["wo"][jc])
        with nc.Block() as blk:
            @blk.tensor
            def _(pe):
                pe.wait_ge(p.sem["w"], p.slab_wait(f"q{h}"))
                if h == 0:
                    wge(pe, p.sem["dve"], p.c["dve"])
                gi = 0
                for kind, n_t, key in (("k", 4, f"k{h}"), ("v", 4, f"v{h}"),
                                       ("q", 2, f"q{h}")):
                    slab = p.slab(key)
                    for tt in range(n_t):
                        ps = (p.S01, p.S23)[(gi % 4) // 2][:, (gi % 2) * 512:(gi % 2) * 512 + 512]
                        if gi >= 4:
                            pe.wait_ge(p.sem["act"], A0 + gi - 4 + 1)
                        ins = None
                        for hc in range(16):
                            ins = pe.matmul(ps, slab[:, hc * 128:(hc + 1) * 128],
                                            p.xb(hc, tt * 512, (tt + 1) * 512),
                                            start=(hc == 0), stop=(hc == 15))
                        ins.then_inc(p.sem["pe"], 1)
                        p.bump("pe")
                        gi += 1
                    p.free_slab(key, p.c["pe"])
                for tt in range(16):
                    pe.wait_ge(p.sem["act"], A0 + 5 + tt // 4)
                    if tt >= 2:
                        pe.wait_ge(p.sem["dve"], D0 + tt - 1)
                    dstp = (p.ZAp, p.ZBp)[tt % 2][:, 0:128]
                    pe.transpose(dstp, p.V32[:, tt * 128:(tt + 1) * 128],
                                 p.ID32).then_inc(p.sem["pe"], 1)
                    p.bump("pe")
                s_inc = {}

                def emit_S(kc):
                    Sp = (p.S01, p.S23)[kc % 2]
                    pe.wait_ge(p.sem["act"],
                               A0 + 10 + (kc - 2) + 1 if kc >= 2 else A0 + 10)
                    ins = None
                    for half in range(2):
                        ins = pe.matmul(Sp[:, half * 512:(half + 1) * 512],
                                        KSB[:, kc * 128:(kc + 1) * 128],
                                        QSB[:, half * 512:(half + 1) * 512],
                                        start=True, stop=True)
                    ins.then_inc(p.sem["pe"], 1)
                    s_inc[kc] = p.bump("pe")

                emit_S(0)
                emit_S(1)
                for kc in range(16):
                    pe.wait_ge(p.sem["act"], A0 + 10 + kc + 1)
                    pe.wait_ge(p.sem["dve"], D0 + kc + 1)
                    ins = None
                    for half in range(2):
                        ins = pe.matmul(p.PVP[:, half * 512:(half + 1) * 512],
                                        VT[:, kc * 128:(kc + 1) * 128],
                                        PT[kc % 2][:, half * 512:(half + 1) * 512],
                                        start=(kc == 0), stop=(kc == 15))
                    ins.then_inc(p.sem["pe"], 1)
                    p.bump("pe")
                    if kc == 0:
                        pe.wait_ge(p.sem["dve"], D0 + 16)
                    ins = None
                    for half, zp in ((0, p.ZAp), (1, p.ZBp)):
                        ins = pe.matmul(zp[0:1, :], p.ONESB,
                                        PT[kc % 2][:, half * 512:(half + 1) * 512],
                                        start=(kc == 0), stop=(kc == 15))
                    ins.then_inc(p.sem["pe"], 1)
                    p.bump("pe")
                    if kc + 2 < 16:
                        emit_S(kc + 2)
                pe.wait_ge(p.sem["dve"], D0 + 18)
                pe.wait_ge(p.sem["act"], A0 + 26)
                zr = p.rowsp(0, 1024)
                pe.matmul(p.S01[:, 0:512], p.ONESROW, zr[0:1, 0:512],
                          start=True, stop=True)
                pe.matmul(p.S01[:, 512:1024], p.ONESROW, zr[0:1, 512:1024],
                          start=True, stop=True).then_inc(p.sem["pe"], 1)
                p.bump("pe")
                assert p.c["pe"] == P0 + 75, p.c["pe"]
                p.l0_s_inc = s_inc

            @blk.scalar
            def _(a):
                gi = 0
                for kind, n_t in (("k", 4), ("v", 4), ("q", 2)):
                    for tt in range(n_t):
                        ps = (p.S01, p.S23)[(gi % 4) // 2][:, (gi % 2) * 512:(gi % 2) * 512 + 512]
                        a.wait_ge(p.sem["pe"], P0 + gi + 1)
                        if kind == "k":
                            ins = a.activation(KSB[:, tt * 512:(tt + 1) * 512], ps,
                                               AF.Identity, bias=p.smv(16 + h, 17 + h))
                        elif kind == "v":
                            ins = a.activation(p.V32[:, tt * 512:(tt + 1) * 512], ps,
                                               AF.Identity, bias=p.smv(32 + h, 33 + h))
                        else:
                            ins = a.activation(QSB[:, tt * 512:(tt + 1) * 512], ps,
                                               AF.Identity, bias=p.smv(224 + h, 225 + h),
                                               scale=p.smv(256, 257))
                        ins.then_inc(p.sem["act"], 1)
                        p.bump("act")
                        gi += 1
                for kc in range(16):
                    Sp = (p.S01, p.S23)[kc % 2]
                    a.wait_ge(p.sem["pe"], p.l0_s_inc[kc])
                    a.activation(PT[kc % 2], Sp[:, :], AF.Exp,
                                 bias=p.smv(240 + kc, 241 + kc)).then_inc(p.sem["act"], 1)
                    p.bump("act")
                assert p.c["act"] == A0 + 26

            @blk.vector
            def _(v):
                for tt in range(16):
                    v.wait_ge(p.sem["pe"], P0 + 10 + tt + 1)
                    v.tensor_copy(VT[:, tt * 128:(tt + 1) * 128],
                                  (p.ZAp, p.ZBp)[tt % 2][:, 0:128]).then_inc(p.sem["dve"], 1)
                    p.bump("dve")
                zr = p.rowsp(0, 1024)
                v.wait_ge(p.sem["pe"], P0 + 74)
                v.reciprocal(zr[0:1, 0:512], p.ZAp[0:1, :]).then_inc(p.sem["dve"], 1)
                v.reciprocal(zr[0:1, 512:1024], p.ZBp[0:1, :]).then_inc(p.sem["dve"], 1)
                p.bump("dve", 2)
                v.wait_ge(p.sem["pe"], P0 + 75)
                v.tensor_copy(p.RZB, p.S01[:, :]).then_inc(p.sem["dve"], 1)
                p.bump("dve")
                v.tensor_tensor(p.ao(h, 0, TOK), p.PVP[:, :], p.RZB,
                                ALU.mult).then_inc(p.sem["dve"], 1)
                p.bump("dve")
                assert p.c["dve"] == D0 + 20

            @blk.gpsimd
            def _(g):
                p.flush_w(g)


def emit_outproj(p, wo_dram, bias_col0, n_queued):
    nc = p.nc
    P0, A0, D0 = p.c["pe"], p.c["act"], p.c["dve"]
    for jc in range(n_queued, 16):
        p.queue_slab(f"o{jc}", wo_dram[jc])
    with nc.Block() as blk:
        @blk.tensor
        def _(pe):
            gi = 0
            for jc in range(16):
                pe.wait_ge(p.sem["w"], p.slab_wait(f"o{jc}"))
                slab = p.slab(f"o{jc}")
                for th in range(2):
                    ps = (p.S01, p.S23)[(gi % 4) // 2][:, (gi % 2) * 512:(gi % 2) * 512 + 512]
                    if gi >= 4:
                        pe.wait_ge(p.sem["dve"], D0 + gi - 4 + 1)
                    ins = None
                    for hc in range(16):
                        ins = pe.matmul(ps, slab[:, hc * 128:(hc + 1) * 128],
                                        p.ao(hc, th * 512, (th + 1) * 512),
                                        start=(hc == 0), stop=(hc == 15))
                    ins.then_inc(p.sem["pe"], 1)
                    p.bump("pe")
                    gi += 1
                p.free_slab(f"o{jc}", p.c["pe"])

        @blk.vector
        def _(v):
            gi = 0
            for jc in range(16):
                for th in range(2):
                    ps = (p.S01, p.S23)[(gi % 4) // 2][:, (gi % 2) * 512:(gi % 2) * 512 + 512]
                    v.wait_ge(p.sem["pe"], P0 + gi + 1)
                    xfr = p.xf(jc, th * 512, (th + 1) * 512)
                    v.scalar_tensor_tensor(
                        xfr, ps, p.smv(bias_col0 + jc, bias_col0 + jc + 1),
                        xfr, ALU.add, ALU.add).then_inc(p.sem["dve"], 1)
                    p.bump("dve")
                    gi += 1

        @blk.gpsimd
        def _(g):
            p.flush_w(g)


def emit_ln(p, sq_area):
    nc = p.nc
    P0, A0, D0 = p.c["pe"], p.c["act"], p.c["dve"]
    m_row = p.rowsp(0, 1024)
    r_row = p.rowalt(0, 1024)
    e2a = p.rowalt(1024, 1536)
    e2b = p.rowalt(1536, 2048)
    with nc.Block() as blk:
        @blk.tensor
        def _(pe):
            for hc in range(16):
                pe.wait_ge(p.sem["dve"], D0 + hc + 1)
                pe.matmul(p.ZAp[0:1, :], p.ONESB, p.xb(hc, 0, 512),
                          start=(hc == 0), stop=(hc == 15))
                pe.matmul(p.ZBp[0:1, :], p.ONESB, p.xb(hc, 512, 1024),
                          start=(hc == 0), stop=(hc == 15))
                pe.wait_ge(p.sem["act"], A0 + hc + 1)
                sq = sq_area[:, (hc % 2) * 1024:(hc % 2) * 1024 + 1024]
                pe.matmul(p.S01[0:1, 0:512], p.ONESB, sq[:, 0:512],
                          start=(hc == 0), stop=(hc == 15))
                ins = pe.matmul(p.S01[0:1, 512:1024], p.ONESB, sq[:, 512:1024],
                                start=(hc == 0), stop=(hc == 15))
                ins.then_inc(p.sem["pe"], 1)
                p.bump("pe")
            pe.wait_ge(p.sem["dve"], D0 + 19)
            pe.matmul(p.S23[:, 0:512], p.ONESROW, r_row[0:1, 0:512],
                      start=True, stop=True)
            pe.matmul(p.S23[:, 512:1024], p.ONESROW, r_row[0:1, 512:1024],
                      start=True, stop=True).then_inc(p.sem["pe"], 1)
            p.bump("pe")
            pe.matmul(p.PVP[:, 0:512], p.ONESROW, m_row[0:1, 0:512],
                      start=True, stop=True)
            pe.matmul(p.PVP[:, 512:1024], p.ONESROW, m_row[0:1, 512:1024],
                      start=True, stop=True).then_inc(p.sem["pe"], 1)
            p.bump("pe")

        @blk.scalar
        def _(a):
            for hc in range(16):
                a.wait_ge(p.sem["dve"], D0 + hc + 1)
                if hc >= 2:
                    a.wait_ge(p.sem["pe"], P0 + hc - 1)
                a.activation(sq_area[:, (hc % 2) * 1024:(hc % 2) * 1024 + 1024],
                             p.xb(hc, 0, TOK), AF.Square).then_inc(p.sem["act"], 1)
                p.bump("act")
            a.wait_ge(p.sem["pe"], P0 + 16)
            a.activation(m_row[0:1, 0:512], p.ZAp[0:1, :], AF.Identity, scale=p.smv(264, 265)[0:1, :])
            a.activation(m_row[0:1, 512:1024], p.ZBp[0:1, :], AF.Identity,
                         scale=p.smv(264, 265)[0:1, :]).then_inc(p.sem["act"], 2)
            p.bump("act", 2)
            a.wait_ge(p.sem["act"], p.c["act"])
            a.activation(e2a, p.S01[0:1, 0:512], AF.Identity, scale=p.smv(264, 265)[0:1, :])
            a.activation(e2b, p.S01[0:1, 512:1024], AF.Identity,
                         scale=p.smv(264, 265)[0:1, :]).then_inc(p.sem["act"], 2)
            p.bump("act", 2)
            a.activation(p.ZAp[0:1, :], m_row[0:1, 0:512], AF.Square)
            a.activation(p.ZBp[0:1, :], m_row[0:1, 512:1024],
                         AF.Square).then_inc(p.sem["act"], 2)
            p.bump("act", 2)
            a.wait_ge(p.sem["dve"], D0 + 17)
            a.activation(p.ZAp[0:1, :], r_row[0:1, 0:512], AF.Sqrt, bias=p.smv(265, 266)[0:1, :])
            a.activation(p.ZBp[0:1, :], r_row[0:1, 512:1024], AF.Sqrt,
                         bias=p.smv(265, 266)[0:1, :]).then_inc(p.sem["act"], 2)
            p.bump("act", 2)

        @blk.vector
        def _(v):
            for hc in range(16):
                v.tensor_copy(p.xb(hc, 0, TOK),
                              p.xf(hc, 0, TOK)).then_inc(p.sem["dve"], 1)
                p.bump("dve")
            v.wait_ge(p.sem["act"], A0 + 22)
            v.tensor_tensor(r_row[0:1, 0:512], e2a, p.ZAp[0:1, :], ALU.subtract)
            v.tensor_tensor(r_row[0:1, 512:1024], e2b, p.ZBp[0:1, :],
                            ALU.subtract).then_inc(p.sem["dve"], 1)
            p.bump("dve")
            v.wait_ge(p.sem["act"], A0 + 24)
            v.reciprocal(r_row[0:1, 0:512], p.ZAp[0:1, :])
            v.reciprocal(r_row[0:1, 512:1024], p.ZBp[0:1, :]).then_inc(p.sem["dve"], 1)
            p.bump("dve")
            v.wait_ge(p.sem["dve"], p.c["dve"])
            v.tensor_tensor(m_row, m_row, r_row, ALU.mult).then_inc(p.sem["dve"], 1)
            p.bump("dve")
            v.wait_ge(p.sem["pe"], P0 + 18)
            for hc in range(16):
                v.tensor_tensor(p.xf(hc, 0, TOK), p.xf(hc, 0, TOK),
                                p.S23[:, :], ALU.mult)
                v.tensor_tensor(p.xf(hc, 0, TOK), p.xf(hc, 0, TOK),
                                p.PVP[:, :], ALU.subtract)
                v.tensor_copy(p.xb(hc, 0, TOK),
                              p.xf(hc, 0, TOK)).then_inc(p.sem["dve"], 1)
                p.bump("dve")


def emit_ffn_l0(p, t):
    nc = p.nc
    p.set_ring([p.AB[:, i * 2048:(i + 1) * 2048] for i in range(8)]
               + [p.AO[:, 8192 + i * 2048:8192 + (i + 1) * 2048] for i in range(4)])
    s_row = p.rowsp(0, 1024)
    sh_row = p.rowalt(1024, 2048)

    seq = []
    for r in range(8):
        for i in range(8):
            seq.append(("g", 8 * r + i))
            seq.append(("u", 8 * r + i))
        for i in range(8):
            seq.append(("d", 8 * r + i))

    def src_of(kind, idx):
        return {"g": t["gw"], "u": t["uw"], "d": t["dw"]}[kind][idx]

    qptr = [0]

    def queue_upto(n):
        while qptr[0] < min(n, len(seq)):
            kind, idx = seq[qptr[0]]
            p.queue_slab(f"{kind}{idx}", src_of(kind, idx))
            qptr[0] += 1

    P0 = p.c["pe"]
    pre_w = p.c["w"]
    queue_upto(12)
    with nc.Block() as blk:
        @blk.tensor
        def _(pe):
            ins = None
            for hc in range(16):
                pe.matmul(p.ZAp[0:1, :], p.ONESB, p.xb(hc, 0, 512),
                          start=(hc == 0), stop=(hc == 15))
                ins = pe.matmul(p.ZBp[0:1, :], p.ONESB, p.xb(hc, 512, 1024),
                                start=(hc == 0), stop=(hc == 15))
            ins.then_inc(p.sem["pe"], 1)
            p.bump("pe")

        @blk.scalar
        def _(a):
            a.wait_ge(p.sem["pe"], P0 + 1)
            a.activation(s_row[0:1, 0:512], p.ZAp[0:1, :], AF.Identity)
            a.activation(s_row[0:1, 512:1024], p.ZBp[0:1, :],
                         AF.Identity).then_inc(p.sem["act"], 1)
            p.bump("act")

        @blk.gpsimd
        def _(g):
            g.wait_ge(p.sem["w"], pre_w * 16)
            p.flush_w(g)

    for r in range(8):
        P0, A0, D0 = p.c["pe"], p.c["act"], p.c["dve"]
        queue_upto(12 + 24 * (r + 1))
        with nc.Block() as blk:
            @blk.tensor
            def _(pe):
                for i in range(8):
                    ic = 8 * r + i
                    pe.wait_ge(p.sem["w"], p.slab_wait(f"u{ic}"))
                    gs = p.slab(f"g{ic}")
                    us = p.slab(f"u{ic}")
                    for th in range(2):
                        if i >= 1:
                            pe.wait_ge(p.sem["act"], A0 + 2 * (i - 1) + th + 1)
                        ps = p.S01[:, th * 512:(th + 1) * 512]
                        for hc in range(16):
                            pe.matmul(ps, gs[:, hc * 128:(hc + 1) * 128],
                                      p.xb(hc, th * 512, (th + 1) * 512),
                                      start=(hc == 0), stop=False)
                        pe.matmul(ps, p.NEGROW, s_row[0:1, th * 512:(th + 1) * 512],
                                  start=False, stop=True).then_inc(p.sem["pe"], 1)
                        p.bump("pe")
                    for th in range(2):
                        if i >= 1:
                            pe.wait_ge(p.sem["dve"], D0 + 2 * (i - 1) + th + 1)
                        ps = p.S23[:, th * 512:(th + 1) * 512]
                        for hc in range(16):
                            pe.matmul(ps, us[:, hc * 128:(hc + 1) * 128],
                                      p.xb(hc, th * 512, (th + 1) * 512),
                                      start=(hc == 0), stop=False)
                        pe.matmul(ps, p.NEGROW, s_row[0:1, th * 512:(th + 1) * 512],
                                  start=False, stop=True).then_inc(p.sem["pe"], 1)
                        p.bump("pe")
                    p.free_slab(f"g{ic}", p.c["pe"])
                    p.free_slab(f"u{ic}", p.c["pe"])
                ins = None
                for i in range(8):
                    pe.wait_ge(p.sem["dve"], D0 + 2 * i + 2)
                    hsb = p.AO[:, i * 1024:(i + 1) * 1024]
                    pe.matmul(p.ZAp[0:1, :], p.ONESB, hsb[:, 0:512],
                              start=(i == 0), stop=(i == 7))
                    ins = pe.matmul(p.ZBp[0:1, :], p.ONESB, hsb[:, 512:1024],
                                    start=(i == 0), stop=(i == 7))
                ins.then_inc(p.sem["pe"], 1)
                p.bump("pe")
                gi = 0
                for jc in range(16):
                    if jc == 0:
                        pe.wait_ge(p.sem["act"], A0 + 18)
                        pe.wait_ge(p.sem["w"], p.slab_wait(f"d{8 * r + 7}"))
                    for th in range(2):
                        rot = gi % 4
                        ps = (p.PVP, p.S01)[rot // 2][:, (rot % 2) * 512:(rot % 2) * 512 + 512]
                        if gi >= 4:
                            pe.wait_ge(p.sem["dve"], D0 + 16 + (gi - 4) + 1)
                        for i in range(8):
                            ds = p.slab(f"d{8 * r + i}")
                            pe.matmul(ps, ds[:, jc * 128:(jc + 1) * 128],
                                      p.AO[:, i * 1024 + th * 512:i * 1024 + (th + 1) * 512],
                                      start=(i == 0), stop=False)
                        pe.matmul(ps, p.NEGROW, sh_row[0:1, th * 512:(th + 1) * 512],
                                  start=False, stop=True).then_inc(p.sem["pe"], 1)
                        p.bump("pe")
                        gi += 1
                for i in range(8):
                    p.free_slab(f"d{8 * r + i}", p.c["pe"])

            @blk.scalar
            def _(a):
                for i in range(8):
                    ic = 8 * r + i
                    for th in range(2):
                        a.wait_ge(p.sem["pe"], P0 + 4 * i + th + 1)
                        if i >= 1:
                            a.wait_ge(p.sem["dve"], D0 + 2 * (i - 1) + th + 1)
                        a.activation(p.SG[:, th * 512:(th + 1) * 512],
                                     p.S01[:, th * 512:(th + 1) * 512], AF.Silu,
                                     scale=p.smv(80 + ic, 81 + ic)).then_inc(p.sem["act"], 1)
                        p.bump("act")
                a.wait_ge(p.sem["pe"], P0 + 33)
                a.activation(sh_row[0:1, 0:512], p.ZAp[0:1, :], AF.Identity)
                a.activation(sh_row[0:1, 512:1024], p.ZBp[0:1, :],
                             AF.Identity).then_inc(p.sem["act"], 2)
                p.bump("act", 2)

            @blk.vector
            def _(v):
                for i in range(8):
                    ic = 8 * r + i
                    for th in range(2):
                        v.wait_ge(p.sem["pe"], P0 + 4 * i + 2 + th + 1)
                        v.wait_ge(p.sem["act"], A0 + 2 * i + th + 1)
                        v.scalar_tensor_tensor(
                            p.AO[:, i * 1024 + th * 512:i * 1024 + (th + 1) * 512],
                            p.S23[:, th * 512:(th + 1) * 512],
                            p.smv(144 + ic, 145 + ic),
                            p.SG[:, th * 512:(th + 1) * 512],
                            ALU.mult, ALU.mult).then_inc(p.sem["dve"], 1)
                        p.bump("dve")
                gi = 0
                for jc in range(16):
                    for th in range(2):
                        rot = gi % 4
                        ps = (p.PVP, p.S01)[rot // 2][:, (rot % 2) * 512:(rot % 2) * 512 + 512]
                        v.wait_ge(p.sem["pe"], P0 + 33 + gi + 1)
                        xfr = p.xf(jc, th * 512, (th + 1) * 512)
                        v.scalar_tensor_tensor(
                            xfr, ps, p.smv(208 + jc, 209 + jc),
                            xfr, ALU.mult, ALU.add).then_inc(p.sem["dve"], 1)
                        p.bump("dve")
                        gi += 1

            @blk.gpsimd
            def _(g):
                p.flush_w(g)


def emit_kv_exchange(p, t):
    nc = p.nc
    p.set_ring([p.AB[:, i * 2048:(i + 1) * 2048] for i in range(4)])
    KVO = p.AB[:, 8192:16384]
    for c in range(8):
        p.queue_slab(f"KV{c}", t["wqkv"][16 + c])
    P0, A0 = p.c["pe"], p.c["act"]
    with nc.Block() as blk:
        @blk.tensor
        def _(pe):
            gi = 0
            for c in range(8):
                pe.wait_ge(p.sem["w"], p.slab_wait(f"KV{c}"))
                slab = p.slab(f"KV{c}")
                for tc in range(2):
                    ps = (p.S01, p.S23)[(gi % 4) // 2][:, (gi % 2) * 512:(gi % 2) * 512 + 512]
                    if gi >= 4:
                        pe.wait_ge(p.sem["act"], A0 + gi - 4 + 1)
                    ins = None
                    for hc in range(16):
                        ins = pe.matmul(ps, slab[:, hc * 128:(hc + 1) * 128],
                                        p.xb(hc, tc * 512, (tc + 1) * 512),
                                        start=(hc == 0), stop=(hc == 15))
                    ins.then_inc(p.sem["pe"], 1)
                    p.bump("pe")
                    gi += 1
                p.free_slab(f"KV{c}", p.c["pe"])

        @blk.scalar
        def _(a):
            gi = 0
            for c in range(8):
                for tc in range(2):
                    ps = (p.S01, p.S23)[(gi % 4) // 2][:, (gi % 2) * 512:(gi % 2) * 512 + 512]
                    a.wait_ge(p.sem["pe"], P0 + gi + 1)
                    a.activation(KVO[:, c * 1024 + tc * 512:c * 1024 + (tc + 1) * 512],
                                 ps, AF.Identity,
                                 bias=p.smv(286 + c, 287 + c)).then_inc(p.sem["act"], 1)
                    p.bump("act")
                    gi += 1

        @blk.gpsimd
        def _(g):
            p.flush_w(g)

    with nc.Block() as blk:
        @blk.gpsimd
        def _(g):
            dst = t["cc_in"].rearrange("(c q) s -> q c s", q=128)
            src = KVO.rearrange("q (c s) -> q c s", c=8)
            g.dma_start(dst, src).then_inc(p.sem["w"], 16)
            p.bump("w")
            g.wait_ge(p.sem["w"], p.c["w"] * 16)
            g.collective_compute(
                "AllGather", ALU.bypass,
                replica_groups=[[0, 1], [2, 3], [4, 5], [6, 7]],
                ins=[t["cc_in"][:, :]], outs=[t["cc_out"][:, :]]).then_inc(p.sem["cc"], 1)


def emit_attn_l1(p, t):
    nc = p.nc
    p.set_ring([p.AB[:, i * 2048:(i + 1) * 2048] for i in range(2)])
    KSB = p.AB[:, 4096:6144]
    QSB = p.AB[:, 6144:10240]
    PT = [p.AB[:, 10240:11264], p.AB[:, 11264:12288]]
    VB = p.AB[:, 10240:12288]

    def vta(g_loc, kc):
        off = 12288 + g_loc * 1040 + kc * 65
        return p.AB[:, off:off + 65]

    for pr in range(4):
        P0, A0, D0 = p.c["pe"], p.c["act"], p.c["dve"]
        for j in range(4):
            p.queue_slab(f"Q{pr}_{j}", t["wqkv"][4 * pr + j])
        xkv = p.c["x"] + 4
        with nc.Block() as blk:
            @blk.sync
            def _(sy):
                if pr == 0:
                    sy.wait_ge(p.sem["cc"], 1)
                cco = t["cc_out"]
                for r in range(2):
                    sy.dma_start(KSB[:, r * 1024:(r + 1) * 1024],
                                 cco[r * 1024 + pr * 128:r * 1024 + (pr + 1) * 128, :]
                                 ).then_inc(p.sem["x"], 16)
                    sy.dma_start(VB[:, r * 1024:(r + 1) * 1024],
                                 cco[r * 1024 + 512 + pr * 128:r * 1024 + 512 + (pr + 1) * 128, :]
                                 ).then_inc(p.sem["x"], 16)
                    p.bump("x", 2)

            @blk.vector
            def _(v):
                v.wait_ge(p.sem["x"], xkv * 16)
                v.tensor_copy(p.V32, VB).then_inc(p.sem["dve"], 1)
                p.bump("dve")
                if pr == 0:
                    for g_loc in range(2):
                        for kc in range(16):
                            v.memset(vta(g_loc, kc)[:, 64:65], 1.0)
                idx = 0
                for g_loc in range(2):
                    for tt in range(16):
                        v.wait_ge(p.sem["pe"], P0 + idx + 1)
                        v.tensor_copy(vta(g_loc, tt)[:, 0:64],
                                      (p.ZAp, p.ZBp)[idx % 2][:, 0:64]).then_inc(p.sem["dve"], 1)
                        p.bump("dve")
                        idx += 1
                for hd in range(8):
                    g_loc, rr = hd // 4, hd % 4
                    HB = P0 + 40 + hd * 49
                    b64 = g_loc * 64
                    zrow = 0
                    zr = p.rowsp(0, 1024)
                    v.wait_ge(p.sem["pe"], HB + 48)
                    v.reciprocal(zr[0:1, 0:512], p.ZAp[zrow:zrow + 1, :])
                    v.reciprocal(zr[0:1, 512:1024],
                                 p.ZBp[zrow:zrow + 1, :]).then_inc(p.sem["dve"], 1)
                    p.bump("dve")
                    v.wait_ge(p.sem["pe"], HB + 49)
                    v.tensor_copy(p.RZB[:, 0:512], p.ZAp[:, :])
                    v.tensor_copy(p.RZB[:, 512:1024], p.ZBp[:, :]).then_inc(p.sem["dve"], 1)
                    p.bump("dve")
                    ao_c = 4 * pr + rr
                    v.tensor_tensor(
                        p.AO[b64:b64 + 64, ao_c * TOK:(ao_c + 1) * TOK],
                        p.PVP[b64:b64 + 64, :], p.RZB[b64:b64 + 64, :],
                        ALU.mult).then_inc(p.sem["dve"], 1)
                    p.bump("dve")

            @blk.tensor
            def _(pe):
                if pr == 0:
                    pe.wait_ge(p.sem["cc"], 1)
                idx = 0
                for g_loc in range(2):
                    idv = p.ID32[g_loc * 64:(g_loc + 1) * 64,
                                 g_loc * 64:(g_loc + 1) * 64]
                    for tt in range(16):
                        pe.wait_ge(p.sem["dve"], D0 + 1)
                        if idx >= 2:
                            pe.wait_ge(p.sem["dve"], D0 + 1 + idx - 1)
                        dstp = (p.ZAp, p.ZBp)[idx % 2][:, 0:64]
                        pe.transpose(dstp,
                                     p.V32[g_loc * 64:(g_loc + 1) * 64,
                                           tt * 128:(tt + 1) * 128],
                                     idv).then_inc(p.sem["pe"], 1)
                        p.bump("pe")
                        idx += 1
                gi = 0
                for j in range(4):
                    slab = p.slab(f"Q{pr}_{j}")
                    pe.wait_ge(p.sem["w"], p.slab_wait(f"Q{pr}_{j}"))
                    for th in range(2):
                        ps = (p.S01, p.S23)[(gi % 4) // 2][:, (gi % 2) * 512:(gi % 2) * 512 + 512]
                        if gi >= 4:
                            pe.wait_ge(p.sem["act"], A0 + gi - 4 + 1)
                        ins = None
                        for hc in range(16):
                            ins = pe.matmul(ps, slab[:, hc * 128:(hc + 1) * 128],
                                            p.xb(hc, th * 512, (th + 1) * 512),
                                            start=(hc == 0), stop=(hc == 15))
                        ins.then_inc(p.sem["pe"], 1)
                        p.bump("pe")
                        gi += 1
                    p.free_slab(f"Q{pr}_{j}", p.c["pe"])
                p.l1_s_inc = {}
                for hd in range(8):
                    g_loc, rr = hd // 4, hd % 4
                    b64 = g_loc * 64
                    zrow = 0
                    tpos_pv = (0, 64) if g_loc == 1 else None
                    tpos_z = None

                    def emit_S(kc, hd=hd, rr=rr, b64=b64):
                        Sp = (p.S01, p.S23)[kc % 2]
                        e = hd * 16 + kc
                        pe.wait_ge(p.sem["act"],
                                   A0 + 8 + e - 2 + 1 if e >= 2 else A0 + 8)
                        ins = None
                        for half in range(2):
                            ins = pe.matmul(
                                Sp[:, half * 512:(half + 1) * 512],
                                KSB[b64:b64 + 64, kc * 128:(kc + 1) * 128],
                                QSB[b64:b64 + 64,
                                    rr * 1024 + half * 512:rr * 1024 + (half + 1) * 512],
                                start=True, stop=True)
                        ins.then_inc(p.sem["pe"], 1)
                        p.l1_s_inc[(hd, kc)] = p.bump("pe")

                    emit_S(0)
                    emit_S(1)
                    for kc in range(16):
                        e = hd * 16 + kc
                        pe.wait_ge(p.sem["act"], A0 + 8 + e + 1)
                        pe.wait_ge(p.sem["dve"], D0 + 1 + g_loc * 16 + kc + 1)
                        if kc == 0 and hd >= 1:
                            pe.wait_ge(p.sem["dve"], D0 + 33 + (hd - 1) * 3 + 3)
                        if kc == 0 and hd == 0:
                            pe.wait_ge(p.sem["dve"], D0 + 33)
                        ins = None
                        for half in range(2):
                            ins = pe.matmul(p.PVP[b64:b64 + 64, half * 512:(half + 1) * 512],
                                            vta(g_loc, kc)[:, 0:64],
                                            PT[kc % 2][:, half * 512:(half + 1) * 512],
                                            start=(kc == 0), stop=(kc == 15),
                                            tile_position=tpos_pv)
                        ins.then_inc(p.sem["pe"], 1)
                        p.bump("pe")
                        ins = None
                        for half, zp in ((0, p.ZAp), (1, p.ZBp)):
                            ins = pe.matmul(zp[zrow:zrow + 1, :],
                                            vta(g_loc, kc)[:, 64:65],
                                            PT[kc % 2][:, half * 512:(half + 1) * 512],
                                            start=(kc == 0), stop=(kc == 15),
                                            tile_position=tpos_z)
                        ins.then_inc(p.sem["pe"], 1)
                        p.bump("pe")
                        if kc + 2 < 16:
                            emit_S(kc + 2)
                    pe.wait_ge(p.sem["dve"], D0 + 33 + hd * 3 + 1)
                    zr = p.rowsp(0, 1024)
                    pe.matmul(p.ZAp[:, :], p.ONESROW, zr[0:1, 0:512],
                              start=True, stop=True)
                    pe.matmul(p.ZBp[:, :], p.ONESROW, zr[0:1, 512:1024],
                              start=True, stop=True).then_inc(p.sem["pe"], 1)
                    p.bump("pe")
                assert p.c["pe"] == P0 + 40 + 8 * 49

            @blk.scalar
            def _(a):
                gi = 0
                for j in range(4):
                    for th in range(2):
                        ps = (p.S01, p.S23)[(gi % 4) // 2][:, (gi % 2) * 512:(gi % 2) * 512 + 512]
                        a.wait_ge(p.sem["pe"], P0 + 32 + gi + 1)
                        a.activation(
                            QSB[:, j * 1024 + th * 512:j * 1024 + (th + 1) * 512],
                            ps, AF.Identity,
                            bias=p.smv(270 + 4 * pr + j, 271 + 4 * pr + j),
                            scale=p.smv(267, 268)).then_inc(p.sem["act"], 1)
                        p.bump("act")
                        gi += 1
                a.wait_ge(p.sem["dve"], D0 + 1)
                for hd in range(8):
                    for kc in range(16):
                        Sp = (p.S01, p.S23)[kc % 2]
                        a.wait_ge(p.sem["pe"], p.l1_s_inc[(hd, kc)])
                        a.activation(PT[kc % 2], Sp[:, :], AF.Exp,
                                     bias=p.smv(266, 267)).then_inc(p.sem["act"], 1)
                        p.bump("act")
                assert p.c["act"] == A0 + 8 + 128

            @blk.gpsimd
            def _(g):
                p.flush_w(g)


MOE_C = 512


def emit_moe(p, t):
    nc = p.nc
    IOA = bass.IndirectOffsetOnAxis
    RGWB = p.AFR[:, 2304:2560]
    LTRI = p.AFR[:, 2176:2304]
    BASE = p.AFR[0:1, 2560:2576]
    ECROW = p.AFR[0:1, 2576:2592]
    ECB = p.SG[:, 656:672]
    MSCR = p.SG[:, 672:688]
    onesf = p.smv(310, 311)

    def eq1m(tc):
        return p.SG[:, 384 + tc * 16:384 + (tc + 1) * 16]

    def eq2m(tc):
        return p.SG[:, 512 + tc * 16:512 + (tc + 1) * 16]

    def wcol(tc, i):
        return p.smv(246 + 2 * tc + i, 247 + 2 * tc + i)

    def xo_piece(i):
        return p.XB[:, i * SEQ:i * SEQ + 1024]

    def xe(hc):
        return p.AB[:, 8192 + hc * 512:8192 + (hc + 1) * 512]

    def hte(ic):
        pc = 12 + ic // 2
        off = (ic % 2) * 512
        return p.XB[:, pc * SEQ + off:pc * SEQ + off + 512]

    def oute(sc, half):
        pc = (sc % 2) * 2 + half
        return p.XB[:, pc * SEQ:pc * SEQ + 1024]

    def d_split_piece(k, hi):
        if k < 4:
            piece = 4 + 2 * k + hi
            return p.XB[:, piece * SEQ:piece * SEQ + 1024]
        piece = 8 + 2 * (k - 4) + hi
        return p.XB[:, piece * SEQ + 1024:(piece + 1) * SEQ]

    XFB = p.XF[:, :].bitcast(BF16)
    p.set_ring([p.AB[:, i * 2048:(i + 1) * 2048] for i in range(4)]
               + [XFB[:, i * 2048:(i + 1) * 2048] for i in range(12)])

    P0, A0, D0 = p.c["pe"], p.c["act"], p.c["dve"]
    pre_w = p.c["w"]
    rgw_w = pre_w + 3
    with nc.Block() as blk:
        @blk.gpsimd
        def _(g):
            g.wait_ge(p.sem["w"], pre_w * 16)
            g.dma_start(LTRI, t["ltri"]).then_inc(p.sem["w"], 16)
            g.dma_start(ECROW, t["ecrow"]).then_inc(p.sem["w"], 16)
            g.dma_start(RGWB, t["rgw"]).then_inc(p.sem["w"], 16)
            p.bump("w", 3)

        @blk.tensor
        def _(pe):
            pe.wait_ge(p.sem["w"], rgw_w * 16)
            for tc in range(8):
                zp = (p.ZAp, p.ZBp)[tc % 2]
                if tc >= 2:
                    pe.wait_ge(p.sem["act"], A0 + (tc - 2) + 1)
                ins = None
                for hc in range(16):
                    ins = pe.matmul(zp[:, 0:16],
                                    p.xf(hc, tc * 128, (tc + 1) * 128),
                                    RGWB[:, hc * 16:(hc + 1) * 16],
                                    start=(hc == 0), stop=(hc == 15))
                ins.then_inc(p.sem["pe"], 1)
                p.bump("pe")
            pe.wait_ge(p.sem["act"], A0 + 8)
            pe.matmul(p.ZAp[:, 0:16], p.ONESROW, ECROW,
                      start=True, stop=True).then_inc(p.sem["pe"], 1)
            p.bump("pe")

        @blk.scalar
        def _(a):
            for tc in range(8):
                zp = (p.ZAp, p.ZBp)[tc % 2]
                a.wait_ge(p.sem["pe"], P0 + tc + 1)
                a.activation(p.SG[:, tc * 16:(tc + 1) * 16], zp[:, 0:16],
                             AF.Exp).then_inc(p.sem["act"], 1)
                p.bump("act")

        @blk.vector
        def _(v):
            def step(ins):
                ins.then_inc(p.sem["dve"], 1)
                p.bump("dve")
                v.wait_ge(p.sem["dve"], p.c["dve"])

            v.memset(BASE, 0.0)
            c0, c1, c2, c3 = (p.smv(240, 241), p.smv(241, 242),
                              p.smv(242, 243), p.smv(243, 244))
            for tc in range(8):
                pp = p.SG[:, tc * 16:(tc + 1) * 16]
                p2 = p.SG[:, 128 + tc * 16:128 + (tc + 1) * 16]
                v.wait_ge(p.sem["act"], A0 + tc + 1)
                step(v.tensor_reduce(c0, pp, AX.X, ALU.max))
                step(v.tensor_scalar(eq1m(tc), pp, c0, None, ALU.is_equal))
                step(v.scalar_tensor_tensor(p2, eq1m(tc), -1e30, pp,
                                            ALU.mult, ALU.add))
                step(v.tensor_reduce(c1, p2, AX.X, ALU.max))
                step(v.tensor_tensor(c2, c0, c1, ALU.add))
                step(v.reciprocal(c3, c2))
                step(v.tensor_tensor(wcol(tc, 0), c0, c3, ALU.mult))
                step(v.tensor_tensor(wcol(tc, 1), c1, c3, ALU.mult))
                step(v.tensor_scalar(eq2m(tc), p2, c1, None, ALU.is_equal))
            v.wait_ge(p.sem["pe"], P0 + 9)
            v.tensor_copy(ECB, p.ZAp[:, 0:16]).then_inc(p.sem["dve"], 1)
            p.bump("dve")

    P0, A0, D0 = p.c["pe"], p.c["act"], p.c["dve"]
    with nc.Block() as blk:
        @blk.tensor
        def _(pe):
            for tc in range(8):
                zp = (p.ZAp, p.ZBp)[tc % 2]
                zrow = (p.ZBp, p.ZAp)[tc % 2]
                pe.wait_ge(p.sem["dve"], D0 + 10 * tc + 2)
                pe.matmul(zp[:, 0:16], LTRI, MSCR, start=True, stop=False)
                pe.matmul(zp[:, 0:16], p.ONESROW, BASE, start=False, stop=True)
                pe.matmul(zrow[0:1, 0:16], onesf, MSCR,
                          start=True, stop=True).then_inc(p.sem["pe"], 2)
                p.bump("pe", 2)

        @blk.vector
        def _(v):
            def step(ins):
                ins.then_inc(p.sem["dve"], 1)
                p.bump("dve")
                v.wait_ge(p.sem["dve"], p.c["dve"])

            c0 = p.smv(240, 241)
            sv = p.SG[:, 688:704]
            tmp = p.SG[:, 704:720]
            ecap = p.SG[:, 720:736]
            step(v.tensor_scalar(ecap, ECB, float(MOE_C), None, ALU.add))
            for tc in range(8):
                step(v.tensor_tensor(MSCR, eq1m(tc), eq2m(tc), ALU.add))
                v.wait_ge(p.sem["pe"], P0 + 2 * tc + 2)
                zp = (p.ZAp, p.ZBp)[tc % 2]
                zrow = (p.ZBp, p.ZAp)[tc % 2]
                step(v.tensor_tensor(sv, zp[:, 0:16], ECB, ALU.add))
                step(v.tensor_tensor(sv, sv, ecap, ALU.min))
                step(v.tensor_tensor(tmp, sv, eq1m(tc), ALU.mult))
                step(v.tensor_reduce(c0, tmp, AX.X, ALU.add))
                step(v.tensor_copy(p.SLOTS[:, 2 * tc:2 * tc + 1], c0))
                step(v.tensor_tensor(tmp, sv, eq2m(tc), ALU.mult))
                step(v.tensor_reduce(c0, tmp, AX.X, ALU.add))
                step(v.tensor_copy(p.SLOTS[:, 2 * tc + 1:2 * tc + 2], c0))
                step(v.tensor_tensor(BASE, BASE, zrow[0:1, 0:16], ALU.add))

    P0, A0, D0 = p.c["pe"], p.c["act"], p.c["dve"]
    with nc.Block() as blk:
        @blk.tensor
        def _(pe):
            for idx in range(128):
                tc, hc = idx // 16, idx % 16
                if idx >= 2:
                    pe.wait_ge(p.sem["dve"], D0 + idx - 1)
                dstp = (p.ZAp, p.ZBp)[idx % 2][:, 0:128]
                pe.transpose(dstp, p.xf(hc, tc * 128, (tc + 1) * 128),
                             p.ID32).then_inc(p.sem["pe"], 1)
                p.bump("pe")

        @blk.vector
        def _(v):
            for idx in range(128):
                tc, hc = idx // 16, idx % 16
                v.wait_ge(p.sem["pe"], P0 + idx + 1)
                v.tensor_copy(
                    p.AO[:, tc * 2048 + hc * 128:tc * 2048 + (hc + 1) * 128],
                    (p.ZAp, p.ZBp)[idx % 2][:, 0:128]).then_inc(p.sem["dve"], 1)
                p.bump("dve")

        @blk.gpsimd
        def _(g):
            for tc in range(8):
                g.wait_ge(p.sem["dve"], D0 + (tc + 1) * 16)
                for i in range(2):
                    g.indirect_dma_start(
                        t["ein"][:, :],
                        IOA(ap=p.SLOTS[:, 2 * tc + i:2 * tc + i + 1], axis=0),
                        p.AO[:, tc * 2048:(tc + 1) * 2048],
                        None).then_inc(p.sem["ind"], 16)
                    p.bump("ind")
    scatter_done_ind = p.c["ind"]

    NSC = MOE_C // 128

    def xe_dma_dest(sc):
        return (p.AB[:, 8192:16384]
                .rearrange("q (c s) -> q c s", c=16)
                [:, :, sc * 128:sc * 128 + 128])

    def queue_expert(e):
        for i in range(8):
            p.queue_slab(f"g{e}_{i}", t["wg"][e, i])
            p.queue_slab(f"u{e}_{i}", t["wu"][e, i])
        for k in range(8):
            gk = f"d{e-1}_{k}" if e > 0 else None
            dsrc = t["wd"][e, k]
            p.queue_slab_at(f"d{e}_{k}", None, None, guard_key=gk, parts=[
                (dsrc[:, 0:1024], d_split_piece(k, 0)),
                (dsrc[:, 1024:2048], d_split_piece(k, 1))])

    p.xe_wait = {}
    for e in range(16):
        P0, A0, D0 = p.c["pe"], p.c["act"], p.c["dve"]
        X0 = p.c["x"]
        if e == 0:
            queue_expert(0)
        if e < 15:
            queue_expert(e + 1)

        with nc.Block() as blk:
            @blk.sync
            def _(sy):
                if e == 0:
                    sy.wait_ge(p.sem["ind"], scatter_done_ind * 16)
                    for sc in range(NSC):
                        sy.dma_start(xe_dma_dest(sc),
                                     t["ein"][sc * 128:(sc + 1) * 128, :],
                                     transpose=True).then_inc(p.sem["x"], 16)
                        p.bump("x")
                    p.xe_wait[0] = p.c["x"]
                if e < 15:
                    sy.wait_ge(p.sem["pe"], P0 + 16)
                    for sc in range(NSC):
                        sy.dma_start(xe_dma_dest(sc),
                                     t["ein"][(e + 1) * MOE_C + sc * 128:(e + 1) * MOE_C + (sc + 1) * 128, :],
                                     transpose=True).then_inc(p.sem["x"], 16)
                        p.bump("x")
                    p.xe_wait[e + 1] = p.c["x"]
                p.xout0 = p.c["x"]
                for sc in range(NSC):
                    sy.wait_ge(p.sem["dve"], D0 + 8 + 4 * sc + 4)
                    for half in range(2):
                        sy.dma_start(
                            t["eout"][e * MOE_C + sc * 128:e * MOE_C + (sc + 1) * 128,
                                      half * 1024:(half + 1) * 1024],
                            oute(sc, half)).then_inc(p.sem["x"], 16)
                        p.bump("x")

            @blk.tensor
            def _(pe):
                pe.wait_ge(p.sem["x"], p.xe_wait[e] * 16)
                for i in range(8):
                    pe.wait_ge(p.sem["w"], p.slab_wait(f"u{e}_{i}"))
                    gs = p.slab(f"g{e}_{i}")
                    us = p.slab(f"u{e}_{i}")
                    ps = p.S01[:, (i % 2) * 512:(i % 2) * 512 + 512]
                    if i >= 2:
                        pe.wait_ge(p.sem["act"], A0 + (i - 2) + 1)
                    ins = None
                    for hc in range(16):
                        ins = pe.matmul(ps, gs[:, hc * 128:(hc + 1) * 128],
                                        xe(hc), start=(hc == 0), stop=(hc == 15))
                    ins.then_inc(p.sem["pe"], 1)
                    p.bump("pe")
                    ps = p.S23[:, (i % 2) * 512:(i % 2) * 512 + 512]
                    if i >= 2:
                        pe.wait_ge(p.sem["dve"], D0 + (i - 2) + 1)
                    ins = None
                    for hc in range(16):
                        ins = pe.matmul(ps, us[:, hc * 128:(hc + 1) * 128],
                                        xe(hc), start=(hc == 0), stop=(hc == 15))
                    ins.then_inc(p.sem["pe"], 1)
                    p.bump("pe")
                    p.free_slab(f"g{e}_{i}", p.c["pe"])
                    p.free_slab(f"u{e}_{i}", p.c["pe"])
                pe.wait_ge(p.sem["w"], p.slab_wait(f"d{e}_7"))
                pe.wait_ge(p.sem["dve"], D0 + 8)
                gi = 0
                for sc in range(NSC):
                    for jq in range(4):
                        rot = gi % 4
                        ps = (p.PVP, p.S01)[rot // 2][:, (rot % 2) * 512:(rot % 2) * 512 + 512]
                        if gi == 2:
                            pe.wait_ge(p.sem["act"], A0 + 8)
                        if gi >= 4:
                            pe.wait_ge(p.sem["dve"], D0 + 8 + (gi - 4) + 1)
                        ins = None
                        for i in range(8):
                            rhs = d_split_piece(i, jq // 2)[:, (jq % 2) * 512:(jq % 2) * 512 + 512]
                            ins = pe.matmul(ps, hte(i)[:, sc * 128:(sc + 1) * 128],
                                            rhs, start=(i == 0), stop=(i == 7))
                        ins.then_inc(p.sem["pe"], 1)
                        p.bump("pe")
                        gi += 1
                for i in range(8):
                    p.free_slab(f"d{e}_{i}", p.c["pe"])

            @blk.vector
            def _(v):
                for i in range(8):
                    v.wait_ge(p.sem["pe"], P0 + 2 * i + 2)
                    v.wait_ge(p.sem["act"], A0 + i + 1)
                    v.tensor_tensor(
                        hte(i), p.SG[:, (i % 2) * 512:(i % 2) * 512 + 512],
                        p.S23[:, (i % 2) * 512:(i % 2) * 512 + 512],
                        ALU.mult).then_inc(p.sem["dve"], 1)
                    p.bump("dve")
                gi = 0
                for sc in range(NSC):
                    for jq in range(4):
                        rot = gi % 4
                        ps = (p.PVP, p.S01)[rot // 2][:, (rot % 2) * 512:(rot % 2) * 512 + 512]
                        v.wait_ge(p.sem["pe"], P0 + 16 + gi + 1)
                        if sc >= 2:
                            v.wait_ge(p.sem["x"], (p.xout0 + 2 * (sc - 2) + 2) * 16)
                        v.tensor_copy(oute(sc, jq // 2)[:, (jq % 2) * 512:(jq % 2) * 512 + 512],
                                      ps).then_inc(p.sem["dve"], 1)
                        p.bump("dve")
                        gi += 1

            @blk.scalar
            def _(a):
                for i in range(8):
                    a.wait_ge(p.sem["pe"], P0 + 2 * i + 1)
                    if i >= 2:
                        a.wait_ge(p.sem["dve"], D0 + (i - 2) + 1)
                    a.activation(p.SG[:, (i % 2) * 512:(i % 2) * 512 + 512],
                                 p.S01[:, (i % 2) * 512:(i % 2) * 512 + 512],
                                 AF.Silu).then_inc(p.sem["act"], 1)
                    p.bump("act")

            @blk.gpsimd
            def _(g):
                p.flush_w(g)

    eout_done_x = p.c["x"]

    for tc in range(8):
        P0, A0, D0 = p.c["pe"], p.c["act"], p.c["dve"]
        I0 = p.c["ind"]
        o1 = p.AB[:, 0:2048]
        o2 = p.AB[:, 2048:4096]
        mile = {}
        with nc.Block() as blk:
            @blk.gpsimd
            def _(g):
                if tc == 0:
                    g.wait_ge(p.sem["x"], eout_done_x * 16)
                for i, dst in ((0, o1), (1, o2)):
                    g.indirect_dma_start(
                        dst, None, t["eout"][:, :],
                        IOA(ap=p.SLOTS[:, 2 * tc + i:2 * tc + i + 1], axis=0)
                    ).then_inc(p.sem["ind"], 16)
                    p.bump("ind")

            @blk.vector
            def _(v):
                def step(ins):
                    ins.then_inc(p.sem["dve"], 1)
                    p.bump("dve")
                    v.wait_ge(p.sem["dve"], p.c["dve"])
                if tc > 0:
                    v.wait_ge(p.sem["out"], p.c["out"] * 16)
                v.wait_ge(p.sem["ind"], (I0 + 2) * 16)
                step(v.tensor_scalar(p.ZT, o2, wcol(tc, 1), None, ALU.mult))
                v.scalar_tensor_tensor(p.ZT, o1, wcol(tc, 0), p.ZT,
                                       ALU.mult, ALU.add)
                v.tensor_tensor(p.ZT, p.ZT,
                                p.AO[:, tc * 2048:(tc + 1) * 2048],
                                ALU.add).then_inc(p.sem["dve"], 1)
                p.bump("dve")
                mile["zt"] = p.c["dve"]
                step(v.tensor_reduce(p.smv(241, 242), p.ZT, AX.X, ALU.add))
                v.wait_ge(p.sem["act"], A0 + 1)
                step(v.tensor_scalar(p.smv(242, 243), p.smv(241, 242), 1.0 / D, None, ALU.mult))
                step(v.tensor_scalar(p.smv(243, 244), p.smv(240, 241), 1.0 / D, None, ALU.mult))
                step(v.tensor_tensor(p.smv(245, 246), p.smv(242, 243), p.smv(242, 243), ALU.mult))
                step(v.tensor_tensor(p.smv(243, 244), p.smv(243, 244), p.smv(245, 246), ALU.subtract))
                mile["var"] = p.c["dve"]
                v.wait_ge(p.sem["act"], A0 + 2)
                step(v.reciprocal(p.smv(244, 245), p.smv(246, 247)))
                step(v.tensor_tensor(p.smv(245, 246), p.smv(242, 243), p.smv(244, 245), ALU.mult))
                v.tensor_scalar(p.ZT, p.ZT, p.smv(244, 245), p.smv(245, 246),
                                ALU.mult, ALU.subtract).then_inc(p.sem["dve"], 1)
                p.bump("dve")
                mile["zn"] = p.c["dve"]

            @blk.scalar
            def _(a):
                a.wait_ge(p.sem["dve"], mile["zt"])
                a.activation(p.AB[:, 4096:6144], p.ZT, AF.Square,
                             accum_out=p.smv(240, 241)).then_inc(p.sem["act"], 1)
                p.bump("act")
                a.wait_ge(p.sem["dve"], mile["var"])
                a.activation(p.smv(246 + 16, 247 + 16) if False else p.smv(246, 247),
                             p.smv(243, 244), AF.Sqrt,
                             bias=p.smv(265, 266)).then_inc(p.sem["act"], 1)
                p.bump("act")

        with nc.Block() as blk:
            @blk.sync
            def _(sy):
                sy.dma_start(t["out"][tc * 128:(tc + 1) * 128, :],
                             p.ZT).then_inc(p.sem["out"], 16)
                p.bump("out")

    with nc.Block() as blk:
        @blk.sync
        def _(sy):
            sy.wait_ge(p.sem["out"], p.c["out"] * 16)
            sy.wait_ge(p.sem["x"], p.c["x"] * 16)
            sy.wait_ge(p.sem["w"], p.c["w"] * 16)
            sy.wait_ge(p.sem["ind"], p.c["ind"] * 16)
            sy.wait_ge(p.sem["cc"], 1)


def emit_store_x2(p, t):
    nc = p.nc
    with nc.Block() as blk:
        @blk.sync
        def _(sy):
            dst = t["x2T"].rearrange("(c p) s -> p c s", p=128)
            for i in range(2):
                src = p.XF[:, i * 8 * TOK:(i + 1) * 8 * TOK].rearrange(
                    "q (c s) -> q c s", c=8)
                sy.dma_start(dst[:, i * 8:(i + 1) * 8, :], src).then_inc(p.sem["out"], 16)
                p.bump("out")
            sy.wait_ge(p.sem["out"], p.c["out"] * 16)
            sy.wait_ge(p.sem["x"], p.c["x"] * 16)
            sy.wait_ge(p.sem["w"], p.c["w"] * 16)



def build_l0():
    nc = bass.Bass("TRN2", target_bir_lowering=False)
    t = {}
    t["xT"] = nc.dram_tensor("xT", [D, SEQ], F32, kind="ExternalInput").ap()
    for nm in ("wq", "wk", "wv", "wo"):
        t[nm] = nc.dram_tensor(nm, [16, 128, D], F32, kind="ExternalInput").ap()
    t["b_attn"] = nc.dram_tensor("b_attn", [128, 64], F32, kind="ExternalInput").ap()
    t["gates"] = nc.dram_tensor("gates", [1, 5], F32, kind="ExternalInput").ap()
    t["obc"] = nc.dram_tensor("obc", [128, 16], F32, kind="ExternalInput").ap()
    t["ident"] = nc.dram_tensor("ident", [128, 128], F32, kind="ExternalInput").ap()
    t["gw"] = nc.dram_tensor("gw", [64, 128, D], I32, kind="ExternalInput").ap()
    t["uw"] = nc.dram_tensor("uw", [64, 128, D], I32, kind="ExternalInput").ap()
    t["dw"] = nc.dram_tensor("dw", [64, 128, D], I32, kind="ExternalInput").ap()
    t["gsc"] = nc.dram_tensor("gsc", [128, 64], F32, kind="ExternalInput").ap()
    t["usc"] = nc.dram_tensor("usc", [128, 64], F32, kind="ExternalInput").ap()
    t["dsc"] = nc.dram_tensor("dsc", [128, 16], F32, kind="ExternalInput").ap()
    t["x2T"] = nc.dram_tensor("x2T", [D, TOK], F32, kind="ExternalOutput").ap()
    p = P(nc)
    emit_init(p, t, l0=True)
    emit_attn_l0(p, t)
    emit_outproj(p, t["wo"], bias_col0=48, n_queued=4)
    emit_ln(p, p.AB[:, 11264:13312])
    emit_ffn_l0(p, t)
    emit_ln(p, p.AB[:, 0:2048])
    emit_store_x2(p, t)
    return nc


def build_fused(dbg_x3=False):
    nc = bass.Bass("TRN2", target_bir_lowering=False)
    t = {}
    t["xT"] = nc.dram_tensor("xT", [D, SEQ], F32, kind="ExternalInput").ap()
    for nm in ("wq", "wk", "wv", "wo"):
        t[nm] = nc.dram_tensor(nm, [16, 128, D], F32, kind="ExternalInput").ap()
    t["b_attn"] = nc.dram_tensor("b_attn", [128, 64], F32, kind="ExternalInput").ap()
    t["gates"] = nc.dram_tensor("gates", [1, 5], F32, kind="ExternalInput").ap()
    t["obc"] = nc.dram_tensor("obc", [128, 16], F32, kind="ExternalInput").ap()
    t["ident"] = nc.dram_tensor("ident", [128, 128], F32, kind="ExternalInput").ap()
    t["gw"] = nc.dram_tensor("gw", [64, 128, D], I32, kind="ExternalInput").ap()
    t["uw"] = nc.dram_tensor("uw", [64, 128, D], I32, kind="ExternalInput").ap()
    t["dw"] = nc.dram_tensor("dw", [64, 128, D], I32, kind="ExternalInput").ap()
    t["gsc"] = nc.dram_tensor("gsc", [128, 64], F32, kind="ExternalInput").ap()
    t["usc"] = nc.dram_tensor("usc", [128, 64], F32, kind="ExternalInput").ap()
    t["dsc"] = nc.dram_tensor("dsc", [128, 16], F32, kind="ExternalInput").ap()
    t["wqkv"] = nc.dram_tensor("wqkv", [24, 128, D], F32, kind="ExternalInput").ap()
    t["b_qkv"] = nc.dram_tensor("b_qkv", [128, 24], F32, kind="ExternalInput").ap()
    t["wout"] = nc.dram_tensor("wout", [16, 128, D], F32, kind="ExternalInput").ap()
    t["b_out"] = nc.dram_tensor("b_out", [128, 16], F32, kind="ExternalInput").ap()
    t["rgw"] = nc.dram_tensor("rgw", [128, 256], F32, kind="ExternalInput").ap()
    t["wg"] = nc.dram_tensor("wg", [16, 8, 128, D], F32, kind="ExternalInput").ap()
    t["wu"] = nc.dram_tensor("wu", [16, 8, 128, D], F32, kind="ExternalInput").ap()
    t["wd"] = nc.dram_tensor("wd", [16, 8, 128, D], F32, kind="ExternalInput").ap()
    t["cc_in"] = nc.dram_tensor("cc_in", [1024, TOK], BF16, kind="Internal").ap()
    t["cc_out"] = nc.dram_tensor("cc_out", [2048, TOK], BF16, kind="Internal").ap()
    t["ltri"] = nc.dram_tensor("ltri", [128, 128], F32, kind="ExternalInput").ap()
    t["ecrow"] = nc.dram_tensor("ecrow", [1, 16], F32, kind="ExternalInput").ap()
    t["ein"] = nc.dram_tensor("ein", [16 * MOE_C, D], BF16, kind="Internal").ap()
    t["eout"] = nc.dram_tensor("eout", [16 * MOE_C, D], BF16, kind="Internal").ap()
    t["out"] = nc.dram_tensor("out", [TOK, D], F32, kind="ExternalOutput").ap()
    if dbg_x3:
        t["x2T"] = nc.dram_tensor("x2T", [D, TOK], F32, kind="ExternalOutput").ap()
        t["x3T"] = nc.dram_tensor("x3T", [D, TOK], F32, kind="ExternalOutput").ap()
    p = P(nc)
    emit_init(p, t, l0=True)
    emit_attn_l0(p, t)
    emit_outproj(p, t["wo"], bias_col0=48, n_queued=4)
    emit_ln(p, p.AB[:, 11264:13312])
    emit_ffn_l0(p, t)
    emit_ln(p, p.AB[:, 0:2048])
    if dbg_x3:
        with nc.Block() as blk:
            @blk.sync
            def _(sy):
                dst = t["x2T"].rearrange("(c q) s -> q c s", q=128)
                for i in range(2):
                    src_ = p.XF[:, i * 8 * TOK:(i + 1) * 8 * TOK].rearrange(
                        "q (c s) -> q c s", c=8)
                    sy.dma_start(dst[:, i * 8:(i + 1) * 8, :], src_).then_inc(p.sem["out"], 16)
                    p.bump("out")
    emit_kv_exchange(p, t)
    emit_attn_l1(p, t)
    emit_outproj(p, t["wout"], bias_col0=294, n_queued=0)
    emit_ln(p, p.AB[:, 0:2048])
    if dbg_x3:
        with nc.Block() as blk:
            @blk.sync
            def _(sy):
                dst = t["x3T"].rearrange("(c q) s -> q c s", q=128)
                for i in range(2):
                    src_ = p.XF[:, i * 8 * TOK:(i + 1) * 8 * TOK].rearrange(
                        "q (c s) -> q c s", c=8)
                    sy.dma_start(dst[:, i * 8:(i + 1) * 8, :], src_).then_inc(p.sem["out"], 16)
                    p.bump("out")
    emit_moe(p, t)
    return nc


def kernel(**inp):
    f32 = np.float32
    x = np.asarray(inp["L0_attn_hidden_states"], f32)

    perm = l1_q_perm()
    w_qkv = np.asarray(inp["L1_attn_w_qkv"], f32)
    wqkv_al = np.concatenate([w_qkv[0:2048][perm], w_qkv[2048:3072]], axis=0)
    b_qkv = np.asarray(inp["L1_attn_b_qkv"], f32)
    b_al = np.concatenate([b_qkv[0:2048][perm], b_qkv[2048:3072]])
    w_out = np.asarray(inp["L1_attn_w_out"], f32)
    rg = np.asarray(inp["L1_ffn_gate_w"], f32)
    rgw = np.ascontiguousarray(
        rg.T.reshape(16, 128, 16).transpose(1, 0, 2).reshape(128, 256))

    shared = {
        "wq": lay_w_out(np.asarray(inp["L0_attn_Wq"], f32)),
        "wk": lay_w_out(np.asarray(inp["L0_attn_Wk"], f32)),
        "wv": lay_w_out(np.asarray(inp["L0_attn_Wv"], f32)),
        "wo": lay_w_out(np.asarray(inp["L0_attn_Wo"], f32)),
        "b_attn": np.concatenate([lay_vec(np.asarray(inp["L0_attn_b" + k], f32))
                                  for k in ("q", "k", "v", "o")], axis=1),
        "gates": np.stack([np.asarray(inp["L0_attn_truth_gate"], f32)[0],
                           np.asarray(inp["L0_attn_balance_gate"], f32)[0],
                           np.asarray(inp["L0_attn_order_gate"], f32)[0],
                           np.asarray(inp["L0_attn_justice_gate"], f32)[0],
                           np.asarray(inp["L0_attn_harmony_gate"], f32)[0]])[None, :].astype(f32),
        "ident": np.eye(128, dtype=f32),
        "gw": lay_w_out(np.asarray(inp["L0_ffn_gate_w"], np.int32)),
        "uw": lay_w_out(np.asarray(inp["L0_ffn_up_w"], np.int32)),
        "dw": lay_w_in(np.asarray(inp["L0_ffn_down_w"], np.int32)),
        "gsc": lay_vec(np.asarray(inp["L0_ffn_gate_scale"], f32)),
        "usc": lay_vec(np.asarray(inp["L0_ffn_up_scale"], f32)),
        "dsc": lay_vec(np.asarray(inp["L0_ffn_down_scale"], f32)),
        "wqkv": lay_w_out(wqkv_al),
        "b_qkv": lay_vec(b_al),
        "wout": lay_w_out(np.ascontiguousarray(w_out[:, perm])),
        "b_out": lay_vec(np.asarray(inp["L1_attn_b_out"], f32)),
        "rgw": rgw,
        "wg": lay_moe_up(np.asarray(inp["L1_ffn_wg"], f32)),
        "wu": lay_moe_up(np.asarray(inp["L1_ffn_wu"], f32)),
        "wd": lay_moe_down(np.asarray(inp["L1_ffn_wd"], f32)),
        "ltri": np.tril(np.ones((128, 128), dtype=f32)).T.copy(),
        "ecrow": (np.arange(16, dtype=f32) * MOE_C - 1.0)[None, :].copy(),
    }

    pos = np.arange(SEQ, dtype=f32) / SEQ * 0.05
    in_maps = []
    for c in range(8):
        b, hh = c // 2, c % 2
        xT = np.ascontiguousarray(x[b].T)
        own = slice(hh * TOK, (hh + 1) * TOK)
        par = slice((1 - hh) * TOK, (2 - hh) * TOK)
        m = dict(shared)
        m["xT"] = np.ascontiguousarray(np.concatenate([xT[:, own], xT[:, par]], axis=1))
        m["obc"] = lay_vec(np.concatenate([pos[own], pos[par]]))
        in_maps.append(m)

    nc = build_fused()
    res = run_bass_kernel_spmd(nc, in_maps, core_ids=list(range(8)))

    out = np.empty((4, SEQ, D), np.float32)
    for c in range(8):
        b, hh = c // 2, c % 2
        out[b, hh * TOK:(hh + 1) * TOK, :] = np.asarray(res.results[c]["out"])
    return out


# version 10
# speedup vs baseline: 1.1738x; 1.0146x over previous
import numpy as np
import ml_dtypes
import concourse.bass as bass
import concourse.mybir as mybir
from concourse.bass_utils import run_bass_kernel_spmd

F32 = mybir.dt.float32
BF16 = mybir.dt.bfloat16
I32 = mybir.dt.int32
AF = mybir.ActivationFunctionType
ALU = mybir.AluOpType
AX = mybir.AxisListType

D = 2048
SEQ = 2048
TOK = 1024
C_STAB = 4.0
LN_EPS = 1e-5
BF = ml_dtypes.bfloat16



def lay_w_out(W):
    O, I = W.shape
    return np.ascontiguousarray(
        W.reshape(O // 128, 128, I // 128, 128).transpose(0, 3, 2, 1).reshape(O // 128, 128, I))


def lay_w_in(W):
    O, I = W.shape
    return np.ascontiguousarray(
        W.reshape(O // 128, 128, I // 128, 128).transpose(2, 3, 0, 1).reshape(I // 128, 128, O))


def lay_vec(b):
    return np.ascontiguousarray(b.reshape(-1, 128).T)


def lay_moe_up(w):
    E, H, I = w.shape
    return np.ascontiguousarray(
        w.reshape(E, H // 128, 128, I // 128, 128).transpose(0, 3, 2, 1, 4).reshape(E, I // 128, 128, H))


def lay_moe_down(w):
    E, I, H = w.shape
    return np.ascontiguousarray(w.reshape(E, I // 128, 128, H))


def l1_q_perm():
    perm = np.empty(2048, np.int64)
    for a in range(2048):
        p_ = a // 512
        r = (a // 128) % 4
        gpar = (a // 64) % 2
        d = a % 64
        perm[a] = (2 * p_ + gpar) * 256 + r * 64 + d
    return perm



class P:
    def __init__(self, nc):
        self.nc = nc
        self.c = {"pe": 0, "act": 0, "dve": 0, "w": 0, "x": 0, "out": 0, "ind": 0}
        self.XB = nc.alloc_sbuf_tensor("XB", [128, 16 * SEQ], BF16)
        self.XF = nc.alloc_sbuf_tensor("XF", [128, 16 * TOK], F32)
        self.AO = nc.alloc_sbuf_tensor("AO", [128, 16 * TOK], BF16)
        self.AB = nc.alloc_sbuf_tensor("AB", [128, 16384], BF16)
        self.AFR = nc.alloc_sbuf_tensor("AFR", [128, 3590], F32)
        self.SM = nc.alloc_sbuf_tensor("SM", [128, 312], F32)
        self.SMB = nc.alloc_sbuf_tensor("SMB", [128, 8], BF16)
        self.SLOTS = nc.alloc_sbuf_tensor("SLOTS", [128, 16], I32)
        self.S01 = nc.alloc_psum_tensor("S01", [128, 1024], F32)
        self.S23 = nc.alloc_psum_tensor("S23", [128, 1024], F32)
        self.PVP = nc.alloc_psum_tensor("PVP", [128, 1024], F32)
        self.ZAp = nc.alloc_psum_tensor("ZAp", [128, 512], F32)
        self.ZBp = nc.alloc_psum_tensor("ZBp", [128, 512], F32)
        self.sem = {k: nc.alloc_semaphore("sem_" + k)
                    for k in ("pe", "act", "dve", "w", "x", "out", "cc", "ind")}
        self.ring = []
        self.ring_next = 0
        self.last_in_slot = {}
        self.freed = {}
        self.w_pos = {}
        self.w_queue = []
        self.slab_home = {}

    def bump(self, k, n=1):
        self.c[k] += n
        return self.c[k]

    def xb(self, hc, a, b):
        return self.XB[:, hc * SEQ + a: hc * SEQ + b]

    def xf(self, hc, a, b):
        return self.XF[:, hc * TOK + a: hc * TOK + b]

    def ao(self, hc, a, b):
        return self.AO[:, hc * TOK + a: hc * TOK + b]

    @property
    def V32(self):
        return self.AFR[:, 0:2048]

    @property
    def SG(self):
        return self.AFR[:, 0:1024]

    @property
    def RZB(self):
        return self.AFR[:, 0:1024]

    @property
    def ZT(self):
        return self.AFR[:, 0:2048]

    @property
    def ID32(self):
        return self.AFR[:, 2048:2176]

    @property
    def COMB(self):
        return self.AFR[:, 2176:2304]

    def rowsp(self, a, b):
        return self.AFR[0:1, 2304 + a:2304 + b]

    def rowalt(self, a, b):
        return self.AFR[0:1, a:b]

    @property
    def ONESROW(self):
        return self.AFR[0:1, 3328:3456]

    @property
    def NEGROW(self):
        return self.AFR[0:1, 3456:3584]

    @property
    def GATESROW(self):
        return self.AFR[0:1, 3584:3589]

    @property
    def ONESB(self):
        return self.SMB[:, 0:1]

    def smv(self, a, b):
        return self.SM[:, a:b]

    def set_ring(self, aps):
        self.ring = list(aps)
        self.ring_next = 0
        self.last_in_slot = {}
        self.freed = {}

    def queue_slab(self, key, src_ap):
        slot = self.ring_next % len(self.ring)
        self.ring_next += 1
        guard_key = self.last_in_slot.get(slot)
        self.last_in_slot[slot] = key
        self.w_queue.append((src_ap, ("ring", slot), guard_key))
        self.c["w"] += 1
        self.w_pos[key] = self.c["w"]
        self.slab_home[key] = ("ring", slot)

    def queue_slab_at(self, key, src_ap, dst_ap, guard_key=None, parts=None):
        if parts is None:
            self.w_queue.append((src_ap, ("fixed", dst_ap), guard_key))
            self.c["w"] += 1
        else:
            for (s, d_) in parts:
                self.w_queue.append((s, ("fixed", d_), guard_key))
                self.c["w"] += 1
                guard_key = None
        self.w_pos[key] = self.c["w"]
        self.slab_home[key] = ("fixed", dst_ap)

    def slab(self, key):
        kind, v = self.slab_home[key]
        return self.ring[v] if kind == "ring" else v

    def slab_wait(self, key):
        return self.w_pos[key] * 16

    def flush_w(self, g):
        for src, home, guard_key in self.w_queue:
            if guard_key is not None:
                if isinstance(guard_key, tuple) and guard_key[0] == "val":
                    g.wait_ge(self.sem["pe"], guard_key[1])
                else:
                    g.wait_ge(self.sem["pe"], self.freed[guard_key])
            kind, v = home
            dst = self.ring[v] if kind == "ring" else v
            g.dma_start(dst, src).then_inc(self.sem["w"], 16)
        self.w_queue.clear()

    def free_slab(self, key, pe_value):
        self.freed[key] = pe_value


def wge(eng, sem, v):
    if v > 0:
        eng.wait_ge(sem, v)



def emit_init(p, t, l0):
    nc = p.nc
    with nc.Block() as blk:
        @blk.sync
        def _(sy):
            src = t["xT"].rearrange("(c p) s -> p c s", p=128)
            for i in range(4):
                dst = p.XF[:, i * 4 * TOK:(i + 1) * 4 * TOK].rearrange(
                    "q (c s) -> q c s", c=4)
                sy.dma_start(dst, src[:, i * 4:(i + 1) * 4, 0:TOK]).then_inc(p.sem["x"], 16)
                p.bump("x")
            if l0:
                sy.dma_start(p.smv(0, 64), t["b_attn"]).then_inc(p.sem["x"], 16)
                sy.dma_start(p.smv(64, 80), t["obc"]).then_inc(p.sem["x"], 16)
                sy.dma_start(p.smv(80, 144), t["gsc"]).then_inc(p.sem["x"], 16)
                sy.dma_start(p.smv(144, 208), t["usc"]).then_inc(p.sem["x"], 16)
                sy.dma_start(p.smv(208, 224), t["dsc"]).then_inc(p.sem["x"], 16)
                sy.dma_start(p.GATESROW, t["gates"]).then_inc(p.sem["x"], 16)
                p.bump("x", 6)
            if "b_qkv" in t:
                sy.dma_start(p.smv(270, 294), t["b_qkv"]).then_inc(p.sem["x"], 16)
                sy.dma_start(p.smv(294, 310), t["b_out"]).then_inc(p.sem["x"], 16)
                p.bump("x", 2)
            sy.dma_start(p.ID32, t["ident"]).then_inc(p.sem["x"], 16)
            p.bump("x")

        @blk.gpsimd
        def _(g):
            src = t["xT"].rearrange("(c p) s -> p c s", p=128)
            for i in range(4):
                dst = p.XB[:, i * 4 * SEQ:(i + 1) * 4 * SEQ].rearrange(
                    "q (c s) -> q c s", c=4)
                g.dma_start(dst, src[:, i * 4:(i + 1) * 4, :]).then_inc(p.sem["w"], 16)
                p.bump("w")

        @blk.vector
        def _(v):
            v.memset(p.ONESB, 1.0)
            v.memset(p.ONESROW, 1.0)
            v.memset(p.NEGROW, -128.0)
            v.memset(p.smv(264, 265), 1.0 / D)
            v.memset(p.smv(265, 266), LN_EPS)
            v.memset(p.smv(266, 267), -C_STAB)
            v.memset(p.smv(267, 268), 0.125)
            v.memset(p.smv(310, 311), 1.0)

    if not l0:
        with nc.Block() as blk:
            @blk.sync
            def _(sy):
                sy.wait_ge(p.sem["x"], p.c["x"] * 16)
            @blk.tensor
            def _(pe):
                pe.wait_ge(p.sem["x"], p.c["x"] * 16)
            @blk.vector
            def _(v):
                v.wait_ge(p.sem["x"], p.c["x"] * 16)
                v.tensor_scalar(p.smv(0, 16), p.smv(0, 16), 0.125, None, ALU.mult)
    if l0:
        with nc.Block() as blk:
            @blk.tensor
            def _(pe):
                pe.wait_ge(p.sem["x"], p.c["x"] * 16)
                pe.matmul(p.ZAp[:, 0:5], p.ONESROW, p.GATESROW,
                          start=True, stop=True).then_inc(p.sem["pe"], 1)
                p.bump("pe")

            @blk.scalar
            def _(a):
                a.wait_ge(p.sem["pe"], p.c["pe"])
                a.activation(p.smv(257, 262), p.ZAp[:, 0:5],
                             AF.Sigmoid).then_inc(p.sem["act"], 1)
                p.bump("act")

            @blk.vector
            def _(v):
                v.wait_ge(p.sem["act"], p.c["act"])
                inv = 1.0 / np.sqrt(128.0)
                v.tensor_scalar(p.smv(256, 257), p.smv(260, 261), -0.1 * inv,
                                inv, ALU.mult, ALU.add)
                v.tensor_scalar(p.smv(240, 256), p.smv(64, 80),
                                p.smv(259, 260), -C_STAB, ALU.mult, ALU.add)
                ins = v.tensor_scalar(p.smv(224, 240), p.smv(0, 16),
                                      p.smv(256, 257), None, ALU.mult)
                if "b_qkv" in t:
                    ins = v.tensor_scalar(p.smv(270, 286), p.smv(270, 286),
                                          0.125, None, ALU.mult)
                ins.then_inc(p.sem["dve"], 1)
                p.bump("dve")


def emit_attn_l0(p, t):
    nc = p.nc
    p.set_ring([p.AB[:, i * 2048:(i + 1) * 2048] for i in range(4)])
    KSB = p.AB[:, 8192:10240]
    QSB = p.AB[:, 10240:11264]
    PT = [p.AB[:, 11264:12288], p.AB[:, 12288:13312]]
    VT = p.AB[:, 13312:15360]

    for h in range(16):
        P0, A0, D0 = p.c["pe"], p.c["act"], p.c["dve"]
        if h == 0:
            p.queue_slab("k0", t["wk"][0])
            p.queue_slab("v0", t["wv"][0])
            p.queue_slab("q0", t["wq"][0])
        if h < 15:
            p.queue_slab(f"k{h+1}", t["wk"][h + 1])
            p.queue_slab(f"v{h+1}", t["wv"][h + 1])
            p.queue_slab(f"q{h+1}", t["wq"][h + 1])
        else:
            for jc in range(4):
                p.queue_slab(f"o{jc}", t["wo"][jc])
        with nc.Block() as blk:
            @blk.tensor
            def _(pe):
                pe.wait_ge(p.sem["w"], p.slab_wait(f"q{h}"))
                if h == 0:
                    wge(pe, p.sem["dve"], p.c["dve"])
                gi = 0
                for kind, n_t, key in (("k", 4, f"k{h}"), ("v", 4, f"v{h}"),
                                       ("q", 2, f"q{h}")):
                    slab = p.slab(key)
                    for tt in range(n_t):
                        ps = (p.S01, p.S23)[(gi % 4) // 2][:, (gi % 2) * 512:(gi % 2) * 512 + 512]
                        if gi >= 4:
                            pe.wait_ge(p.sem["act"], A0 + gi - 4 + 1)
                        ins = None
                        for hc in range(16):
                            ins = pe.matmul(ps, slab[:, hc * 128:(hc + 1) * 128],
                                            p.xb(hc, tt * 512, (tt + 1) * 512),
                                            start=(hc == 0), stop=(hc == 15))
                        ins.then_inc(p.sem["pe"], 1)
                        p.bump("pe")
                        gi += 1
                    p.free_slab(key, p.c["pe"])
                for tt in range(16):
                    pe.wait_ge(p.sem["act"], A0 + 5 + tt // 4)
                    if tt >= 2:
                        pe.wait_ge(p.sem["dve"], D0 + tt - 1)
                    dstp = (p.ZAp, p.ZBp)[tt % 2][:, 0:128]
                    pe.transpose(dstp, p.V32[:, tt * 128:(tt + 1) * 128],
                                 p.ID32).then_inc(p.sem["pe"], 1)
                    p.bump("pe")
                s_inc = {}

                def emit_S(kc):
                    Sp = (p.S01, p.S23)[kc % 2]
                    pe.wait_ge(p.sem["act"],
                               A0 + 10 + (kc - 2) + 1 if kc >= 2 else A0 + 10)
                    ins = None
                    for half in range(2):
                        ins = pe.matmul(Sp[:, half * 512:(half + 1) * 512],
                                        KSB[:, kc * 128:(kc + 1) * 128],
                                        QSB[:, half * 512:(half + 1) * 512],
                                        start=True, stop=True)
                    ins.then_inc(p.sem["pe"], 1)
                    s_inc[kc] = p.bump("pe")

                emit_S(0)
                emit_S(1)
                for kc in range(16):
                    pe.wait_ge(p.sem["act"], A0 + 10 + kc + 1)
                    pe.wait_ge(p.sem["dve"], D0 + kc + 1)
                    ins = None
                    for half in range(2):
                        ins = pe.matmul(p.PVP[:, half * 512:(half + 1) * 512],
                                        VT[:, kc * 128:(kc + 1) * 128],
                                        PT[kc % 2][:, half * 512:(half + 1) * 512],
                                        start=(kc == 0), stop=(kc == 15))
                    ins.then_inc(p.sem["pe"], 1)
                    p.bump("pe")
                    if kc == 0:
                        pe.wait_ge(p.sem["dve"], D0 + 16)
                    ins = None
                    for half, zp in ((0, p.ZAp), (1, p.ZBp)):
                        ins = pe.matmul(zp[0:1, :], p.ONESB,
                                        PT[kc % 2][:, half * 512:(half + 1) * 512],
                                        start=(kc == 0), stop=(kc == 15))
                    ins.then_inc(p.sem["pe"], 1)
                    p.bump("pe")
                    if kc + 2 < 16:
                        emit_S(kc + 2)
                pe.wait_ge(p.sem["dve"], D0 + 18)
                pe.wait_ge(p.sem["act"], A0 + 26)
                zr = p.rowsp(0, 1024)
                pe.matmul(p.S01[:, 0:512], p.ONESROW, zr[0:1, 0:512],
                          start=True, stop=True)
                pe.matmul(p.S01[:, 512:1024], p.ONESROW, zr[0:1, 512:1024],
                          start=True, stop=True).then_inc(p.sem["pe"], 1)
                p.bump("pe")
                assert p.c["pe"] == P0 + 75, p.c["pe"]
                p.l0_s_inc = s_inc

            @blk.scalar
            def _(a):
                gi = 0
                for kind, n_t in (("k", 4), ("v", 4), ("q", 2)):
                    for tt in range(n_t):
                        ps = (p.S01, p.S23)[(gi % 4) // 2][:, (gi % 2) * 512:(gi % 2) * 512 + 512]
                        a.wait_ge(p.sem["pe"], P0 + gi + 1)
                        if kind == "k":
                            ins = a.activation(KSB[:, tt * 512:(tt + 1) * 512], ps,
                                               AF.Identity, bias=p.smv(16 + h, 17 + h))
                        elif kind == "v":
                            ins = a.activation(p.V32[:, tt * 512:(tt + 1) * 512], ps,
                                               AF.Identity, bias=p.smv(32 + h, 33 + h))
                        else:
                            ins = a.activation(QSB[:, tt * 512:(tt + 1) * 512], ps,
                                               AF.Identity, bias=p.smv(224 + h, 225 + h),
                                               scale=p.smv(256, 257))
                        ins.then_inc(p.sem["act"], 1)
                        p.bump("act")
                        gi += 1
                for kc in range(16):
                    Sp = (p.S01, p.S23)[kc % 2]
                    a.wait_ge(p.sem["pe"], p.l0_s_inc[kc])
                    a.activation(PT[kc % 2], Sp[:, :], AF.Exp,
                                 bias=p.smv(240 + kc, 241 + kc)).then_inc(p.sem["act"], 1)
                    p.bump("act")
                assert p.c["act"] == A0 + 26

            @blk.vector
            def _(v):
                for tt in range(16):
                    v.wait_ge(p.sem["pe"], P0 + 10 + tt + 1)
                    v.tensor_copy(VT[:, tt * 128:(tt + 1) * 128],
                                  (p.ZAp, p.ZBp)[tt % 2][:, 0:128]).then_inc(p.sem["dve"], 1)
                    p.bump("dve")
                zr = p.rowsp(0, 1024)
                v.wait_ge(p.sem["pe"], P0 + 74)
                v.reciprocal(zr[0:1, 0:512], p.ZAp[0:1, :]).then_inc(p.sem["dve"], 1)
                v.reciprocal(zr[0:1, 512:1024], p.ZBp[0:1, :]).then_inc(p.sem["dve"], 1)
                p.bump("dve", 2)
                v.wait_ge(p.sem["pe"], P0 + 75)
                v.tensor_copy(p.RZB, p.S01[:, :]).then_inc(p.sem["dve"], 1)
                p.bump("dve")
                v.tensor_tensor(p.ao(h, 0, TOK), p.PVP[:, :], p.RZB,
                                ALU.mult).then_inc(p.sem["dve"], 1)
                p.bump("dve")
                assert p.c["dve"] == D0 + 20

            @blk.gpsimd
            def _(g):
                p.flush_w(g)


def emit_outproj(p, wo_dram, bias_col0, n_queued):
    nc = p.nc
    P0, A0, D0 = p.c["pe"], p.c["act"], p.c["dve"]
    for jc in range(n_queued, 16):
        p.queue_slab(f"o{jc}", wo_dram[jc])
    with nc.Block() as blk:
        @blk.tensor
        def _(pe):
            gi = 0
            for jc in range(16):
                pe.wait_ge(p.sem["w"], p.slab_wait(f"o{jc}"))
                slab = p.slab(f"o{jc}")
                for th in range(2):
                    ps = (p.S01, p.S23)[(gi % 4) // 2][:, (gi % 2) * 512:(gi % 2) * 512 + 512]
                    if gi >= 4:
                        pe.wait_ge(p.sem["dve"], D0 + gi - 4 + 1)
                    ins = None
                    for hc in range(16):
                        ins = pe.matmul(ps, slab[:, hc * 128:(hc + 1) * 128],
                                        p.ao(hc, th * 512, (th + 1) * 512),
                                        start=(hc == 0), stop=(hc == 15))
                    ins.then_inc(p.sem["pe"], 1)
                    p.bump("pe")
                    gi += 1
                p.free_slab(f"o{jc}", p.c["pe"])

        @blk.vector
        def _(v):
            gi = 0
            for jc in range(16):
                for th in range(2):
                    ps = (p.S01, p.S23)[(gi % 4) // 2][:, (gi % 2) * 512:(gi % 2) * 512 + 512]
                    v.wait_ge(p.sem["pe"], P0 + gi + 1)
                    xfr = p.xf(jc, th * 512, (th + 1) * 512)
                    v.scalar_tensor_tensor(
                        xfr, ps, p.smv(bias_col0 + jc, bias_col0 + jc + 1),
                        xfr, ALU.add, ALU.add).then_inc(p.sem["dve"], 1)
                    p.bump("dve")
                    gi += 1

        @blk.gpsimd
        def _(g):
            p.flush_w(g)


def emit_ln(p, sq_area):
    nc = p.nc
    P0, A0, D0 = p.c["pe"], p.c["act"], p.c["dve"]
    m_row = p.rowsp(0, 1024)
    r_row = p.rowalt(0, 1024)
    e2a = p.rowalt(1024, 1536)
    e2b = p.rowalt(1536, 2048)
    with nc.Block() as blk:
        @blk.tensor
        def _(pe):
            for hc in range(16):
                pe.wait_ge(p.sem["dve"], D0 + hc + 1)
                pe.matmul(p.ZAp[0:1, :], p.ONESB, p.xb(hc, 0, 512),
                          start=(hc == 0), stop=(hc == 15))
                pe.matmul(p.ZBp[0:1, :], p.ONESB, p.xb(hc, 512, 1024),
                          start=(hc == 0), stop=(hc == 15))
                pe.wait_ge(p.sem["act"], A0 + hc + 1)
                sq = sq_area[:, (hc % 2) * 1024:(hc % 2) * 1024 + 1024]
                pe.matmul(p.S01[0:1, 0:512], p.ONESB, sq[:, 0:512],
                          start=(hc == 0), stop=(hc == 15))
                ins = pe.matmul(p.S01[0:1, 512:1024], p.ONESB, sq[:, 512:1024],
                                start=(hc == 0), stop=(hc == 15))
                ins.then_inc(p.sem["pe"], 1)
                p.bump("pe")
            pe.wait_ge(p.sem["dve"], D0 + 19)
            pe.matmul(p.S23[:, 0:512], p.ONESROW, r_row[0:1, 0:512],
                      start=True, stop=True)
            pe.matmul(p.S23[:, 512:1024], p.ONESROW, r_row[0:1, 512:1024],
                      start=True, stop=True).then_inc(p.sem["pe"], 1)
            p.bump("pe")
            pe.matmul(p.PVP[:, 0:512], p.ONESROW, m_row[0:1, 0:512],
                      start=True, stop=True)
            pe.matmul(p.PVP[:, 512:1024], p.ONESROW, m_row[0:1, 512:1024],
                      start=True, stop=True).then_inc(p.sem["pe"], 1)
            p.bump("pe")

        @blk.scalar
        def _(a):
            for hc in range(16):
                a.wait_ge(p.sem["dve"], D0 + hc + 1)
                if hc >= 2:
                    a.wait_ge(p.sem["pe"], P0 + hc - 1)
                a.activation(sq_area[:, (hc % 2) * 1024:(hc % 2) * 1024 + 1024],
                             p.xb(hc, 0, TOK), AF.Square).then_inc(p.sem["act"], 1)
                p.bump("act")
            a.wait_ge(p.sem["pe"], P0 + 16)
            a.activation(m_row[0:1, 0:512], p.ZAp[0:1, :], AF.Identity, scale=p.smv(264, 265)[0:1, :])
            a.activation(m_row[0:1, 512:1024], p.ZBp[0:1, :], AF.Identity,
                         scale=p.smv(264, 265)[0:1, :]).then_inc(p.sem["act"], 2)
            p.bump("act", 2)
            a.wait_ge(p.sem["act"], p.c["act"])
            a.activation(e2a, p.S01[0:1, 0:512], AF.Identity, scale=p.smv(264, 265)[0:1, :])
            a.activation(e2b, p.S01[0:1, 512:1024], AF.Identity,
                         scale=p.smv(264, 265)[0:1, :]).then_inc(p.sem["act"], 2)
            p.bump("act", 2)
            a.activation(p.ZAp[0:1, :], m_row[0:1, 0:512], AF.Square)
            a.activation(p.ZBp[0:1, :], m_row[0:1, 512:1024],
                         AF.Square).then_inc(p.sem["act"], 2)
            p.bump("act", 2)
            a.wait_ge(p.sem["dve"], D0 + 17)
            a.activation(p.ZAp[0:1, :], r_row[0:1, 0:512], AF.Sqrt, bias=p.smv(265, 266)[0:1, :])
            a.activation(p.ZBp[0:1, :], r_row[0:1, 512:1024], AF.Sqrt,
                         bias=p.smv(265, 266)[0:1, :]).then_inc(p.sem["act"], 2)
            p.bump("act", 2)

        @blk.vector
        def _(v):
            for hc in range(16):
                v.tensor_copy(p.xb(hc, 0, TOK),
                              p.xf(hc, 0, TOK)).then_inc(p.sem["dve"], 1)
                p.bump("dve")
            v.wait_ge(p.sem["act"], A0 + 22)
            v.tensor_tensor(r_row[0:1, 0:512], e2a, p.ZAp[0:1, :], ALU.subtract)
            v.tensor_tensor(r_row[0:1, 512:1024], e2b, p.ZBp[0:1, :],
                            ALU.subtract).then_inc(p.sem["dve"], 1)
            p.bump("dve")
            v.wait_ge(p.sem["act"], A0 + 24)
            v.reciprocal(r_row[0:1, 0:512], p.ZAp[0:1, :])
            v.reciprocal(r_row[0:1, 512:1024], p.ZBp[0:1, :]).then_inc(p.sem["dve"], 1)
            p.bump("dve")
            v.wait_ge(p.sem["dve"], p.c["dve"])
            v.tensor_tensor(m_row, m_row, r_row, ALU.mult).then_inc(p.sem["dve"], 1)
            p.bump("dve")
            v.wait_ge(p.sem["pe"], P0 + 18)
            for hc in range(16):
                v.tensor_tensor(p.xf(hc, 0, TOK), p.xf(hc, 0, TOK),
                                p.S23[:, :], ALU.mult)
                v.tensor_tensor(p.xf(hc, 0, TOK), p.xf(hc, 0, TOK),
                                p.PVP[:, :], ALU.subtract)
                v.tensor_copy(p.xb(hc, 0, TOK),
                              p.xf(hc, 0, TOK)).then_inc(p.sem["dve"], 1)
                p.bump("dve")


def emit_ffn_l0(p, t):
    nc = p.nc
    p.set_ring([p.AB[:, i * 2048:(i + 1) * 2048] for i in range(8)]
               + [p.AO[:, 8192 + i * 2048:8192 + (i + 1) * 2048] for i in range(4)])
    s_row = p.rowsp(0, 1024)
    sh_row = p.rowalt(1024, 2048)

    seq = []
    for r in range(8):
        for i in range(8):
            seq.append(("g", 8 * r + i))
            seq.append(("u", 8 * r + i))
        for i in range(8):
            seq.append(("d", 8 * r + i))

    def src_of(kind, idx):
        return {"g": t["gw"], "u": t["uw"], "d": t["dw"]}[kind][idx]

    qptr = [0]

    def queue_upto(n):
        while qptr[0] < min(n, len(seq)):
            kind, idx = seq[qptr[0]]
            p.queue_slab(f"{kind}{idx}", src_of(kind, idx))
            qptr[0] += 1

    P0 = p.c["pe"]
    pre_w = p.c["w"]
    queue_upto(12)
    with nc.Block() as blk:
        @blk.tensor
        def _(pe):
            ins = None
            for hc in range(16):
                pe.matmul(p.ZAp[0:1, :], p.ONESB, p.xb(hc, 0, 512),
                          start=(hc == 0), stop=(hc == 15))
                ins = pe.matmul(p.ZBp[0:1, :], p.ONESB, p.xb(hc, 512, 1024),
                                start=(hc == 0), stop=(hc == 15))
            ins.then_inc(p.sem["pe"], 1)
            p.bump("pe")

        @blk.scalar
        def _(a):
            a.wait_ge(p.sem["pe"], P0 + 1)
            a.activation(s_row[0:1, 0:512], p.ZAp[0:1, :], AF.Identity)
            a.activation(s_row[0:1, 512:1024], p.ZBp[0:1, :],
                         AF.Identity).then_inc(p.sem["act"], 1)
            p.bump("act")

        @blk.gpsimd
        def _(g):
            g.wait_ge(p.sem["w"], pre_w * 16)
            p.flush_w(g)

    for r in range(8):
        P0, A0, D0 = p.c["pe"], p.c["act"], p.c["dve"]
        queue_upto(12 + 24 * (r + 1))
        with nc.Block() as blk:
            @blk.tensor
            def _(pe):
                for i in range(8):
                    ic = 8 * r + i
                    pe.wait_ge(p.sem["w"], p.slab_wait(f"u{ic}"))
                    gs = p.slab(f"g{ic}")
                    us = p.slab(f"u{ic}")
                    for th in range(2):
                        if i >= 1:
                            pe.wait_ge(p.sem["act"], A0 + 2 * (i - 1) + th + 1)
                        ps = p.S01[:, th * 512:(th + 1) * 512]
                        for hc in range(16):
                            pe.matmul(ps, gs[:, hc * 128:(hc + 1) * 128],
                                      p.xb(hc, th * 512, (th + 1) * 512),
                                      start=(hc == 0), stop=False)
                        pe.matmul(ps, p.NEGROW, s_row[0:1, th * 512:(th + 1) * 512],
                                  start=False, stop=True).then_inc(p.sem["pe"], 1)
                        p.bump("pe")
                    for th in range(2):
                        if i >= 1:
                            pe.wait_ge(p.sem["dve"], D0 + 2 * (i - 1) + th + 1)
                        ps = p.S23[:, th * 512:(th + 1) * 512]
                        for hc in range(16):
                            pe.matmul(ps, us[:, hc * 128:(hc + 1) * 128],
                                      p.xb(hc, th * 512, (th + 1) * 512),
                                      start=(hc == 0), stop=False)
                        pe.matmul(ps, p.NEGROW, s_row[0:1, th * 512:(th + 1) * 512],
                                  start=False, stop=True).then_inc(p.sem["pe"], 1)
                        p.bump("pe")
                    p.free_slab(f"g{ic}", p.c["pe"])
                    p.free_slab(f"u{ic}", p.c["pe"])
                ins = None
                for i in range(8):
                    pe.wait_ge(p.sem["dve"], D0 + 2 * i + 2)
                    hsb = p.AO[:, i * 1024:(i + 1) * 1024]
                    pe.matmul(p.ZAp[0:1, :], p.ONESB, hsb[:, 0:512],
                              start=(i == 0), stop=(i == 7))
                    ins = pe.matmul(p.ZBp[0:1, :], p.ONESB, hsb[:, 512:1024],
                                    start=(i == 0), stop=(i == 7))
                ins.then_inc(p.sem["pe"], 1)
                p.bump("pe")
                gi = 0
                for jc in range(16):
                    if jc == 0:
                        pe.wait_ge(p.sem["act"], A0 + 18)
                        pe.wait_ge(p.sem["w"], p.slab_wait(f"d{8 * r + 7}"))
                    for th in range(2):
                        rot = gi % 4
                        ps = (p.PVP, p.S01)[rot // 2][:, (rot % 2) * 512:(rot % 2) * 512 + 512]
                        if gi >= 4:
                            pe.wait_ge(p.sem["dve"], D0 + 16 + (gi - 4) + 1)
                        for i in range(8):
                            ds = p.slab(f"d{8 * r + i}")
                            pe.matmul(ps, ds[:, jc * 128:(jc + 1) * 128],
                                      p.AO[:, i * 1024 + th * 512:i * 1024 + (th + 1) * 512],
                                      start=(i == 0), stop=False)
                        pe.matmul(ps, p.NEGROW, sh_row[0:1, th * 512:(th + 1) * 512],
                                  start=False, stop=True).then_inc(p.sem["pe"], 1)
                        p.bump("pe")
                        gi += 1
                for i in range(8):
                    p.free_slab(f"d{8 * r + i}", p.c["pe"])

            @blk.scalar
            def _(a):
                for i in range(8):
                    ic = 8 * r + i
                    for th in range(2):
                        a.wait_ge(p.sem["pe"], P0 + 4 * i + th + 1)
                        if i >= 1:
                            a.wait_ge(p.sem["dve"], D0 + 2 * (i - 1) + th + 1)
                        a.activation(p.SG[:, th * 512:(th + 1) * 512],
                                     p.S01[:, th * 512:(th + 1) * 512], AF.Silu,
                                     scale=p.smv(80 + ic, 81 + ic)).then_inc(p.sem["act"], 1)
                        p.bump("act")
                a.wait_ge(p.sem["pe"], P0 + 33)
                a.activation(sh_row[0:1, 0:512], p.ZAp[0:1, :], AF.Identity)
                a.activation(sh_row[0:1, 512:1024], p.ZBp[0:1, :],
                             AF.Identity).then_inc(p.sem["act"], 2)
                p.bump("act", 2)

            @blk.vector
            def _(v):
                for i in range(8):
                    ic = 8 * r + i
                    for th in range(2):
                        v.wait_ge(p.sem["pe"], P0 + 4 * i + 2 + th + 1)
                        v.wait_ge(p.sem["act"], A0 + 2 * i + th + 1)
                        v.scalar_tensor_tensor(
                            p.AO[:, i * 1024 + th * 512:i * 1024 + (th + 1) * 512],
                            p.S23[:, th * 512:(th + 1) * 512],
                            p.smv(144 + ic, 145 + ic),
                            p.SG[:, th * 512:(th + 1) * 512],
                            ALU.mult, ALU.mult).then_inc(p.sem["dve"], 1)
                        p.bump("dve")
                gi = 0
                for jc in range(16):
                    for th in range(2):
                        rot = gi % 4
                        ps = (p.PVP, p.S01)[rot // 2][:, (rot % 2) * 512:(rot % 2) * 512 + 512]
                        v.wait_ge(p.sem["pe"], P0 + 33 + gi + 1)
                        xfr = p.xf(jc, th * 512, (th + 1) * 512)
                        v.scalar_tensor_tensor(
                            xfr, ps, p.smv(208 + jc, 209 + jc),
                            xfr, ALU.mult, ALU.add).then_inc(p.sem["dve"], 1)
                        p.bump("dve")
                        gi += 1

            @blk.gpsimd
            def _(g):
                p.flush_w(g)


def emit_kv_exchange(p, t):
    nc = p.nc
    p.set_ring([p.AB[:, i * 2048:(i + 1) * 2048] for i in range(4)])
    KVO = p.AB[:, 8192:16384]
    for c in range(8):
        p.queue_slab(f"KV{c}", t["wqkv"][16 + c])
    P0, A0 = p.c["pe"], p.c["act"]
    with nc.Block() as blk:
        @blk.tensor
        def _(pe):
            gi = 0
            for c in range(8):
                pe.wait_ge(p.sem["w"], p.slab_wait(f"KV{c}"))
                slab = p.slab(f"KV{c}")
                for tc in range(2):
                    ps = (p.S01, p.S23)[(gi % 4) // 2][:, (gi % 2) * 512:(gi % 2) * 512 + 512]
                    if gi >= 4:
                        pe.wait_ge(p.sem["act"], A0 + gi - 4 + 1)
                    ins = None
                    for hc in range(16):
                        ins = pe.matmul(ps, slab[:, hc * 128:(hc + 1) * 128],
                                        p.xb(hc, tc * 512, (tc + 1) * 512),
                                        start=(hc == 0), stop=(hc == 15))
                    ins.then_inc(p.sem["pe"], 1)
                    p.bump("pe")
                    gi += 1
                p.free_slab(f"KV{c}", p.c["pe"])

        @blk.scalar
        def _(a):
            gi = 0
            for c in range(8):
                for tc in range(2):
                    ps = (p.S01, p.S23)[(gi % 4) // 2][:, (gi % 2) * 512:(gi % 2) * 512 + 512]
                    a.wait_ge(p.sem["pe"], P0 + gi + 1)
                    a.activation(KVO[:, c * 1024 + tc * 512:c * 1024 + (tc + 1) * 512],
                                 ps, AF.Identity,
                                 bias=p.smv(286 + c, 287 + c)).then_inc(p.sem["act"], 1)
                    p.bump("act")
                    gi += 1

        @blk.gpsimd
        def _(g):
            p.flush_w(g)

    with nc.Block() as blk:
        @blk.gpsimd
        def _(g):
            dst = t["cc_in"].rearrange("(c q) s -> q c s", q=128)
            src = KVO.rearrange("q (c s) -> q c s", c=8)
            g.dma_start(dst, src).then_inc(p.sem["w"], 16)
            p.bump("w")
            g.wait_ge(p.sem["w"], p.c["w"] * 16)
            g.collective_compute(
                "AllGather", ALU.bypass,
                replica_groups=[[0, 1], [2, 3], [4, 5], [6, 7]],
                ins=[t["cc_in"][:, :]], outs=[t["cc_out"][:, :]]).then_inc(p.sem["cc"], 1)


def emit_attn_l1(p, t):
    nc = p.nc
    p.set_ring([p.AB[:, i * 2048:(i + 1) * 2048] for i in range(2)])
    KSB = p.AB[:, 4096:6144]
    QSB = p.AB[:, 6144:10240]
    PT = [p.AB[:, 10240:11264], p.AB[:, 11264:12288]]
    VB = p.AB[:, 10240:12288]

    def vta(g_loc, kc):
        off = 12288 + g_loc * 1040 + kc * 65
        return p.AB[:, off:off + 65]

    for pr in range(4):
        P0, A0, D0 = p.c["pe"], p.c["act"], p.c["dve"]
        for j in range(4):
            p.queue_slab(f"Q{pr}_{j}", t["wqkv"][4 * pr + j])
        xkv = p.c["x"] + 4
        with nc.Block() as blk:
            @blk.sync
            def _(sy):
                if pr == 0:
                    sy.wait_ge(p.sem["cc"], 1)
                cco = t["cc_out"]
                for r in range(2):
                    sy.dma_start(KSB[:, r * 1024:(r + 1) * 1024],
                                 cco[r * 1024 + pr * 128:r * 1024 + (pr + 1) * 128, :]
                                 ).then_inc(p.sem["x"], 16)
                    sy.dma_start(VB[:, r * 1024:(r + 1) * 1024],
                                 cco[r * 1024 + 512 + pr * 128:r * 1024 + 512 + (pr + 1) * 128, :]
                                 ).then_inc(p.sem["x"], 16)
                    p.bump("x", 2)

            @blk.vector
            def _(v):
                v.wait_ge(p.sem["x"], xkv * 16)
                v.tensor_copy(p.V32, VB).then_inc(p.sem["dve"], 1)
                p.bump("dve")
                if pr == 0:
                    for g_loc in range(2):
                        for kc in range(16):
                            v.memset(vta(g_loc, kc)[:, 64:65], 1.0)
                idx = 0
                for g_loc in range(2):
                    for tt in range(16):
                        v.wait_ge(p.sem["pe"], P0 + idx + 1)
                        v.tensor_copy(vta(g_loc, tt)[:, 0:64],
                                      (p.ZAp, p.ZBp)[idx % 2][:, 0:64]).then_inc(p.sem["dve"], 1)
                        p.bump("dve")
                        idx += 1
                for hd in range(8):
                    g_loc, rr = hd // 4, hd % 4
                    HB = P0 + 40 + hd * 49
                    b64 = g_loc * 64
                    zrow = 0
                    zr = p.rowsp(0, 1024)
                    v.wait_ge(p.sem["pe"], HB + 48)
                    v.reciprocal(zr[0:1, 0:512], p.ZAp[zrow:zrow + 1, :])
                    v.reciprocal(zr[0:1, 512:1024],
                                 p.ZBp[zrow:zrow + 1, :]).then_inc(p.sem["dve"], 1)
                    p.bump("dve")
                    v.wait_ge(p.sem["pe"], HB + 49)
                    v.tensor_copy(p.RZB[:, 0:512], p.ZAp[:, :])
                    v.tensor_copy(p.RZB[:, 512:1024], p.ZBp[:, :]).then_inc(p.sem["dve"], 1)
                    p.bump("dve")
                    ao_c = 4 * pr + rr
                    v.tensor_tensor(
                        p.AO[b64:b64 + 64, ao_c * TOK:(ao_c + 1) * TOK],
                        p.PVP[b64:b64 + 64, :], p.RZB[b64:b64 + 64, :],
                        ALU.mult).then_inc(p.sem["dve"], 1)
                    p.bump("dve")

            @blk.tensor
            def _(pe):
                if pr == 0:
                    pe.wait_ge(p.sem["cc"], 1)
                idx = 0
                for g_loc in range(2):
                    idv = p.ID32[g_loc * 64:(g_loc + 1) * 64,
                                 g_loc * 64:(g_loc + 1) * 64]
                    for tt in range(16):
                        pe.wait_ge(p.sem["dve"], D0 + 1)
                        if idx >= 2:
                            pe.wait_ge(p.sem["dve"], D0 + 1 + idx - 1)
                        dstp = (p.ZAp, p.ZBp)[idx % 2][:, 0:64]
                        pe.transpose(dstp,
                                     p.V32[g_loc * 64:(g_loc + 1) * 64,
                                           tt * 128:(tt + 1) * 128],
                                     idv).then_inc(p.sem["pe"], 1)
                        p.bump("pe")
                        idx += 1
                gi = 0
                for j in range(4):
                    slab = p.slab(f"Q{pr}_{j}")
                    pe.wait_ge(p.sem["w"], p.slab_wait(f"Q{pr}_{j}"))
                    for th in range(2):
                        ps = (p.S01, p.S23)[(gi % 4) // 2][:, (gi % 2) * 512:(gi % 2) * 512 + 512]
                        if gi >= 4:
                            pe.wait_ge(p.sem["act"], A0 + gi - 4 + 1)
                        ins = None
                        for hc in range(16):
                            ins = pe.matmul(ps, slab[:, hc * 128:(hc + 1) * 128],
                                            p.xb(hc, th * 512, (th + 1) * 512),
                                            start=(hc == 0), stop=(hc == 15))
                        ins.then_inc(p.sem["pe"], 1)
                        p.bump("pe")
                        gi += 1
                    p.free_slab(f"Q{pr}_{j}", p.c["pe"])
                p.l1_s_inc = {}
                for hd in range(8):
                    g_loc, rr = hd // 4, hd % 4
                    b64 = g_loc * 64
                    zrow = 0
                    tpos_pv = (0, 64) if g_loc == 1 else None
                    tpos_z = None

                    def emit_S(kc, hd=hd, rr=rr, b64=b64):
                        Sp = (p.S01, p.S23)[kc % 2]
                        e = hd * 16 + kc
                        pe.wait_ge(p.sem["act"],
                                   A0 + 8 + e - 2 + 1 if e >= 2 else A0 + 8)
                        ins = None
                        for half in range(2):
                            ins = pe.matmul(
                                Sp[:, half * 512:(half + 1) * 512],
                                KSB[b64:b64 + 64, kc * 128:(kc + 1) * 128],
                                QSB[b64:b64 + 64,
                                    rr * 1024 + half * 512:rr * 1024 + (half + 1) * 512],
                                start=True, stop=True)
                        ins.then_inc(p.sem["pe"], 1)
                        p.l1_s_inc[(hd, kc)] = p.bump("pe")

                    emit_S(0)
                    emit_S(1)
                    for kc in range(16):
                        e = hd * 16 + kc
                        pe.wait_ge(p.sem["act"], A0 + 8 + e + 1)
                        pe.wait_ge(p.sem["dve"], D0 + 1 + g_loc * 16 + kc + 1)
                        if kc == 0 and hd >= 1:
                            pe.wait_ge(p.sem["dve"], D0 + 33 + (hd - 1) * 3 + 3)
                        if kc == 0 and hd == 0:
                            pe.wait_ge(p.sem["dve"], D0 + 33)
                        ins = None
                        for half in range(2):
                            ins = pe.matmul(p.PVP[b64:b64 + 64, half * 512:(half + 1) * 512],
                                            vta(g_loc, kc)[:, 0:64],
                                            PT[kc % 2][:, half * 512:(half + 1) * 512],
                                            start=(kc == 0), stop=(kc == 15),
                                            tile_position=tpos_pv)
                        ins.then_inc(p.sem["pe"], 1)
                        p.bump("pe")
                        ins = None
                        for half, zp in ((0, p.ZAp), (1, p.ZBp)):
                            ins = pe.matmul(zp[zrow:zrow + 1, :],
                                            vta(g_loc, kc)[:, 64:65],
                                            PT[kc % 2][:, half * 512:(half + 1) * 512],
                                            start=(kc == 0), stop=(kc == 15),
                                            tile_position=tpos_z)
                        ins.then_inc(p.sem["pe"], 1)
                        p.bump("pe")
                        if kc + 2 < 16:
                            emit_S(kc + 2)
                    pe.wait_ge(p.sem["dve"], D0 + 33 + hd * 3 + 1)
                    zr = p.rowsp(0, 1024)
                    pe.matmul(p.ZAp[:, :], p.ONESROW, zr[0:1, 0:512],
                              start=True, stop=True)
                    pe.matmul(p.ZBp[:, :], p.ONESROW, zr[0:1, 512:1024],
                              start=True, stop=True).then_inc(p.sem["pe"], 1)
                    p.bump("pe")
                assert p.c["pe"] == P0 + 40 + 8 * 49

            @blk.scalar
            def _(a):
                gi = 0
                for j in range(4):
                    for th in range(2):
                        ps = (p.S01, p.S23)[(gi % 4) // 2][:, (gi % 2) * 512:(gi % 2) * 512 + 512]
                        a.wait_ge(p.sem["pe"], P0 + 32 + gi + 1)
                        a.activation(
                            QSB[:, j * 1024 + th * 512:j * 1024 + (th + 1) * 512],
                            ps, AF.Identity,
                            bias=p.smv(270 + 4 * pr + j, 271 + 4 * pr + j),
                            scale=p.smv(267, 268)).then_inc(p.sem["act"], 1)
                        p.bump("act")
                        gi += 1
                a.wait_ge(p.sem["dve"], D0 + 1)
                for hd in range(8):
                    for kc in range(16):
                        Sp = (p.S01, p.S23)[kc % 2]
                        a.wait_ge(p.sem["pe"], p.l1_s_inc[(hd, kc)])
                        a.activation(PT[kc % 2], Sp[:, :], AF.Exp,
                                     bias=p.smv(266, 267)).then_inc(p.sem["act"], 1)
                        p.bump("act")
                assert p.c["act"] == A0 + 8 + 128

            @blk.gpsimd
            def _(g):
                p.flush_w(g)


MOE_C = 512


def emit_moe(p, t):
    nc = p.nc
    IOA = bass.IndirectOffsetOnAxis
    RGWB = p.AFR[:, 2304:2560]
    LTRI = p.AFR[:, 2176:2304]
    BASE = p.AFR[0:1, 2560:2576]
    ECROW = p.AFR[0:1, 2576:2592]
    ECB = p.SG[:, 656:672]
    MSCR = p.SG[:, 672:688]
    onesf = p.smv(310, 311)

    def eq1m(tc):
        return p.SG[:, 384 + tc * 16:384 + (tc + 1) * 16]

    def eq2m(tc):
        return p.SG[:, 512 + tc * 16:512 + (tc + 1) * 16]

    def wcol(tc, i):
        return p.smv(246 + 2 * tc + i, 247 + 2 * tc + i)

    def xo_piece(i):
        return p.XB[:, i * SEQ:i * SEQ + 1024]

    def xe(hc):
        return p.AB[:, 8192 + hc * 512:8192 + (hc + 1) * 512]

    def hte(ic):
        pc = 12 + ic // 2
        off = (ic % 2) * 512
        return p.XB[:, pc * SEQ + off:pc * SEQ + off + 512]

    def oute(sc, half):
        pc = (sc % 2) * 2 + half
        return p.XB[:, pc * SEQ:pc * SEQ + 1024]

    def d_split_piece(k, hi):
        if k < 4:
            piece = 4 + 2 * k + hi
            return p.XB[:, piece * SEQ:piece * SEQ + 1024]
        piece = 8 + 2 * (k - 4) + hi
        return p.XB[:, piece * SEQ + 1024:(piece + 1) * SEQ]

    p.set_ring([p.AB[:, i * 2048:(i + 1) * 2048] for i in range(4)])

    P0, A0, D0 = p.c["pe"], p.c["act"], p.c["dve"]
    pre_w = p.c["w"]
    rgw_w = pre_w + 3
    with nc.Block() as blk:
        @blk.gpsimd
        def _(g):
            g.wait_ge(p.sem["w"], pre_w * 16)
            g.dma_start(LTRI, t["ltri"]).then_inc(p.sem["w"], 16)
            g.dma_start(ECROW, t["ecrow"]).then_inc(p.sem["w"], 16)
            g.dma_start(RGWB, t["rgw"]).then_inc(p.sem["w"], 16)
            p.bump("w", 3)

        @blk.tensor
        def _(pe):
            pe.wait_ge(p.sem["w"], rgw_w * 16)
            for tc in range(8):
                zp = (p.ZAp, p.ZBp)[tc % 2]
                if tc >= 2:
                    pe.wait_ge(p.sem["act"], A0 + (tc - 2) + 1)
                ins = None
                for hc in range(16):
                    ins = pe.matmul(zp[:, 0:16],
                                    p.xf(hc, tc * 128, (tc + 1) * 128),
                                    RGWB[:, hc * 16:(hc + 1) * 16],
                                    start=(hc == 0), stop=(hc == 15))
                ins.then_inc(p.sem["pe"], 1)
                p.bump("pe")
            pe.wait_ge(p.sem["act"], A0 + 8)
            pe.matmul(p.ZAp[:, 0:16], p.ONESROW, ECROW,
                      start=True, stop=True).then_inc(p.sem["pe"], 1)
            p.bump("pe")

        @blk.scalar
        def _(a):
            for tc in range(8):
                zp = (p.ZAp, p.ZBp)[tc % 2]
                a.wait_ge(p.sem["pe"], P0 + tc + 1)
                a.activation(p.SG[:, tc * 16:(tc + 1) * 16], zp[:, 0:16],
                             AF.Exp).then_inc(p.sem["act"], 1)
                p.bump("act")

        @blk.vector
        def _(v):
            def step(ins):
                ins.then_inc(p.sem["dve"], 1)
                p.bump("dve")
                v.wait_ge(p.sem["dve"], p.c["dve"])

            v.memset(BASE, 0.0)
            c0, c1, c2, c3 = (p.smv(240, 241), p.smv(241, 242),
                              p.smv(242, 243), p.smv(243, 244))
            for tc in range(8):
                pp = p.SG[:, tc * 16:(tc + 1) * 16]
                p2 = p.SG[:, 128 + tc * 16:128 + (tc + 1) * 16]
                v.wait_ge(p.sem["act"], A0 + tc + 1)
                step(v.tensor_reduce(c0, pp, AX.X, ALU.max))
                step(v.tensor_scalar(eq1m(tc), pp, c0, None, ALU.is_equal))
                step(v.scalar_tensor_tensor(p2, eq1m(tc), -1e30, pp,
                                            ALU.mult, ALU.add))
                step(v.tensor_reduce(c1, p2, AX.X, ALU.max))
                step(v.tensor_tensor(c2, c0, c1, ALU.add))
                step(v.reciprocal(c3, c2))
                step(v.tensor_tensor(wcol(tc, 0), c0, c3, ALU.mult))
                step(v.tensor_tensor(wcol(tc, 1), c1, c3, ALU.mult))
                step(v.tensor_scalar(eq2m(tc), p2, c1, None, ALU.is_equal))
            v.wait_ge(p.sem["pe"], P0 + 9)
            v.tensor_copy(ECB, p.ZAp[:, 0:16]).then_inc(p.sem["dve"], 1)
            p.bump("dve")

    P0, A0, D0 = p.c["pe"], p.c["act"], p.c["dve"]
    with nc.Block() as blk:
        @blk.tensor
        def _(pe):
            for tc in range(8):
                zp = (p.ZAp, p.ZBp)[tc % 2]
                zrow = (p.ZBp, p.ZAp)[tc % 2]
                pe.wait_ge(p.sem["dve"], D0 + 10 * tc + 2)
                pe.matmul(zp[:, 0:16], LTRI, MSCR, start=True, stop=False)
                pe.matmul(zp[:, 0:16], p.ONESROW, BASE, start=False, stop=True)
                pe.matmul(zrow[0:1, 0:16], onesf, MSCR,
                          start=True, stop=True).then_inc(p.sem["pe"], 2)
                p.bump("pe", 2)

        @blk.vector
        def _(v):
            def step(ins):
                ins.then_inc(p.sem["dve"], 1)
                p.bump("dve")
                v.wait_ge(p.sem["dve"], p.c["dve"])

            c0 = p.smv(240, 241)
            sv = p.SG[:, 688:704]
            tmp = p.SG[:, 704:720]
            ecap = p.SG[:, 720:736]
            step(v.tensor_scalar(ecap, ECB, float(MOE_C), None, ALU.add))
            for tc in range(8):
                step(v.tensor_tensor(MSCR, eq1m(tc), eq2m(tc), ALU.add))
                v.wait_ge(p.sem["pe"], P0 + 2 * tc + 2)
                zp = (p.ZAp, p.ZBp)[tc % 2]
                zrow = (p.ZBp, p.ZAp)[tc % 2]
                step(v.tensor_tensor(sv, zp[:, 0:16], ECB, ALU.add))
                step(v.tensor_tensor(sv, sv, ecap, ALU.min))
                step(v.tensor_tensor(tmp, sv, eq1m(tc), ALU.mult))
                step(v.tensor_reduce(c0, tmp, AX.X, ALU.add))
                step(v.tensor_copy(p.SLOTS[:, 2 * tc:2 * tc + 1], c0))
                step(v.tensor_tensor(tmp, sv, eq2m(tc), ALU.mult))
                step(v.tensor_reduce(c0, tmp, AX.X, ALU.add))
                step(v.tensor_copy(p.SLOTS[:, 2 * tc + 1:2 * tc + 2], c0))
                step(v.tensor_tensor(BASE, BASE, zrow[0:1, 0:16], ALU.add))

    P0, A0, D0 = p.c["pe"], p.c["act"], p.c["dve"]
    with nc.Block() as blk:
        @blk.tensor
        def _(pe):
            for idx in range(128):
                tc, hc = idx // 16, idx % 16
                if idx >= 2:
                    pe.wait_ge(p.sem["dve"], D0 + idx - 1)
                dstp = (p.ZAp, p.ZBp)[idx % 2][:, 0:128]
                pe.transpose(dstp, p.xf(hc, tc * 128, (tc + 1) * 128),
                             p.ID32).then_inc(p.sem["pe"], 1)
                p.bump("pe")

        @blk.vector
        def _(v):
            for idx in range(128):
                tc, hc = idx // 16, idx % 16
                v.wait_ge(p.sem["pe"], P0 + idx + 1)
                v.tensor_copy(
                    p.AO[:, tc * 2048 + hc * 128:tc * 2048 + (hc + 1) * 128],
                    (p.ZAp, p.ZBp)[idx % 2][:, 0:128]).then_inc(p.sem["dve"], 1)
                p.bump("dve")

        @blk.gpsimd
        def _(g):
            for tc in range(8):
                g.wait_ge(p.sem["dve"], D0 + (tc + 1) * 16)
                for i in range(2):
                    g.indirect_dma_start(
                        t["ein"][:, :],
                        IOA(ap=p.SLOTS[:, 2 * tc + i:2 * tc + i + 1], axis=0),
                        p.AO[:, tc * 2048:(tc + 1) * 2048],
                        None).then_inc(p.sem["ind"], 16)
                    p.bump("ind")
    scatter_done_ind = p.c["ind"]

    NSC = MOE_C // 128

    def xe_dma_dest(sc):
        return (p.AB[:, 8192:16384]
                .rearrange("q (c s) -> q c s", c=16)
                [:, :, sc * 128:sc * 128 + 128])

    def queue_expert_head(e):
        for i in range(2):
            p.queue_slab(f"g{e}_{i}", t["wg"][e, i])
            p.queue_slab(f"u{e}_{i}", t["wu"][e, i])
        for k in range(8):
            gk = f"d{e-1}_{k}" if e > 0 else None
            dsrc = t["wd"][e, k]
            p.queue_slab_at(f"d{e}_{k}", None, None, guard_key=gk, parts=[
                (dsrc[:, 0:1024], d_split_piece(k, 0)),
                (dsrc[:, 1024:2048], d_split_piece(k, 1))])

    def queue_expert_tail(e):
        for i in range(2, 8):
            p.queue_slab(f"g{e}_{i}", t["wg"][e, i])
            p.queue_slab(f"u{e}_{i}", t["wu"][e, i])

    p.xe_wait = {}
    for e in range(16):
        P0, A0, D0 = p.c["pe"], p.c["act"], p.c["dve"]
        X0 = p.c["x"]
        if e == 0:
            queue_expert_head(0)
        queue_expert_tail(e)
        if e < 15:
            queue_expert_head(e + 1)

        with nc.Block() as blk:
            @blk.sync
            def _(sy):
                if e == 0:
                    sy.wait_ge(p.sem["ind"], scatter_done_ind * 16)
                    for sc in range(NSC):
                        sy.dma_start(xe_dma_dest(sc),
                                     t["ein"][sc * 128:(sc + 1) * 128, :],
                                     transpose=True).then_inc(p.sem["x"], 16)
                        p.bump("x")
                    p.xe_wait[0] = p.c["x"]
                if e < 15:
                    sy.wait_ge(p.sem["pe"], P0 + 16)
                    for sc in range(NSC):
                        sy.dma_start(xe_dma_dest(sc),
                                     t["ein"][(e + 1) * MOE_C + sc * 128:(e + 1) * MOE_C + (sc + 1) * 128, :],
                                     transpose=True).then_inc(p.sem["x"], 16)
                        p.bump("x")
                    p.xe_wait[e + 1] = p.c["x"]
                p.xout0 = p.c["x"]
                for sc in range(NSC):
                    sy.wait_ge(p.sem["dve"], D0 + 8 + 4 * sc + 4)
                    for half in range(2):
                        sy.dma_start(
                            t["eout"][e * MOE_C + sc * 128:e * MOE_C + (sc + 1) * 128,
                                      half * 1024:(half + 1) * 1024],
                            oute(sc, half)).then_inc(p.sem["x"], 16)
                        p.bump("x")

            @blk.tensor
            def _(pe):
                pe.wait_ge(p.sem["x"], p.xe_wait[e] * 16)
                for i in range(8):
                    pe.wait_ge(p.sem["w"], p.slab_wait(f"u{e}_{i}"))
                    gs = p.slab(f"g{e}_{i}")
                    us = p.slab(f"u{e}_{i}")
                    ps = p.S01[:, (i % 2) * 512:(i % 2) * 512 + 512]
                    if i >= 2:
                        pe.wait_ge(p.sem["act"], A0 + (i - 2) + 1)
                    ins = None
                    for hc in range(16):
                        ins = pe.matmul(ps, gs[:, hc * 128:(hc + 1) * 128],
                                        xe(hc), start=(hc == 0), stop=(hc == 15))
                    ins.then_inc(p.sem["pe"], 1)
                    p.bump("pe")
                    ps = p.S23[:, (i % 2) * 512:(i % 2) * 512 + 512]
                    if i >= 2:
                        pe.wait_ge(p.sem["dve"], D0 + (i - 2) + 1)
                    ins = None
                    for hc in range(16):
                        ins = pe.matmul(ps, us[:, hc * 128:(hc + 1) * 128],
                                        xe(hc), start=(hc == 0), stop=(hc == 15))
                    ins.then_inc(p.sem["pe"], 1)
                    p.bump("pe")
                    p.free_slab(f"g{e}_{i}", p.c["pe"])
                    p.free_slab(f"u{e}_{i}", p.c["pe"])
                pe.wait_ge(p.sem["w"], p.slab_wait(f"d{e}_7"))
                pe.wait_ge(p.sem["dve"], D0 + 8)
                gi = 0
                for sc in range(NSC):
                    for jq in range(4):
                        rot = gi % 4
                        ps = (p.PVP, p.S01)[rot // 2][:, (rot % 2) * 512:(rot % 2) * 512 + 512]
                        if gi == 2:
                            pe.wait_ge(p.sem["act"], A0 + 8)
                        if gi >= 4:
                            pe.wait_ge(p.sem["dve"], D0 + 8 + (gi - 4) + 1)
                        ins = None
                        for i in range(8):
                            rhs = d_split_piece(i, jq // 2)[:, (jq % 2) * 512:(jq % 2) * 512 + 512]
                            ins = pe.matmul(ps, hte(i)[:, sc * 128:(sc + 1) * 128],
                                            rhs, start=(i == 0), stop=(i == 7))
                        ins.then_inc(p.sem["pe"], 1)
                        p.bump("pe")
                        gi += 1
                for i in range(8):
                    p.free_slab(f"d{e}_{i}", p.c["pe"])

            @blk.vector
            def _(v):
                for i in range(8):
                    v.wait_ge(p.sem["pe"], P0 + 2 * i + 2)
                    v.wait_ge(p.sem["act"], A0 + i + 1)
                    v.tensor_tensor(
                        hte(i), p.SG[:, (i % 2) * 512:(i % 2) * 512 + 512],
                        p.S23[:, (i % 2) * 512:(i % 2) * 512 + 512],
                        ALU.mult).then_inc(p.sem["dve"], 1)
                    p.bump("dve")
                gi = 0
                for sc in range(NSC):
                    for jq in range(4):
                        rot = gi % 4
                        ps = (p.PVP, p.S01)[rot // 2][:, (rot % 2) * 512:(rot % 2) * 512 + 512]
                        v.wait_ge(p.sem["pe"], P0 + 16 + gi + 1)
                        if sc >= 2:
                            v.wait_ge(p.sem["x"], (p.xout0 + 2 * (sc - 2) + 2) * 16)
                        v.tensor_copy(oute(sc, jq // 2)[:, (jq % 2) * 512:(jq % 2) * 512 + 512],
                                      ps).then_inc(p.sem["dve"], 1)
                        p.bump("dve")
                        gi += 1

            @blk.scalar
            def _(a):
                for i in range(8):
                    a.wait_ge(p.sem["pe"], P0 + 2 * i + 1)
                    if i >= 2:
                        a.wait_ge(p.sem["dve"], D0 + (i - 2) + 1)
                    a.activation(p.SG[:, (i % 2) * 512:(i % 2) * 512 + 512],
                                 p.S01[:, (i % 2) * 512:(i % 2) * 512 + 512],
                                 AF.Silu).then_inc(p.sem["act"], 1)
                    p.bump("act")

            @blk.gpsimd
            def _(g):
                p.flush_w(g)

    eout_done_x = p.c["x"]

    for tc in range(8):
        P0, A0, D0 = p.c["pe"], p.c["act"], p.c["dve"]
        I0 = p.c["ind"]
        o1 = p.AB[:, 0:2048]
        o2 = p.AB[:, 2048:4096]
        mile = {}
        with nc.Block() as blk:
            @blk.gpsimd
            def _(g):
                if tc == 0:
                    g.wait_ge(p.sem["x"], eout_done_x * 16)
                for i, dst in ((0, o1), (1, o2)):
                    g.indirect_dma_start(
                        dst, None, t["eout"][:, :],
                        IOA(ap=p.SLOTS[:, 2 * tc + i:2 * tc + i + 1], axis=0)
                    ).then_inc(p.sem["ind"], 16)
                    p.bump("ind")

            @blk.vector
            def _(v):
                def step(ins):
                    ins.then_inc(p.sem["dve"], 1)
                    p.bump("dve")
                    v.wait_ge(p.sem["dve"], p.c["dve"])
                if tc > 0:
                    v.wait_ge(p.sem["out"], p.c["out"] * 16)
                v.wait_ge(p.sem["ind"], (I0 + 2) * 16)
                step(v.tensor_scalar(p.ZT, o2, wcol(tc, 1), None, ALU.mult))
                v.scalar_tensor_tensor(p.ZT, o1, wcol(tc, 0), p.ZT,
                                       ALU.mult, ALU.add)
                v.tensor_tensor(p.ZT, p.ZT,
                                p.AO[:, tc * 2048:(tc + 1) * 2048],
                                ALU.add).then_inc(p.sem["dve"], 1)
                p.bump("dve")
                mile["zt"] = p.c["dve"]
                step(v.tensor_reduce(p.smv(241, 242), p.ZT, AX.X, ALU.add))
                v.wait_ge(p.sem["act"], A0 + 1)
                step(v.tensor_scalar(p.smv(242, 243), p.smv(241, 242), 1.0 / D, None, ALU.mult))
                step(v.tensor_scalar(p.smv(243, 244), p.smv(240, 241), 1.0 / D, None, ALU.mult))
                step(v.tensor_tensor(p.smv(245, 246), p.smv(242, 243), p.smv(242, 243), ALU.mult))
                step(v.tensor_tensor(p.smv(243, 244), p.smv(243, 244), p.smv(245, 246), ALU.subtract))
                mile["var"] = p.c["dve"]
                v.wait_ge(p.sem["act"], A0 + 2)
                step(v.reciprocal(p.smv(244, 245), p.smv(246, 247)))
                step(v.tensor_tensor(p.smv(245, 246), p.smv(242, 243), p.smv(244, 245), ALU.mult))
                v.tensor_scalar(p.ZT, p.ZT, p.smv(244, 245), p.smv(245, 246),
                                ALU.mult, ALU.subtract).then_inc(p.sem["dve"], 1)
                p.bump("dve")
                mile["zn"] = p.c["dve"]

            @blk.scalar
            def _(a):
                a.wait_ge(p.sem["dve"], mile["zt"])
                a.activation(p.AB[:, 4096:6144], p.ZT, AF.Square,
                             accum_out=p.smv(240, 241)).then_inc(p.sem["act"], 1)
                p.bump("act")
                a.wait_ge(p.sem["dve"], mile["var"])
                a.activation(p.smv(246 + 16, 247 + 16) if False else p.smv(246, 247),
                             p.smv(243, 244), AF.Sqrt,
                             bias=p.smv(265, 266)).then_inc(p.sem["act"], 1)
                p.bump("act")

        with nc.Block() as blk:
            @blk.sync
            def _(sy):
                sy.dma_start(t["out"][tc * 128:(tc + 1) * 128, :],
                             p.ZT).then_inc(p.sem["out"], 16)
                p.bump("out")

    with nc.Block() as blk:
        @blk.sync
        def _(sy):
            sy.wait_ge(p.sem["out"], p.c["out"] * 16)
            sy.wait_ge(p.sem["x"], p.c["x"] * 16)
            sy.wait_ge(p.sem["w"], p.c["w"] * 16)
            sy.wait_ge(p.sem["ind"], p.c["ind"] * 16)
            sy.wait_ge(p.sem["cc"], 1)


def emit_store_x2(p, t):
    nc = p.nc
    with nc.Block() as blk:
        @blk.sync
        def _(sy):
            dst = t["x2T"].rearrange("(c p) s -> p c s", p=128)
            for i in range(2):
                src = p.XF[:, i * 8 * TOK:(i + 1) * 8 * TOK].rearrange(
                    "q (c s) -> q c s", c=8)
                sy.dma_start(dst[:, i * 8:(i + 1) * 8, :], src).then_inc(p.sem["out"], 16)
                p.bump("out")
            sy.wait_ge(p.sem["out"], p.c["out"] * 16)
            sy.wait_ge(p.sem["x"], p.c["x"] * 16)
            sy.wait_ge(p.sem["w"], p.c["w"] * 16)



def build_l0():
    nc = bass.Bass("TRN2", target_bir_lowering=False)
    t = {}
    t["xT"] = nc.dram_tensor("xT", [D, SEQ], F32, kind="ExternalInput").ap()
    for nm in ("wq", "wk", "wv", "wo"):
        t[nm] = nc.dram_tensor(nm, [16, 128, D], F32, kind="ExternalInput").ap()
    t["b_attn"] = nc.dram_tensor("b_attn", [128, 64], F32, kind="ExternalInput").ap()
    t["gates"] = nc.dram_tensor("gates", [1, 5], F32, kind="ExternalInput").ap()
    t["obc"] = nc.dram_tensor("obc", [128, 16], F32, kind="ExternalInput").ap()
    t["ident"] = nc.dram_tensor("ident", [128, 128], F32, kind="ExternalInput").ap()
    t["gw"] = nc.dram_tensor("gw", [64, 128, D], I32, kind="ExternalInput").ap()
    t["uw"] = nc.dram_tensor("uw", [64, 128, D], I32, kind="ExternalInput").ap()
    t["dw"] = nc.dram_tensor("dw", [64, 128, D], I32, kind="ExternalInput").ap()
    t["gsc"] = nc.dram_tensor("gsc", [128, 64], F32, kind="ExternalInput").ap()
    t["usc"] = nc.dram_tensor("usc", [128, 64], F32, kind="ExternalInput").ap()
    t["dsc"] = nc.dram_tensor("dsc", [128, 16], F32, kind="ExternalInput").ap()
    t["x2T"] = nc.dram_tensor("x2T", [D, TOK], F32, kind="ExternalOutput").ap()
    p = P(nc)
    emit_init(p, t, l0=True)
    emit_attn_l0(p, t)
    emit_outproj(p, t["wo"], bias_col0=48, n_queued=4)
    emit_ln(p, p.AB[:, 11264:13312])
    emit_ffn_l0(p, t)
    emit_ln(p, p.AB[:, 0:2048])
    emit_store_x2(p, t)
    return nc


def build_fused(dbg_x3=False):
    nc = bass.Bass("TRN2", target_bir_lowering=False)
    t = {}
    t["xT"] = nc.dram_tensor("xT", [D, SEQ], F32, kind="ExternalInput").ap()
    for nm in ("wq", "wk", "wv", "wo"):
        t[nm] = nc.dram_tensor(nm, [16, 128, D], F32, kind="ExternalInput").ap()
    t["b_attn"] = nc.dram_tensor("b_attn", [128, 64], F32, kind="ExternalInput").ap()
    t["gates"] = nc.dram_tensor("gates", [1, 5], F32, kind="ExternalInput").ap()
    t["obc"] = nc.dram_tensor("obc", [128, 16], F32, kind="ExternalInput").ap()
    t["ident"] = nc.dram_tensor("ident", [128, 128], F32, kind="ExternalInput").ap()
    t["gw"] = nc.dram_tensor("gw", [64, 128, D], I32, kind="ExternalInput").ap()
    t["uw"] = nc.dram_tensor("uw", [64, 128, D], I32, kind="ExternalInput").ap()
    t["dw"] = nc.dram_tensor("dw", [64, 128, D], I32, kind="ExternalInput").ap()
    t["gsc"] = nc.dram_tensor("gsc", [128, 64], F32, kind="ExternalInput").ap()
    t["usc"] = nc.dram_tensor("usc", [128, 64], F32, kind="ExternalInput").ap()
    t["dsc"] = nc.dram_tensor("dsc", [128, 16], F32, kind="ExternalInput").ap()
    t["wqkv"] = nc.dram_tensor("wqkv", [24, 128, D], F32, kind="ExternalInput").ap()
    t["b_qkv"] = nc.dram_tensor("b_qkv", [128, 24], F32, kind="ExternalInput").ap()
    t["wout"] = nc.dram_tensor("wout", [16, 128, D], F32, kind="ExternalInput").ap()
    t["b_out"] = nc.dram_tensor("b_out", [128, 16], F32, kind="ExternalInput").ap()
    t["rgw"] = nc.dram_tensor("rgw", [128, 256], F32, kind="ExternalInput").ap()
    t["wg"] = nc.dram_tensor("wg", [16, 8, 128, D], F32, kind="ExternalInput").ap()
    t["wu"] = nc.dram_tensor("wu", [16, 8, 128, D], F32, kind="ExternalInput").ap()
    t["wd"] = nc.dram_tensor("wd", [16, 8, 128, D], F32, kind="ExternalInput").ap()
    t["cc_in"] = nc.dram_tensor("cc_in", [1024, TOK], BF16, kind="Internal").ap()
    t["cc_out"] = nc.dram_tensor("cc_out", [2048, TOK], BF16, kind="Internal").ap()
    t["ltri"] = nc.dram_tensor("ltri", [128, 128], F32, kind="ExternalInput").ap()
    t["ecrow"] = nc.dram_tensor("ecrow", [1, 16], F32, kind="ExternalInput").ap()
    t["ein"] = nc.dram_tensor("ein", [16 * MOE_C, D], BF16, kind="Internal").ap()
    t["eout"] = nc.dram_tensor("eout", [16 * MOE_C, D], BF16, kind="Internal").ap()
    t["out"] = nc.dram_tensor("out", [TOK, D], F32, kind="ExternalOutput").ap()
    if dbg_x3:
        t["x2T"] = nc.dram_tensor("x2T", [D, TOK], F32, kind="ExternalOutput").ap()
        t["x3T"] = nc.dram_tensor("x3T", [D, TOK], F32, kind="ExternalOutput").ap()
    p = P(nc)
    emit_init(p, t, l0=True)
    emit_attn_l0(p, t)
    emit_outproj(p, t["wo"], bias_col0=48, n_queued=4)
    emit_ln(p, p.AB[:, 11264:13312])
    emit_ffn_l0(p, t)
    emit_ln(p, p.AB[:, 0:2048])
    if dbg_x3:
        with nc.Block() as blk:
            @blk.sync
            def _(sy):
                dst = t["x2T"].rearrange("(c q) s -> q c s", q=128)
                for i in range(2):
                    src_ = p.XF[:, i * 8 * TOK:(i + 1) * 8 * TOK].rearrange(
                        "q (c s) -> q c s", c=8)
                    sy.dma_start(dst[:, i * 8:(i + 1) * 8, :], src_).then_inc(p.sem["out"], 16)
                    p.bump("out")
    emit_kv_exchange(p, t)
    emit_attn_l1(p, t)
    emit_outproj(p, t["wout"], bias_col0=294, n_queued=0)
    emit_ln(p, p.AB[:, 0:2048])
    if dbg_x3:
        with nc.Block() as blk:
            @blk.sync
            def _(sy):
                dst = t["x3T"].rearrange("(c q) s -> q c s", q=128)
                for i in range(2):
                    src_ = p.XF[:, i * 8 * TOK:(i + 1) * 8 * TOK].rearrange(
                        "q (c s) -> q c s", c=8)
                    sy.dma_start(dst[:, i * 8:(i + 1) * 8, :], src_).then_inc(p.sem["out"], 16)
                    p.bump("out")
    emit_moe(p, t)
    return nc


def kernel(**inp):
    f32 = np.float32
    x = np.asarray(inp["L0_attn_hidden_states"], f32)

    perm = l1_q_perm()
    w_qkv = np.asarray(inp["L1_attn_w_qkv"], f32)
    wqkv_al = np.concatenate([w_qkv[0:2048][perm], w_qkv[2048:3072]], axis=0)
    b_qkv = np.asarray(inp["L1_attn_b_qkv"], f32)
    b_al = np.concatenate([b_qkv[0:2048][perm], b_qkv[2048:3072]])
    w_out = np.asarray(inp["L1_attn_w_out"], f32)
    rg = np.asarray(inp["L1_ffn_gate_w"], f32)
    rgw = np.ascontiguousarray(
        rg.T.reshape(16, 128, 16).transpose(1, 0, 2).reshape(128, 256))

    shared = {
        "wq": lay_w_out(np.asarray(inp["L0_attn_Wq"], f32)),
        "wk": lay_w_out(np.asarray(inp["L0_attn_Wk"], f32)),
        "wv": lay_w_out(np.asarray(inp["L0_attn_Wv"], f32)),
        "wo": lay_w_out(np.asarray(inp["L0_attn_Wo"], f32)),
        "b_attn": np.concatenate([lay_vec(np.asarray(inp["L0_attn_b" + k], f32))
                                  for k in ("q", "k", "v", "o")], axis=1),
        "gates": np.stack([np.asarray(inp["L0_attn_truth_gate"], f32)[0],
                           np.asarray(inp["L0_attn_balance_gate"], f32)[0],
                           np.asarray(inp["L0_attn_order_gate"], f32)[0],
                           np.asarray(inp["L0_attn_justice_gate"], f32)[0],
                           np.asarray(inp["L0_attn_harmony_gate"], f32)[0]])[None, :].astype(f32),
        "ident": np.eye(128, dtype=f32),
        "gw": lay_w_out(np.asarray(inp["L0_ffn_gate_w"], np.int32)),
        "uw": lay_w_out(np.asarray(inp["L0_ffn_up_w"], np.int32)),
        "dw": lay_w_in(np.asarray(inp["L0_ffn_down_w"], np.int32)),
        "gsc": lay_vec(np.asarray(inp["L0_ffn_gate_scale"], f32)),
        "usc": lay_vec(np.asarray(inp["L0_ffn_up_scale"], f32)),
        "dsc": lay_vec(np.asarray(inp["L0_ffn_down_scale"], f32)),
        "wqkv": lay_w_out(wqkv_al),
        "b_qkv": lay_vec(b_al),
        "wout": lay_w_out(np.ascontiguousarray(w_out[:, perm])),
        "b_out": lay_vec(np.asarray(inp["L1_attn_b_out"], f32)),
        "rgw": rgw,
        "wg": lay_moe_up(np.asarray(inp["L1_ffn_wg"], f32)),
        "wu": lay_moe_up(np.asarray(inp["L1_ffn_wu"], f32)),
        "wd": lay_moe_down(np.asarray(inp["L1_ffn_wd"], f32)),
        "ltri": np.tril(np.ones((128, 128), dtype=f32)).T.copy(),
        "ecrow": (np.arange(16, dtype=f32) * MOE_C - 1.0)[None, :].copy(),
    }

    pos = np.arange(SEQ, dtype=f32) / SEQ * 0.05
    in_maps = []
    for c in range(8):
        b, hh = c // 2, c % 2
        xT = np.ascontiguousarray(x[b].T)
        own = slice(hh * TOK, (hh + 1) * TOK)
        par = slice((1 - hh) * TOK, (2 - hh) * TOK)
        m = dict(shared)
        m["xT"] = np.ascontiguousarray(np.concatenate([xT[:, own], xT[:, par]], axis=1))
        m["obc"] = lay_vec(np.concatenate([pos[own], pos[par]]))
        in_maps.append(m)

    nc = build_fused()
    res = run_bass_kernel_spmd(nc, in_maps, core_ids=list(range(8)))

    out = np.empty((4, SEQ, D), np.float32)
    for c in range(8):
        b, hh = c // 2, c % 2
        out[b, hh * TOK:(hh + 1) * TOK, :] = np.asarray(res.results[c]["out"])
    return out
